# version 09
# baseline (speedup 1.0000x reference)
import numpy as np
import ml_dtypes
import concourse.bass as bass
import concourse.mybir as mybir
from concourse.bass_utils import run_bass_kernel_spmd

F32 = mybir.dt.float32
BF16 = mybir.dt.bfloat16
ALU = mybir.AluOpType
ACT = mybir.ActivationFunctionType
AX = mybir.AxisListType

NCORES = 8
S = 2048
D = 2048
DQK = 3072
DI = 8192
HD = 128
NH = 16
NKV = 4
NT = S // 128
NF = D // 128
NI = DI // 128

C32 = float(1.5 * 2 ** 23)
C52 = float(1.5 * 2 ** 52)
TWO30 = float(2 ** 30)
ATTN_SCALE = float(np.float32(1.0) / np.sqrt(np.float64(HD)).astype(np.float32))
INV127 = float(np.float32(1.0) / np.float32(127.0))

ENGS = ("sync", "gpsimd", "tensor", "vector", "scalar")
SEM_OF = {"sync": "sd", "gpsimd": "sg", "tensor": "st", "vector": "sv", "scalar": "ss"}
DMA_SEMS = {"sync": ["da0", "da1", "da2", "da3"],
            "gpsimd": ["ga0", "ga1", "ga2", "ga3"]}
ALL_SEMS = ["st", "sv", "ss", "sd", "sg"] + DMA_SEMS["sync"] + DMA_SEMS["gpsimd"]


class Arena:
    def __init__(self, nc, name, dtype, cols):
        self.t = nc.alloc_sbuf_tensor(name, [128, cols], dtype)
        self.free = [(0, cols)]
        self.name = name

    def take(self, n, align=16):
        for i, (off, sz) in enumerate(self.free):
            a = (off + align - 1) // align * align
            if a + n <= off + sz:
                repl = [p for p in ((off, a - off), (a + n, off + sz - a - n)) if p[1] > 0]
                self.free[i:i + 1] = repl
                return a
        raise MemoryError(f"arena {self.name}: need {n}, free={self.free}")

    def give(self, off, n):
        self.free.append((off, n))
        self.free.sort()
        m = []
        for o, s_ in self.free:
            if m and m[-1][0] + m[-1][1] == o:
                m[-1] = (m[-1][0], m[-1][1] + s_)
            else:
                m.append((o, s_))
        self.free = m

    def ap(self, off, n):
        return self.t.ap()[:, off:off + n]


class Plan:
    def __init__(self, nc):
        self.nc = nc
        self.ops = {e: [] for e in ENGS}
        self.cnt = {s_: 0 for s_ in ALL_SEMS}
        self.rr = {"sync": 0, "gpsimd": 0}
        self.maxw = {e: {} for e in ENGS}

    def _waits(self, eng, waits):
        out = []
        own = SEM_OF[eng]
        for t in waits:
            if t is None:
                continue
            sem, val = t
            if sem == own or val <= self.maxw[eng].get(sem, 0):
                continue
            self.maxw[eng][sem] = val
            out.append((sem, val))
        return out

    def do(self, eng, fn, waits=(), inc=False, drain=False):
        w = self._waits(eng, waits)
        tok = None
        auto = eng in ("vector", "scalar")
        if inc or auto:
            sem = SEM_OF[eng]
            self.cnt[sem] += 1
            tok = (sem, self.cnt[sem])
        self.ops[eng].append((fn, w, tok, drain, 1))
        return tok

    def barrier(self):
        return [(sem, cnt) for sem, cnt in self.cnt.items() if cnt > 0]

    def dma(self, q, dst, src, waits=(), slow=False):
        sems = DMA_SEMS[q]
        sem = sems[self.rr[q]]
        self.rr[q] = (self.rr[q] + 1) % len(sems)
        w = self._waits(q, list(waits) + [(sem, self.cnt[sem])])
        self.cnt[sem] += 16
        tok = (sem, self.cnt[sem])
        self.ops[q].append((lambda e, d=dst, s_=src, sl=slow: e.dma_start(
            d, s_, allow_slow_non_contiguous=sl) if sl else e.dma_start(d, s_),
            w, tok, False, 16))
        return tok

    def emit(self, block, sems):
        def runner(eng_name):
            chain = eng_name in ("vector", "scalar")
            own = SEM_OF.get(eng_name)

            def body(e):
                prev = 0
                for (fn, waits, tok, drain, incn) in self.ops[eng_name]:
                    for (sem, val) in waits:
                        e.wait_ge(sems[sem], val)
                    if chain and prev > 0:
                        e.wait_ge(sems[own], prev)
                    inst = fn(e)
                    if tok is not None:
                        if drain:
                            e.drain()
                            e.nop().then_inc(sems[tok[0]], incn)
                        else:
                            inst.then_inc(sems[tok[0]], incn)
                        if chain:
                            prev = tok[1]
                    elif drain:
                        e.drain()
            return body

        block.sync(runner("sync"))
        block.gpsimd(runner("gpsimd"))
        block.tensor(runner("tensor"))
        block.vector(runner("vector"))
        block.scalar(runner("scalar"))


def cap(t, i):
    return t.ap()[:, i:i + 1]


def _consts():
    tvec = np.arange(S, dtype=np.float32).reshape(1, S)
    ivec = (-(np.arange(0, HD, 2, dtype=np.float64) / HD)).astype(np.float32).reshape(1, 64)
    ii = np.arange(128)[:, None]
    jj = np.arange(1024)[None, :]
    maskb = (jj >= ii + 512).astype(ml_dtypes.bfloat16)
    ident = np.eye(128, dtype=ml_dtypes.bfloat16)
    return tvec, ivec, maskb, ident


def build_program(debug=False, phases=5, p2_qbs=4, p3_qs=4):
    nc = bass.Bass("TRN2", target_bir_lowering=False)

    x_d = nc.dram_tensor("x", [S, D], F32, kind="ExternalInput").ap()
    wqkvT_d = nc.dram_tensor("wqkvT", [D, DQK], F32, kind="ExternalInput").ap()
    woT_d = nc.dram_tensor("woT", [D, D], F32, kind="ExternalInput").ap()
    wgT_d = nc.dram_tensor("wgT", [D, DI], F32, kind="ExternalInput").ap()
    wuT_d = nc.dram_tensor("wuT", [D, DI], F32, kind="ExternalInput").ap()
    wdT_d = nc.dram_tensor("wdT", [DI, D], F32, kind="ExternalInput").ap()
    normw_d = nc.dram_tensor("normw", [1, D], F32, kind="ExternalInput").ap()
    lnw_d = nc.dram_tensor("lnw", [1, DI], F32, kind="ExternalInput").ap()

    tvec_np, ivec_np, maskb_np, ident_np = _consts()
    tvec_d = nc.inline_tensor(tvec_np, "tvecc").ap()
    ivec_d = nc.inline_tensor(ivec_np, "ivecc").ap()
    maskb_d = nc.inline_tensor(maskb_np, "maskBc").ap()
    ident_d = nc.inline_tensor(ident_np, "identc").ap()

    out_d = nc.dram_tensor("out", [S, D], F32, kind="ExternalOutput").ap()
    sk = "ExternalOutput" if debug else "Internal"
    qxf_d = nc.dram_tensor("qxf", [D, S], BF16, kind=sk).ap()
    qf_d = nc.dram_tensor("qf", [D, S], BF16, kind=sk).ap()
    kf_d = nc.dram_tensor("kf", [NKV * HD, S], BF16, kind=sk).ap()
    sA_d = nc.dram_tensor("sA", [S, D], BF16, kind=sk).ap()
    tg_d = nc.dram_tensor("tg", [D, DI], BF16, kind=sk).ap()
    tu_d = nc.dram_tensor("tu", [D, DI], BF16, kind=sk).ap()
    td_d = nc.dram_tensor("td", [DI, D], BF16, kind=sk).ap()
    dbg_d = nc.dram_tensor("dbgv", [128, 256], F32, kind=sk).ap() if debug else None
    dbg2_d = nc.dram_tensor("dbg2", [128, 8192], BF16, kind=sk).ap() if debug else None
    dbg3_d = nc.dram_tensor("dbg3", [128, 8192], BF16, kind=sk).ap() if debug else None
    dbg4_d = nc.dram_tensor("dbg4", [128, 8192], BF16, kind=sk).ap() if debug else None
    dbg5_d = nc.dram_tensor("dbg5", [128, 128], F32, kind=sk).ap() if debug else None

    ident_sb = nc.alloc_sbuf_tensor("identb", [128, 128], BF16)
    mask_sb = nc.alloc_sbuf_tensor("maskb", [128, 1024], BF16)
    ropeC_sb = nc.alloc_sbuf_tensor("ropec", [128, NT * 64], BF16)
    ropeS_sb = nc.alloc_sbuf_tensor("ropes", [128, NT * 64], BF16)
    normbc = nc.alloc_sbuf_tensor("normbc", [128, D], BF16)
    s128bc = nc.alloc_sbuf_tensor("s128bc", [128, 1], F32)
    lnwbc = nc.alloc_sbuf_tensor("lnwbc", [128, DI], BF16)
    ones1f = nc.alloc_sbuf_tensor("ones1f", [1, 128], F32)
    onescol = nc.alloc_sbuf_tensor("onescol", [128, 1], F32)
    sq_scr = nc.alloc_sbuf_tensor("sqscr", [128, 4096], BF16)

    def stats(name):
        return nc.alloc_sbuf_tensor(name, [128, NT], F32)

    st_sum = stats("c_sum"); st_ssq = stats("c_ssq"); st_max = stats("c_max")
    st_min = stats("c_min"); st_m = stats("c_m"); st_t1 = stats("c_t1")
    st_t2 = stats("c_t2"); st_r = stats("c_r"); st_ax = stats("c_ax")
    st_sp = stats("c_sp"); st_bp = stats("c_bp")
    st_ag = stats("c_ag"); st_au = stats("c_au")
    o_ssq = stats("o_ssq"); o_r = stats("o_r"); o_ax = stats("o_ax")
    o_sp = stats("o_sp"); o_t1 = stats("o_t1"); o_t2 = stats("o_t2")
    o_inv = stats("o_inv")
    h_ssq = stats("h_ssq"); h_ssq2 = stats("h_ssq2"); h_r = stats("h_r")
    h_ax = stats("h_ax"); h_sp = stats("h_sp"); h_t1 = stats("h_t1")
    h_t2 = stats("h_t2"); h_ad = stats("h_ad")
    dvec = nc.alloc_sbuf_tensor("dvec", [128, 1], F32)
    omd = nc.alloc_sbuf_tensor("omd", [128, 1], F32)
    wsc = nc.alloc_sbuf_tensor("wsc", [128, 8], F32)
    wacc = nc.alloc_sbuf_tensor("wacc", [128, 128], F32)
    wtot = nc.alloc_sbuf_tensor("wtot", [128, 1], F32)

    AB = Arena(nc, "ab", BF16, 69632)
    AF = Arena(nc, "af", F32, 8704)

    quad = nc.alloc_psum_tensor("quad", [128, 2048], F32)
    pv_a = nc.alloc_psum_tensor("pva", [128, 512], F32)
    pv_b = nc.alloc_psum_tensor("pvb", [128, 512], F32)
    tr_a = nc.alloc_psum_tensor("tra", [128, 512], BF16)
    tr_b = nc.alloc_psum_tensor("trb", [128, 512], BF16)
    trp_of = (tr_a, tr_b)

    p = Plan(nc)

    def quad_ap(i, n=512):
        return quad.ap()[:, i * 512:i * 512 + n]

    quad_tok = [None, None, None, None]
    tr_tok = [None, None]
    pv_tok = [None, None]

    g_toks = [
        p.dma("sync", ident_sb.ap(), ident_d),
        p.dma("sync", mask_sb.ap(), maskb_d),
        p.dma("gpsimd", normbc.ap(), bass.AP(normw_d.tensor, 0, [[0, 128], [1, D]])),
        p.dma("gpsimd", lnwbc.ap(), bass.AP(lnw_d.tensor, 0, [[0, 128], [1, DI]])),
    ]
    nb_toks = [g_toks[2]]
    lnw_done = g_toks[3]
    p.do("vector", lambda e: e.memset(ones1f.ap(), 1.0), inc=True, drain=True)
    p.do("vector", lambda e: e.memset(onescol.ap(), 1.0), inc=True, drain=True)
    p.do("vector", lambda e: e.tensor_scalar(dvec.ap(), onescol.ap(), TWO30, -TWO30,
                                             ALU.add, ALU.add), inc=True, drain=True)
    t_ones = p.do("vector", lambda e: e.engine_nop(), inc=True)
    t_omd = p.do("vector", lambda e: e.tensor_scalar(omd.ap(), dvec.ap(), -1.0, 1.0,
                                                     ALU.mult, ALU.add), inc=True, drain=True)
    p.do("vector", lambda e: e.memset(wsc.ap(), 0.0), inc=True)
    p.do("vector", lambda e: e.memset(wacc.ap(), 0.0), inc=True)

    import math as _math
    _tpi = 2 * _math.pi
    _c1 = float(np.float32(_tpi))
    _c2 = float(np.float32(_tpi - _c1))
    _c3 = float(np.float32(_tpi - _c1 - _c2))
    _in2pi = float(np.float32(1.0 / _tpi))
    _d1 = float(np.float32(_math.pi / 2))
    _d2 = float(np.float32(_math.pi / 2 - np.float64(np.float32(_math.pi / 2))))
    rb_off = AF.take(3 * 1024 + 2048 + 128)
    trow = AF.ap(rb_off + 3 * 1024, 2048)[0:1, :]
    irow = AF.ap(rb_off + 3 * 1024 + 2048, 64)[0:1, :]
    lnth = AF.ap(rb_off + 3 * 1024 + 2048 + 64, 1)[0:1, :]
    frow = AF.ap(rb_off + 3 * 1024 + 2048 + 65, 63 + 1)[0:1, 0:64]
    angb = AF.ap(rb_off, 1024)
    ktb = AF.ap(rb_off + 1024, 1024)
    k2b = AF.ap(rb_off + 2048, 1024)
    tok_tv = p.dma("sync", trow, tvec_d)
    tok_iv = p.dma("sync", irow, ivec_d)
    tv_th = p.do("vector", lambda e: e.memset(lnth, 500000.0), inc=True, drain=True)
    ts_fr = p.do("scalar", lambda e: (
        e.activation(lnth, lnth, ACT.Ln), e.drain(),
        e.activation(frow, irow, ACT.Exp, scale=lnth))[-1],
        waits=[tok_tv, tok_iv, tv_th], inc=True, drain=True)
    ang_evs = []
    for ti in range(NT):
        tmr = p.do("tensor", lambda e, ti=ti: e.matmul(
            quad.ap()[:, 64:128], trow[0:1, ti * 128:(ti + 1) * 128], frow,
            start=True, stop=True), waits=[ts_fr, quad_tok[1]] + (
                [ang_evs[-1]] if ang_evs else []), inc=True)
        tve = p.do("vector", lambda e, ti=ti: e.tensor_copy(
            angb[:, ti * 64:(ti + 1) * 64], quad.ap()[:, 64:128]),
            waits=[tmr], inc=True, drain=True)
        ang_evs.append(tve)
    quad_tok[1] = ang_evs[-1]
    rope_done = None
    for half, dst in ((0, ropeS_sb), (1, ropeC_sb)):
        if half == 1:
            p.do("vector", lambda e: e.tensor_scalar(
                angb, angb, _d1, _d2, ALU.add, ALU.add),
                waits=[ang_evs[-1], rope_done], inc=True, drain=True)
        p.do("vector", lambda e: e.tensor_scalar(
            ktb, angb, _in2pi, C32, ALU.mult, ALU.add),
            waits=[t_omd, ang_evs[-1]], inc=True, drain=True)
        p.do("vector", lambda e: e.tensor_single_scalar(
            ktb, ktb, -C32, ALU.add), inc=True, drain=True)
        p.do("vector", lambda e: e.tensor_scalar(
            k2b, ktb, C52, -C52, ALU.add, ALU.add), inc=True, drain=True)
        p.do("vector", lambda e: e.scalar_tensor_tensor(
            ktb, ktb, omd.ap(), k2b, ALU.mult, ALU.add), inc=True, drain=True)
        p.do("vector", lambda e: e.scalar_tensor_tensor(
            k2b, ktb, -_c1, angb, ALU.mult, ALU.add), inc=True, drain=True)
        p.do("vector", lambda e: e.scalar_tensor_tensor(
            k2b, ktb, -_c2, k2b, ALU.mult, ALU.add), inc=True, drain=True)
        p.do("vector", lambda e: e.scalar_tensor_tensor(
            k2b, ktb, -_c3, k2b, ALU.mult, ALU.add), inc=True, drain=True)
        tred = p.do("vector", lambda e: e.tensor_scalar(
            k2b, k2b, 3.141592502593994, -3.141592502593994, ALU.min, ALU.max), inc=True, drain=True)
        rope_done = p.do("scalar", lambda e, dst=dst: e.activation(
            dst.ap(), k2b, ACT.Sin), waits=[tred], inc=True, drain=True)
    tv_s1 = p.do("vector", lambda e: e.memset(lnth, 128.0),
                 waits=[rope_done], inc=True, drain=True)
    ts_s1 = p.do("scalar", lambda e: (
        e.activation(lnth, lnth, ACT.Ln), e.drain(),
        e.activation(lnth, lnth, ACT.Exp, scale=-0.5))[-1],
        waits=[tv_s1], inc=True, drain=True)
    tms = p.do("tensor", lambda e: e.matmul(
        quad.ap()[:, 4:5], ones1f.ap(), lnth, start=True, stop=True),
        waits=[ts_s1, t_ones, quad_tok[0]], inc=True)
    t_s128 = p.do("vector", lambda e: e.tensor_copy(
        s128bc.ap(), quad.ap()[:, 4:5]), waits=[tms], inc=True, drain=True)
    quad_tok[0] = t_s128
    AF.give(rb_off, 3 * 1024 + 2048 + 128)

    WP = 1024

    def phase_w_gen(wT, tern_d, nrows_chunks, wide, col_mc, col_s, done_sink):
        npieces_per_chunk = wide // WP
        npieces = nrows_chunks * npieces_per_chunk
        bar_w1 = p.barrier()
        wb_off = AF.take(2 * WP)
        red_tok = [None, None]
        wsrc = wT.rearrange("(c p) n -> c p n", p=128)
        for i in range(npieces):
            ch, pc = divmod(i, npieces_per_chunk)
            tok_d = p.dma(("sync", "gpsimd")[i % 2], AF.ap(wb_off + (i % 2) * WP, WP),
                          wsrc[ch][:, pc * WP:(pc + 1) * WP],
                          waits=[red_tok[i % 2]] + (bar_w1 if i < 2 else []))
            red_tok[i % 2] = p.do("vector", lambda e, i=i: e.reduce_sum(
                wacc.ap()[:, i:i + 1], AF.ap(wb_off + (i % 2) * WP, WP),
                axis=AX.X, apply_absolute_value=True), waits=[tok_d], inc=True,
                drain=False)
            yield
        tvt = p.do("vector", lambda e, n=npieces: e.reduce_sum(
            wtot.ap(), wacc.ap()[:, 0:n], axis=AX.X), waits=red_tok, inc=True)
        tm = p.do("tensor", lambda e: e.matmul(
            quad.ap()[0:1, 0:1], wtot.ap(), onescol.ap(), start=True, stop=True),
            waits=[tvt, quad_tok[0]], inc=True)
        inv_nel = 1.0 / (128.0 * nrows_chunks * wide)
        tv1 = p.do("vector", lambda e, iv=inv_nel: e.tensor_scalar(
            wsc.ap()[0:1, 6:7], quad.ap()[0:1, 0:1], iv, 1e-5, ALU.mult, ALU.max),
            waits=[tm], inc=True)
        tm2 = p.do("tensor", lambda e: e.matmul(
            quad.ap()[:, 4:5], ones1f.ap(), wsc.ap()[0:1, 6:7], start=True, stop=True),
            waits=[tv1], inc=True)
        p.do("vector", lambda e, c=col_mc: e.tensor_copy(
            wsc.ap()[:, c:c + 1], quad.ap()[:, 4:5]), waits=[tm2], inc=True)
        tv3 = p.do("vector", lambda e, c=col_mc, c2=col_s: e.reciprocal(
            wsc.ap()[:, c2:c2 + 1], wsc.ap()[:, c:c + 1]), inc=True)
        quad_tok[0] = tv3
        AF.give(wb_off, 2 * WP)
        yield

        bar_w2 = p.barrier()
        wb2 = AF.take(2 * WP)
        t1o = AB.take(WP)
        t2o = AB.take(WP)
        too = AB.take(2 * WP)
        tdst = tern_d.rearrange("(c p) n -> c p n", p=128)
        use_tok = [None, None]
        out_tok = [None, None]
        for i in range(npieces):
            ch, pc = divmod(i, npieces_per_chunk)
            tok_d = p.dma(("sync", "gpsimd")[i % 2], AF.ap(wb2 + (i % 2) * WP, WP),
                          wsrc[ch][:, pc * WP:(pc + 1) * WP],
                          waits=[use_tok[i % 2]] + ((bar_w2 + [tv3]) if i < 2 else []))
            sap = wsc.ap()[:, col_s:col_s + 1]
            p.do("vector", lambda e, i=i, sap=sap: e.tensor_scalar(
                AB.ap(t1o, WP), AF.ap(wb2 + (i % 2) * WP, WP), sap, 0.5,
                ALU.mult, ALU.is_ge),
                waits=[tok_d, tv3, out_tok[i % 2]], inc=True)
            p.do("vector", lambda e, i=i, sap=sap: e.tensor_scalar(
                AB.ap(t2o, WP), AF.ap(wb2 + (i % 2) * WP, WP), sap, -0.5,
                ALU.mult, ALU.is_gt), inc=True)
            use_tok[i % 2] = tvq = p.do("vector", lambda e, i=i: e.scalar_tensor_tensor(
                AB.ap(too + (i % 2) * WP, WP), AB.ap(t1o, WP), -1.0, AB.ap(t2o, WP),
                ALU.add, ALU.add), inc=True)
            out_tok[i % 2] = p.dma(("gpsimd", "sync")[i % 2], tdst[ch][:, pc * WP:(pc + 1) * WP],
                                   AB.ap(too + (i % 2) * WP, WP), waits=[tvq])
            yield
        AF.give(wb2, 2 * WP)
        AB.give(t1o, WP); AB.give(t2o, WP); AB.give(too, 2 * WP)
        done_sink.extend(t for t in out_tok if t)
        yield

    tern_done = []

    def _w_gens():
        yield from phase_w_gen(wgT_d, tg_d, NF, DI, 0, 1, tern_done)
        yield from phase_w_gen(wuT_d, tu_d, NF, DI, 2, 3, tern_done)
        yield from phase_w_gen(wdT_d, td_d, NI, D, 4, 5, tern_done)

    w_gen = _w_gens() if phases >= 1 else iter(())

    def drive_w(n):
        for _ in range(n):
            try:
                next(w_gen)
            except StopIteration:
                return

    qxf_off = AB.take(NT * 2048)

    def qxf_ap(f, c0, n):
        return AB.ap(qxf_off + f * 2048 + c0, n)

    p0_last = []
    ag_tok = None
    qxf_done = [None] * NT
    if phases >= 1:
        bar_p0 = p.barrier()
        xt_off = AF.take(2 * 2048)
        y1_off = AF.take(2048)
        t1_off = AB.take(2048)
        t2_off = AB.take(2048)
        qxt_off = AB.take(2 * 2048)
        xt_busy = [None, None]
        qxt_tok = [None, None]
        last_ev = None
        for ti in range(NT):
            xt = AF.ap(xt_off + (ti % 2) * 2048, 2048)
            tok_x = p.dma("sync", xt, x_d[ti * 128:(ti + 1) * 128, :],
                          waits=[xt_busy[ti % 2]] + (bar_p0 if ti < 2 else []))
            p.do("vector", lambda e, xt=xt, ti=ti: e.reduce_sum(
                cap(st_sum, ti), xt, axis=AX.X), waits=[tok_x], inc=True)
            p.do("vector", lambda e, xt=xt, ti=ti: e.reduce_max(
                cap(st_max, ti), xt, axis=AX.X), inc=True)
            p.do("vector", lambda e, xt=xt, ti=ti: e.tensor_reduce(
                cap(st_min, ti), xt, axis=AX.X, op=ALU.min), inc=True)
            ts_sq = p.do("scalar", lambda e, xt=xt, ti=ti: e.activation(
                sq_scr.ap()[:, 0:2048], xt, ACT.Square, accum_out=cap(st_ssq, ti)),
                waits=[tok_x], inc=True, drain=True)
            p.do("vector", lambda e, ti=ti: e.tensor_single_scalar(
                cap(st_m, ti), cap(st_sum, ti), 1.0 / 2048.0, ALU.mult), waits=[ts_sq], inc=True)
            p.do("vector", lambda e, ti=ti: e.tensor_scalar(
                cap(st_t1, ti), cap(st_m, ti), cap(st_m, ti), -1e-5, ALU.mult, ALU.add), inc=True)
            tv_var = p.do("vector", lambda e, ti=ti: e.scalar_tensor_tensor(
                cap(st_t2, ti), cap(st_ssq, ti), 1.0 / 2048.0, cap(st_t1, ti),
                ALU.mult, ALU.subtract), inc=True)
            ts_r0 = p.do("scalar", lambda e, ti=ti: (
                e.activation(cap(st_t1, ti), cap(st_t2, ti), ACT.Ln), e.drain(),
                e.activation(cap(st_r, ti), cap(st_t1, ti), ACT.Exp, scale=-0.5))[-1],
                waits=[tv_var], inc=True, drain=True)
            p.do("vector", lambda e, ti=ti: e.tensor_scalar(
                cap(st_t1, ti), cap(st_t2, ti), cap(st_r, ti), cap(st_r, ti), ALU.mult, ALU.mult),
                waits=[ts_r0], inc=True)
            p.do("vector", lambda e, ti=ti: e.tensor_scalar(
                cap(st_t1, ti), cap(st_t1, ti), -0.5, 1.5, ALU.mult, ALU.add), inc=True)
            p.do("vector", lambda e, ti=ti: e.tensor_scalar_mul(
                cap(st_r, ti), cap(st_r, ti), cap(st_t1, ti)), inc=True)
            p.do("vector", lambda e, ti=ti: e.tensor_single_scalar(
                cap(st_t1, ti), cap(st_max, ti), cap(st_m, ti), ALU.subtract), inc=True)
            p.do("vector", lambda e, ti=ti: e.tensor_scalar(
                cap(st_t2, ti), cap(st_min, ti), -1.0, cap(st_m, ti), ALU.mult, ALU.add), inc=True)
            p.do("vector", lambda e, ti=ti: e.tensor_max(
                cap(st_t1, ti), cap(st_t1, ti), cap(st_t2, ti)), inc=True)
            p.do("vector", lambda e, ti=ti: e.tensor_scalar(
                cap(st_t1, ti), cap(st_t1, ti), cap(st_r, ti), 1e-5, ALU.mult, ALU.max), inc=True)
            p.do("vector", lambda e, ti=ti: e.tensor_single_scalar(
                cap(st_ax, ti), cap(st_t1, ti), INV127, ALU.mult), inc=True)
            p.do("vector", lambda e, ti=ti: e.reciprocal(
                cap(st_t2, ti), cap(st_t1, ti)), inc=True)
            p.do("vector", lambda e, ti=ti: e.tensor_single_scalar(
                cap(st_t2, ti), cap(st_t2, ti), 127.0, ALU.mult), inc=True)
            p.do("vector", lambda e, ti=ti: e.tensor_scalar_mul(
                cap(st_sp, ti), cap(st_r, ti), cap(st_t2, ti)), inc=True)
            p.do("vector", lambda e, ti=ti: e.tensor_scalar(
                cap(st_bp, ti), cap(st_m, ti), cap(st_sp, ti), -1.0, ALU.mult, ALU.mult), inc=True)
            y1 = AF.ap(y1_off, 2048)
            t1b = AB.ap(t1_off, 2048)
            t2b = AB.ap(t2_off, 2048)
            qxt = AB.ap(qxt_off + (ti % 2) * 2048, 2048)
            xt_busy[ti % 2] = p.do("vector", lambda e, xt=xt, y1=y1, ti=ti: e.tensor_scalar(
                y1, xt, cap(st_sp, ti), cap(st_bp, ti), ALU.mult, ALU.add),
                waits=[qxt_tok[ti % 2]], inc=True)
            p.do("vector", lambda e, y1=y1, t1b=t1b: e.tensor_scalar(
                t1b, y1, C32, -C32, ALU.add, ALU.add), inc=True)
            p.do("vector", lambda e, t1b=t1b, t2b=t2b: e.tensor_scalar(
                t2b, t1b, C52, -C52, ALU.add, ALU.add), inc=True)
            tv_q = p.do("vector", lambda e, t1b=t1b, t2b=t2b, qxt=qxt: e.scalar_tensor_tensor(
                qxt, t1b, omd.ap(), t2b, ALU.mult, ALU.add),
                waits=[t_omd], inc=True)
            for j in range(NF):
                trp = trp_of[j % 2]
                tm = p.do("tensor", lambda e, qxt=qxt, j=j, trp=trp: e.transpose(
                    trp.ap()[:, 0:128], qxt[:, j * 128:(j + 1) * 128], ident_sb.ap()),
                    waits=[tv_q, tr_tok[j % 2], g_toks[0]], inc=True)
                tv_e = p.do("vector", lambda e, j=j, ti=ti, trp=trp: e.tensor_copy(
                    qxf_ap(j, ti * 128, 128), trp.ap()[:, 0:128]), waits=[tm], inc=True)
                tr_tok[j % 2] = tv_e
                if j == NF - 1:
                    qxt_tok[ti % 2] = tv_e
                    qxf_done[ti] = tv_e
                    last_ev = tv_e
            drive_w(2)
        tv_d = p.do("vector", lambda e: e.engine_nop(), waits=[last_ev], inc=True)
        qxf_store = p.dma("gpsimd", qxf_d.rearrange("(c p) n -> p c n", p=128),
                          AB.t.ap()[:, qxf_off:qxf_off + NT * 2048]
                          .rearrange("p (c n) -> p c n", n=2048), waits=[tv_d])
        p0_last = [qxf_store, tv_d]
        p.do("vector", lambda e: e.tensor_scalar_mul(
            st_ag.ap(), st_ax.ap(), wsc.ap()[:, 0:1]), waits=[tv_d], inc=True)
        ag_tok = p.do("vector", lambda e: e.tensor_scalar_mul(
            st_au.ap(), st_ax.ap(), wsc.ap()[:, 2:3]), inc=True, drain=True)
        if debug:
            for nm, src_t, c0 in [("sum", st_sum, 0), ("ssq", st_ssq, 16),
                                  ("max", st_max, 32), ("min", st_min, 48),
                                  ("m", st_m, 64), ("r", st_r, 80),
                                  ("ax", st_ax, 96), ("sp", st_sp, 112),
                                  ("bp", st_bp, 128), ("wsc", wsc, 144),
                                  ("wacc", wacc, 160)]:
                n = src_t.shape[1] if hasattr(src_t, 'shape') else NT
                p.dma("gpsimd", dbg_d[:, c0:c0 + src_t.ap().shape[-1]],
                      src_t.ap(), waits=[ag_tok] + tern_done, slow=True)
            p.dma("gpsimd", dbg_d[:, 240:241], wtot.ap(), waits=[ag_tok], slow=True)
            p.dma("gpsimd", dbg_d[:, 241:242], dvec.ap(), waits=[ag_tok], slow=True)
            p.dma("gpsimd", dbg_d[:, 242:243], omd.ap(), waits=[ag_tok], slow=True)

    p1_last = []
    vtok_off = None
    if phases >= 2:
        vtok_off = AB.take(NT * 516)
        wkb_off = AB.take(2 * NF * 512)
        AF.give(xt_off, 2 * 2048); AF.give(y1_off, 2048)
        qev_off = AF.take(2 * 512)
        ta_off = AF.take(256)
        tb_off = AF.take(256)
        rq_off = AB.take(2 * 512)
        assert t2_off == t1_off + 2048 and qxt_off == t2_off + 2048
        stage_off = t1_off
        vt_set = p.do("vector", lambda e: e.memset(AB.ap(vtok_off, NT * 516), 1.0),
                      inc=True)

        def vt_dst(ti):
            return AB.t.ap()[:, vtok_off + ti * 516:vtok_off + (ti + 1) * 516] \
                .rearrange("p (g n) -> p g n", n=129)[:, :, 0:128]

        def rope_bcast(tab, ti):
            return bass.AP(tab, ti * 64, [[NT * 64, 128], [0, 4], [1, 64]])

        OBLK = [("q", 0, 0), ("q", 1, 512), ("q", 2, 1024), ("q", 3, 1536),
                ("k", 0, 2048), ("v", 0, 2560)]
        st_last = [None, None]
        stage_tok = None
        for m, (sec, bi, c0) in enumerate(OBLK):
            tok_w = p.dma("gpsimd",
                          AB.t.ap()[:, wkb_off + (m % 2) * NF * 512:
                                    wkb_off + (m % 2 + 1) * NF * 512]
                          .rearrange("p (c n) -> p c n", n=512),
                          wqkvT_d[:, c0:c0 + 512].rearrange("(c p) n -> p c n", p=128),
                          waits=[st_last[m % 2]])
            for ti in range(NT):
                qi = ti % 4
                tm = None
                for f in range(NF):
                    tm = p.do("tensor", lambda e, f=f, ti=ti, m=m, qi=qi: e.matmul(
                        quad_ap(qi), qxf_ap(f, ti * 128, 128),
                        AB.ap(wkb_off + (m % 2) * NF * 512 + f * 512, 512),
                        start=(f == 0), stop=(f == NF - 1)),
                        waits=[tok_w, quad_tok[qi], qxf_done[ti]] if f == 0 else (),
                        inc=(f == NF - 1))
                st_last[m % 2] = tm
                drive_w(1)
                axc = st_ax.ap()[:, ti:ti + 1]
                if sec == "v":
                    tv = p.do("vector", lambda e, ti=ti, qi=qi, axc=axc: e.tensor_scalar_mul(
                        vt_dst(ti), quad_ap(qi), axc), waits=[tm, vt_set],
                        inc=True)
                    quad_tok[qi] = tv
                    p1_last.append(tv)
                else:
                    qev = AF.ap(qev_off + (ti % 2) * 512, 512)
                    tv0 = p.do("vector", lambda e, qev=qev, qi=qi, axc=axc: e.tensor_scalar_mul(
                        qev, quad_ap(qi), axc), waits=[tm, rope_done], inc=True)
                    quad_tok[qi] = tv0
                    qa = qev.rearrange("p (h n) -> p h n", n=128)[:, :, 0:64]
                    qb_ = qev.rearrange("p (h n) -> p h n", n=128)[:, :, 64:128]
                    cA = rope_bcast(ropeC_sb, ti)
                    sA = rope_bcast(ropeS_sb, ti)
                    ta = AF.ap(ta_off, 256).rearrange("p (h n) -> p h n", n=64)
                    tb = AF.ap(tb_off, 256).rearrange("p (h n) -> p h n", n=64)
                    rq = AB.ap(rq_off + (ti % 2) * 512, 512)
                    ra = rq.rearrange("p (h n) -> p h n", n=128)[:, :, 0:64]
                    rb = rq.rearrange("p (h n) -> p h n", n=128)[:, :, 64:128]
                    p.do("vector", lambda e, ta=ta, qa=qa, cA=cA: e.tensor_tensor(
                        ta, qa, cA, ALU.mult), inc=True)
                    p.do("vector", lambda e, tb=tb, qb_=qb_, sA=sA: e.tensor_tensor(
                        tb, qb_, sA, ALU.mult), inc=True)
                    p.do("vector", lambda e, ra=ra, ta=ta, tb=tb: e.tensor_tensor(
                        ra, ta, tb, ALU.subtract), inc=True)
                    p.do("vector", lambda e, ta=ta, qa=qa, sA=sA: e.tensor_tensor(
                        ta, qa, sA, ALU.mult), inc=True)
                    p.do("vector", lambda e, tb=tb, qb_=qb_, cA=cA: e.tensor_tensor(
                        tb, qb_, cA, ALU.mult), inc=True)
                    tv_r = p.do("vector", lambda e, rb=rb, ta=ta, tb=tb: e.tensor_tensor(
                        rb, ta, tb, ALU.add), inc=True)
                    for r in range(4):
                        trp = trp_of[r % 2]
                        tm2 = p.do("tensor", lambda e, rq=rq, r=r, trp=trp: e.transpose(
                            trp.ap()[:, 0:128], rq[:, r * 128:(r + 1) * 128], ident_sb.ap()),
                            waits=[tv_r, tr_tok[r % 2]] +
                            ([stage_tok] if ti == 0 else []), inc=True)
                        tv_e = p.do("vector", lambda e, r=r, ti=ti, trp=trp: e.tensor_copy(
                            AB.ap(stage_off + r * 2048 + ti * 128, 128),
                            trp.ap()[:, 0:128]), waits=[tm2], inc=True)
                        tr_tok[r % 2] = tv_e
                    if ti == NT - 1:
                        tv_d2 = p.do("vector", lambda e: e.engine_nop(), inc=True)
                        if sec == "q":
                            dst = qf_d.rearrange("(c p) n -> p c n", p=128)[:, bi * 4:(bi + 1) * 4, :]
                        else:
                            dst = kf_d.rearrange("(c p) n -> p c n", p=128)
                        stage_tok = p.dma("gpsimd", dst,
                                          AB.t.ap()[:, stage_off:stage_off + 4 * 2048]
                                          .rearrange("p (c n) -> p c n", n=2048),
                                          waits=[tv_d2])
                        p1_last.append(stage_tok)
        AB.give(wkb_off, 2 * NF * 512)
        AF.give(qev_off, 2 * 512); AF.give(ta_off, 256); AF.give(tb_off, 256)
        AB.give(rq_off, 2 * 512); AB.give(stage_off, 4 * 2048)
        AB.give(qxf_off, NT * 2048)

    p2_last = list(p1_last)
    if phases >= 3:
        bar_p2 = p.barrier()
        p1_last = p1_last + bar_p2
        kf_off = AB.take(NKV * 2048)
        probs_off = AB.take(NT * 512)
        wo_off = AB.take(NF * 512)
        ob_off = AB.take(4 * 2048)
        qof_off = AB.take(4 * 2048)
        sas_off = AB.take(4 * 2048)
        qft_off = AB.take(2 * 512)
        qo_off = AB.take(2048)
        y1_off2 = AF.take(2048)

        tok_kf = p.dma("sync", AB.t.ap()[:, kf_off:kf_off + NKV * 2048]
                       .rearrange("p (c n) -> p c n", n=2048),
                       kf_d.rearrange("(c p) n -> p c n", p=128), waits=p1_last)
        qf_tok = [None, None]
        probs_guard = None
        ob_guard = None
        qo_guard = None
        wo_mm_last = [None]
        sas_tok = [None, None, None, None]
        for qb in range(p2_qbs):
            nch = 4 * (qb + 1)
            pv_done = []
            for h in range(NH):
                g = h // 4
                hh = qb * NH + h
                tok_qf = p.dma("sync", AB.ap(qft_off + (hh % 2) * 512, 512),
                               qf_d[h * 128:(h + 1) * 128, qb * 512:(qb + 1) * 512],
                               waits=[qf_tok[hh % 2]] + p1_last)
                exp_toks = []
                for grp in range(qb + 1):
                    mm_toks = []
                    for cc in range(4):
                        c = grp * 4 + cc
                        tmm = p.do("tensor", lambda e, c=c, cc=cc, g=g, hh=hh: e.matmul(
                            quad_ap(cc),
                            AB.ap(kf_off + g * 2048 + c * 128, 128),
                            AB.ap(qft_off + (hh % 2) * 512, 512),
                            start=True, stop=True),
                            waits=[tok_qf, tok_kf, quad_tok[cc]], inc=True)
                        mm_toks.append(tmm)
                    qf_tok[hh % 2] = mm_toks[-1]
                    texp = p.do("scalar", lambda e, grp=grp: e.activation(
                        AB.ap(probs_off + grp * 2048, 2048), quad.ap(),
                        ACT.Exp, scale=s128bc.ap()),
                        waits=mm_toks + [probs_guard, t_s128], inc=True, drain=True)
                    exp_toks.append(texp)
                    for cc in range(4):
                        quad_tok[cc] = texp
                def _mask(e, qb=qb):
                    r = None
                    for cc in range(4):
                        r = e.tensor_tensor(
                            AB.ap(probs_off + (4 * qb + cc) * 512, 512),
                            AB.ap(probs_off + (4 * qb + cc) * 512, 512),
                            mask_sb.ap()[:, 512 - cc * 128:1024 - cc * 128], ALU.mult)
                    return r
                tmask = p.do("vector", _mask, waits=[exp_toks[-1]], inc=True)
                for j in range(4):
                    pvp = (pv_a, pv_b)[j % 2]
                    tmm = None
                    for c in range(nch):
                        tmm = p.do("tensor", lambda e, c=c, j=j, g=g, pvp=pvp, nch=nch: e.matmul(
                            pvp.ap()[:, 0:129],
                            AB.ap(probs_off + c * 512 + j * 128, 128),
                            AB.ap(vtok_off + c * 516 + g * 129, 129),
                            start=(c == 0), stop=(c == nch - 1)),
                            waits=[tmask, pv_tok[j % 2]] + exp_toks if c == 0 else (),
                            inc=(c == nch - 1))
                    jj = qb * 4 + j
                    p.do("vector", lambda e, pvp=pvp, jj=jj: e.reciprocal(
                        o_inv.ap()[:, jj:jj + 1], pvp.ap()[:, 128:129]),
                        waits=[tmm], inc=True)
                    tpv = p.do("vector", lambda e, pvp=pvp, jj=jj, j=j, h=h: e.tensor_scalar_mul(
                        AB.ap(ob_off + j * 2048 + h * 128, 128),
                        pvp.ap()[:, 0:128], o_inv.ap()[:, jj:jj + 1]),
                        waits=[ob_guard] if (h == 0 and ob_guard) else (),
                        inc=True)
                    pv_tok[j % 2] = tpv
                    if j == 3:
                        probs_guard = tmm
                    if h == NH - 1:
                        pv_done.append(tpv)
                drive_w(9)
            dbg2_tok = None
            if debug and qb == 0:
                dbg2_tok = p.dma("gpsimd", dbg2_d, AB.ap(ob_off, 8192), waits=pv_done)
            qof_done = []
            for j in range(4):
                drive_w(6)
                jj = qb * 4 + j
                obj = AB.ap(ob_off + j * 2048, 2048)
                ts_sq = p.do("scalar", lambda e, obj=obj, jj=jj: e.activation(
                    sq_scr.ap()[:, 0:2048], obj, ACT.Square, accum_out=cap(o_ssq, jj)),
                    waits=pv_done, inc=True, drain=True)
                tnw = p.do("vector", lambda e, obj=obj: e.tensor_tensor(
                    obj, obj, normbc.ap(), ALU.mult),
                    waits=[ts_sq, dbg2_tok] + nb_toks, inc=True)
                tvv = p.do("vector", lambda e, jj=jj: e.tensor_scalar(
                    cap(o_t2, jj), cap(o_ssq, jj), 1.0 / 2048.0, 1e-5, ALU.mult, ALU.add),
                    inc=True)
                tsr = p.do("scalar", lambda e, jj=jj: (
                    e.activation(cap(o_t1, jj), cap(o_t2, jj), ACT.Ln), e.drain(),
                    e.activation(cap(o_r, jj), cap(o_t1, jj), ACT.Exp, scale=-0.5))[-1],
                    waits=[tvv], inc=True, drain=True)
                p.do("vector", lambda e, jj=jj: e.tensor_scalar(
                    cap(o_t1, jj), cap(o_t2, jj), cap(o_r, jj), cap(o_r, jj), ALU.mult, ALU.mult),
                    waits=[tsr], inc=True)
                p.do("vector", lambda e, jj=jj: e.tensor_scalar(
                    cap(o_t1, jj), cap(o_t1, jj), -0.5, 1.5, ALU.mult, ALU.add), inc=True)
                p.do("vector", lambda e, jj=jj: e.tensor_scalar_mul(
                    cap(o_r, jj), cap(o_r, jj), cap(o_t1, jj)), inc=True)
                p.do("vector", lambda e, obj=obj, jj=jj: e.reduce_max(
                    cap(o_t1, jj), obj, axis=AX.X, apply_absolute_value=True),
                    waits=[tnw], inc=True)
                p.do("vector", lambda e, jj=jj: e.tensor_scalar(
                    cap(o_t1, jj), cap(o_t1, jj), cap(o_r, jj), 1e-5, ALU.mult, ALU.max), inc=True)
                p.do("vector", lambda e, jj=jj: e.tensor_single_scalar(
                    cap(o_ax, jj), cap(o_t1, jj), INV127, ALU.mult), inc=True)
                p.do("vector", lambda e, jj=jj: e.reciprocal(
                    cap(o_t2, jj), cap(o_t1, jj)), inc=True)
                p.do("vector", lambda e, jj=jj: e.tensor_single_scalar(
                    cap(o_t2, jj), cap(o_t2, jj), 127.0, ALU.mult), inc=True)
                p.do("vector", lambda e, jj=jj: e.tensor_scalar_mul(
                    cap(o_sp, jj), cap(o_r, jj), cap(o_t2, jj)), inc=True)
                y1 = AF.ap(y1_off2, 2048)
                qo = AB.ap(qo_off, 2048)
                t2b = sq_scr.ap()[:, 2048:4096]
                p.do("vector", lambda e, obj=obj, y1=y1, jj=jj: e.tensor_scalar(
                    y1, obj, cap(o_sp, jj), C32, ALU.mult, ALU.add),
                    waits=[qo_guard], inc=True)
                p.do("vector", lambda e, y1=y1, qo=qo: e.tensor_single_scalar(
                    qo, y1, -C32, ALU.add), inc=True)
                p.do("vector", lambda e, qo=qo, t2b=t2b: e.tensor_scalar(
                    t2b, qo, C52, -C52, ALU.add, ALU.add), inc=True)
                tq = p.do("vector", lambda e, qo=qo, t2b=t2b: e.scalar_tensor_tensor(
                    qo, qo, omd.ap(), t2b, ALU.mult, ALU.add), inc=True)
                ob_guard = tq
                evs = []
                for f in range(NF):
                    trp = trp_of[f % 2]
                    tm2 = p.do("tensor", lambda e, qo=qo, f=f, trp=trp: e.transpose(
                        trp.ap()[:, 0:128], qo[:, f * 128:(f + 1) * 128], ident_sb.ap()),
                        waits=[tq, tr_tok[f % 2]] +
                        ([wo_mm_last[0]] if (f == 0 and wo_mm_last[0]) else []), inc=True)
                    tv_e = p.do("vector", lambda e, f=f, j=j, trp=trp: e.tensor_copy(
                        AB.ap(qof_off + j * 2048 + f * 128, 128), trp.ap()[:, 0:128]),
                        waits=[tm2], inc=True)
                    tr_tok[f % 2] = tv_e
                    evs.append(tv_e)
                qo_guard = evs[-1]
                qof_done.append(p.do("vector", lambda e: e.engine_nop(), waits=[evs[-1]],
                                     inc=True))
            if debug and qb == 0:
                dbg3_tok = p.dma("gpsimd", dbg3_d, AB.ap(qof_off, 8192), waits=qof_done)
                qof_done = qof_done + [dbg3_tok]
            for ob in range(4):
                tok_wo = []
                for hf in range(2):
                    tok_wo.append(p.dma(
                        "gpsimd",
                        AB.t.ap()[:, wo_off + hf * 8 * 512:wo_off + (hf + 1) * 8 * 512]
                        .rearrange("p (c n) -> p c n", n=512),
                        woT_d[hf * 1024:(hf + 1) * 1024, ob * 512:(ob + 1) * 512]
                        .rearrange("(c p) n -> p c n", p=128),
                        waits=[wo_mm_last[0]] + (p1_last if qb == 0 and ob == 0 else [])))
                for j in range(4):
                    jj = qb * 4 + j
                    qi = (2 * (ob % 2)) + (j % 2)
                    tmm = None
                    for f in range(NF):
                        tmm = p.do("tensor", lambda e, f=f, j=j, ob=ob, qi=qi: e.matmul(
                            quad_ap(qi), AB.ap(qof_off + j * 2048 + f * 128, 128),
                            AB.ap(wo_off + f * 512, 512),
                            start=(f == 0), stop=(f == NF - 1)),
                            waits=[tok_wo[0] if f < 8 else tok_wo[1], qof_done[j],
                                   quad_tok[qi]] if f in (0, 8) else (),
                            inc=(f == NF - 1))
                    wo_mm_last[0] = tmm
                    tv_s = p.do("vector", lambda e, ob=ob, j=j, qi=qi, jj=jj:
                                e.tensor_scalar_mul(
                                    AB.ap(sas_off + j * 2048 + ob * 512, 512),
                                    quad_ap(qi), o_ax.ap()[:, jj:jj + 1]),
                                waits=[tmm, sas_tok[j]], inc=True)
                    quad_tok[qi] = tv_s
                    if ob == 3:
                        sas_tok[j] = p.dma(
                            "gpsimd", sA_d[qb * 512 + j * 128: qb * 512 + (j + 1) * 128, :],
                            AB.ap(sas_off + j * 2048, 2048), waits=[tv_s])
            p2_last = [t for t in sas_tok if t]
        drive_w(10000)
        p.do("vector", lambda e: e.tensor_scalar_mul(
            st_ag.ap(), st_ax.ap(), wsc.ap()[:, 0:1]), inc=True, drain=True)
        ag_tok = p.do("vector", lambda e: e.tensor_scalar_mul(
            st_au.ap(), st_ax.ap(), wsc.ap()[:, 2:3]), inc=True, drain=True)
        AB.give(kf_off, NKV * 2048); AB.give(probs_off, NT * 512)
        AB.give(ob_off, 4 * 2048); AB.give(qft_off, 2 * 512)
        AB.give(qo_off, 2048); AB.give(qof_off, 4 * 2048)
        AB.give(sas_off, 4 * 2048); AB.give(wo_off, NF * 512)
        AF.give(y1_off2, 2048)
        AB.give(vtok_off, NT * 516)

    if phases >= 4:
        if phases < 3 or p2_qbs < 4:
            drive_w(10000)
        bar_p3 = p.barrier()
        p2_last = p2_last + bar_p3
        hq_off = AB.take(4 * DI)
        qxq_off = AB.take(NF * 512)
        if debug:
            print("P3 AB free after hq/qxq:", AB.free, "hq_off=", hq_off, "qxq_off=", qxq_off)
        wg_off = AB.take(NF * 512)
        wu_off = AB.take(NF * 512)
        sg_off = AB.take(2 * 512)
        tq_off = AB.take(4096)
        saf_off = AB.take(2048)
        t2s_off = AB.take(4096)
        if debug:
            print("P3 offs: wg", wg_off, "wu", wu_off, "sg", sg_off, "tq", tq_off,
                  "saf", saf_off, "free", AB.free)
        y1h_off = AF.take(4096)
        xf_off = AF.take(2048)
        ou_off = AF.take(2048)

        wgt_tok = [None]
        wut_tok = [None]
        sg_guard = [None, None]
        hq_guard = None
        qx_guard = None
        gq_guard = None
        saf_guard = None
        out_guard = None
        xf_guard = None
        final_toks = []
        for Q in range(p3_qs):
            tok_qx = p.dma("sync", AB.t.ap()[:, qxq_off:qxq_off + NF * 512]
                           .rearrange("p (c n) -> p c n", n=512),
                           qxf_d[:, Q * 512:(Q + 1) * 512]
                           .rearrange("(c p) n -> p c n", p=128),
                           waits=[qx_guard] + p0_last + p2_last)
            h_toks = []
            for ob in range(NF):
                tok_g = []
                tok_u = []
                for hf in range(2):
                    tok_g.append(p.dma(
                        "sync", AB.t.ap()[:, wg_off + hf * 8 * 512:wg_off + (hf + 1) * 8 * 512]
                        .rearrange("p (c n) -> p c n", n=512),
                        tg_d[hf * 1024:(hf + 1) * 1024, ob * 512:(ob + 1) * 512]
                        .rearrange("(c p) n -> p c n", p=128),
                        waits=[wgt_tok[0]] + (tern_done + p2_last if Q == 0 and ob == 0 else [])))
                    tok_u.append(p.dma(
                        "gpsimd", AB.t.ap()[:, wu_off + hf * 8 * 512:wu_off + (hf + 1) * 8 * 512]
                        .rearrange("p (c n) -> p c n", n=512),
                        tu_d[hf * 1024:(hf + 1) * 1024, ob * 512:(ob + 1) * 512]
                        .rearrange("(c p) n -> p c n", p=128),
                        waits=[wut_tok[0]] + (tern_done + p2_last if Q == 0 and ob == 0 else [])))
                for ti in range(4):
                    gi = (ti % 2) * 2
                    ui = gi + 1
                    tg_mm = None
                    tu_mm = None
                    for f in range(NF):
                        tg_mm = p.do("tensor", lambda e, f=f, ti=ti, gi=gi: e.matmul(
                            quad_ap(gi), AB.ap(qxq_off + f * 512 + ti * 128, 128),
                            AB.ap(wg_off + f * 512, 512),
                            start=(f == 0), stop=(f == NF - 1)),
                            waits=[tok_g[0] if f < 8 else tok_g[1], tok_qx,
                                   quad_tok[gi], gq_guard] if f in (0, 8) else (),
                            inc=(f == NF - 1))
                    for f in range(NF):
                        tu_mm = p.do("tensor", lambda e, f=f, ti=ti, ui=ui: e.matmul(
                            quad_ap(ui), AB.ap(qxq_off + f * 512 + ti * 128, 128),
                            AB.ap(wu_off + f * 512, 512),
                            start=(f == 0), stop=(f == NF - 1)),
                            waits=[tok_u[0] if f < 8 else tok_u[1],
                                   quad_tok[ui]] if f in (0, 8) else (),
                            inc=(f == NF - 1))
                    if ti == 3:
                        wgt_tok[0] = tg_mm
                        wut_tok[0] = tu_mm
                    tt = Q * 4 + ti
                    sg = AB.ap(sg_off + (ti % 2) * 512, 512)
                    tsg = p.do("scalar", lambda e, sg=sg, gi=gi, tt=tt: e.activation(
                        sg, quad_ap(gi), ACT.Silu, scale=st_ag.ap()[:, tt:tt + 1]),
                        waits=[tg_mm, ag_tok, sg_guard[ti % 2]], inc=True, drain=True)
                    quad_tok[gi] = tsg
                    th = p.do("vector", lambda e, sg=sg, ui=ui, ti=ti, ob=ob, tt=tt:
                              e.scalar_tensor_tensor(
                                  AB.ap(hq_off + ti * DI + ob * 512, 512),
                                  quad_ap(ui), st_au.ap()[:, tt:tt + 1], sg,
                                  ALU.mult, ALU.mult),
                              waits=[tu_mm, tsg] +
                              ([hq_guard] if (ob == 0 and ti == 0 and hq_guard) else []),
                              inc=True)
                    sg_guard[ti % 2] = th
                    quad_tok[ui] = th
                    h_toks.append(th)
            dbgh_tok = None
            if debug and Q == 0 and phases >= 4:
                dbgh_tok = p.dma("gpsimd", dbg3_d, AB.ap(hq_off, DI), waits=h_toks)
            qhf_done = []
            tq_free = None
            for ti in range(4):
                tt = Q * 4 + ti
                hten = AB.ap(hq_off + ti * DI, DI)
                ts_a = p.do("scalar", lambda e, hten=hten, tt=tt: e.activation(
                    sq_scr.ap(), hten[:, 0:4096], ACT.Square, accum_out=cap(h_ssq, tt)),
                    waits=h_toks, inc=True, drain=True)
                ts_b = p.do("scalar", lambda e, hten=hten, tt=tt: e.activation(
                    sq_scr.ap(), hten[:, 4096:8192], ACT.Square, accum_out=cap(h_ssq2, tt)),
                    inc=True, drain=True)
                tlnw = p.do("vector", lambda e, hten=hten: e.tensor_tensor(
                    hten, hten, lnwbc.ap(), ALU.mult),
                    waits=[ts_b, lnw_done, dbgh_tok], inc=True)
                if debug and Q == 0 and ti == 0 and phases >= 4:
                    dbgl_tok = p.dma("gpsimd", dbg3_d, hten, waits=[tlnw])
                    p.dma("gpsimd", dbg5_d[:, 96:97], cap(h_ssq, tt),
                          waits=[ts_b], slow=True)
                    p.dma("gpsimd", dbg5_d[:, 97:98], cap(h_ssq2, tt),
                          waits=[ts_b], slow=True)
                if debug and Q == 0 and phases >= 4:
                    p.dma("gpsimd", dbg5_d[:, 100 + ti:101 + ti], cap(h_ssq, 0),
                          waits=[tsr], slow=True)
                p.do("vector", lambda e, tt=tt: e.tensor_add(
                    cap(h_ssq, tt), cap(h_ssq, tt), cap(h_ssq2, tt)), inc=True)
                tvv = p.do("vector", lambda e, tt=tt: e.tensor_scalar(
                    cap(h_t2, tt), cap(h_ssq, tt), 1.0 / 8192.0, 1e-6, ALU.mult, ALU.add),
                    inc=True)
                tsr = p.do("scalar", lambda e, tt=tt: (
                    e.activation(cap(h_t1, tt), cap(h_t2, tt), ACT.Ln), e.drain(),
                    e.activation(cap(h_r, tt), cap(h_t1, tt), ACT.Exp, scale=-0.5))[-1],
                    waits=[tvv], inc=True, drain=True)
                p.do("vector", lambda e, tt=tt: e.tensor_scalar(
                    cap(h_t1, tt), cap(h_t2, tt), cap(h_r, tt), cap(h_r, tt), ALU.mult, ALU.mult),
                    waits=[tsr], inc=True)
                p.do("vector", lambda e, tt=tt: e.tensor_scalar(
                    cap(h_t1, tt), cap(h_t1, tt), -0.5, 1.5, ALU.mult, ALU.add), inc=True)
                p.do("vector", lambda e, tt=tt: e.tensor_scalar_mul(
                    cap(h_r, tt), cap(h_r, tt), cap(h_t1, tt)), inc=True)
                p.do("vector", lambda e, hten=hten, tt=tt: e.reduce_max(
                    cap(h_t1, tt), hten, axis=AX.X, apply_absolute_value=True),
                    waits=[tlnw], inc=True)
                p.do("vector", lambda e, tt=tt: e.tensor_scalar(
                    cap(h_t1, tt), cap(h_t1, tt), cap(h_r, tt), 1e-5, ALU.mult, ALU.max), inc=True)
                p.do("vector", lambda e, tt=tt: e.tensor_single_scalar(
                    cap(h_ax, tt), cap(h_t1, tt), INV127, ALU.mult), inc=True)
                p.do("vector", lambda e, tt=tt: e.reciprocal(
                    cap(h_t2, tt), cap(h_t1, tt)), inc=True)
                p.do("vector", lambda e, tt=tt: e.tensor_single_scalar(
                    cap(h_t2, tt), cap(h_t2, tt), 127.0, ALU.mult), inc=True)
                p.do("vector", lambda e, tt=tt: e.tensor_scalar_mul(
                    cap(h_sp, tt), cap(h_r, tt), cap(h_t2, tt)), inc=True)
                tad = p.do("vector", lambda e, tt=tt: e.tensor_scalar_mul(
                    cap(h_ad, tt), cap(h_ax, tt), wsc.ap()[:, 4:5]), inc=True, drain=True)
                if debug and Q == 0 and ti == 0 and phases >= 4:
                    for c0, st_ in [(70, h_ax), (71, h_sp), (72, h_ad), (73, h_r), (74, h_t2)]:
                        p.dma("gpsimd", dbg5_d[:, c0:c0 + 1], cap(st_, tt),
                              waits=[tad], slow=True)
                for half in range(2):
                    hsl = slice(half * 4096, (half + 1) * 4096)
                    qht = AB.ap(tq_off, 4096)
                    p.do("vector", lambda e, hten=hten, hsl=hsl, tt=tt: e.tensor_scalar(
                        AF.ap(y1h_off, 4096), hten[:, hsl], cap(h_sp, tt), C32,
                        ALU.mult, ALU.add), waits=[tq_free], inc=True)
                    p.do("vector", lambda e, qht=qht: e.tensor_single_scalar(
                        qht, AF.ap(y1h_off, 4096), -C32, ALU.add), inc=True)
                    p.do("vector", lambda e, qht=qht: e.tensor_scalar(
                        AB.ap(t2s_off, 4096), qht, C52, -C52, ALU.add, ALU.add), inc=True)
                    tqq = p.do("vector", lambda e, qht=qht: e.scalar_tensor_tensor(
                        qht, qht, omd.ap(), AB.ap(t2s_off, 4096), ALU.mult, ALU.add),
                        inc=True)
                    if debug and Q == 0 and ti == 0 and half == 0 and phases >= 4:
                        tqdump = p.dma("gpsimd", dbg2_d[:, 0:4096], qht, waits=[tqq])
                    evs = []
                    for f in range(32):
                        fg = half * 32 + f
                        trp = trp_of[f % 2]
                        tm2 = p.do("tensor", lambda e, qht=qht, f=f, trp=trp: e.transpose(
                            trp.ap()[:, 0:128], qht[:, f * 128:(f + 1) * 128],
                            ident_sb.ap()), waits=[tqq, tr_tok[f % 2]], inc=True)
                        tv_e = p.do("vector", lambda e, fg=fg, ti=ti, trp=trp: e.tensor_copy(
                            AB.ap(hq_off + ti * DI + fg * 128, 128), trp.ap()[:, 0:128]),
                            waits=[tm2], inc=True)
                        tr_tok[f % 2] = tv_e
                        evs.append(tv_e)
                    tq_free = evs[-1]
                    if debug and Q == 0 and ti == 0 and half == 0 and phases >= 4:
                        p.dma("gpsimd", dbg3_d[:, 0:4096], AB.ap(hq_off, 4096),
                              waits=[tq_free])
                if debug and Q == 0 and phases >= 4:
                    p.dma("gpsimd", dbg5_d[:, 104 + ti:105 + ti], cap(h_ssq, 0),
                          waits=[tq_free], slow=True)
                if debug and Q == 0 and ti == 3 and phases >= 4:
                    p.dma("gpsimd", dbg2_d[:, 4096:8192], AB.ap(hq_off, 4096),
                          waits=[tq_free])
                qhf_done.append(p.do("vector", lambda e: e.engine_nop(),
                                     waits=[tq_free], inc=True))
            td_tok = [None, None]
            ev_d = []
            for ob in range(4):
                dmm_last = [None] * 4
                for sl in range(4):
                    buf = wg_off if sl % 2 == 0 else wu_off
                    tok_td = p.dma(("sync", "gpsimd")[sl % 2],
                                   AB.t.ap()[:, buf:buf + NF * 512]
                                   .rearrange("p (c n) -> p c n", n=512),
                                   td_d[sl * 2048:(sl + 1) * 2048, ob * 512:(ob + 1) * 512]
                                   .rearrange("(c p) n -> p c n", p=128),
                                   waits=[td_tok[sl % 2]] +
                                   ([wgt_tok[0], wut_tok[0]] if ob == 0 and sl < 2 else []))
                    for ti in range(4):
                        for f in range(NF):
                            fg = sl * NF + f
                            dmm_last[ti] = p.do(
                                "tensor", lambda e, f=f, ti=ti, fg=fg, buf=buf: e.matmul(
                                    quad_ap(ti), AB.ap(hq_off + ti * DI + fg * 128, 128),
                                    AB.ap(buf + f * 512, 512),
                                    start=(fg == 0), stop=(fg == NI - 1)),
                                waits=[tok_td, quad_tok[ti]] + qhf_done if f == 0 else (),
                                inc=(f == NF - 1))
                    td_tok[sl % 2] = dmm_last[3]
                if debug and Q == 0 and phases >= 4:
                    p.dma("gpsimd", dbg5_d[:, 108 + ob:109 + ob], cap(h_ssq, 0),
                          waits=[td_tok[1] or td_tok[0]], slow=True)
                for ti in range(4):
                    tt = Q * 4 + ti
                    tsd = p.do("scalar", lambda e, ti=ti, ob=ob, tt=tt: e.activation(
                        AB.ap(qxq_off + ti * 2048 + ob * 512, 512), quad_ap(ti),
                        ACT.Silu, scale=h_ad.ap()[:, tt:tt + 1]),
                        waits=[dmm_last[ti], gq_guard], inc=True, drain=True)
                    quad_tok[ti] = tsd
                    ev_d.append(tsd)
                    if debug and Q == 0 and phases >= 4:
                        p.dma("gpsimd", dbg5_d[:, 112 + 4 * ob + ti:113 + 4 * ob + ti],
                              cap(h_ssq, 0), waits=[tsd], slow=True)
            if debug and Q == 0 and phases >= 4:
                p.dma("gpsimd", dbg3_d[:, 0:2048], AB.ap(qxq_off, 2048), waits=ev_d)
            hq_guard = td_tok[1]
            qx_guard = ev_d[-1]
            for ti in range(4):
                tt = Q * 4 + ti
                xf = AF.ap(xf_off, 2048)
                saf = AB.ap(saf_off, 2048)
                ou = AF.ap(ou_off, 2048)
                tok_x2 = p.dma("sync", xf, x_d[tt * 128:(tt + 1) * 128, :],
                               waits=[xf_guard])
                tok_sa = p.dma("sync", saf, sA_d[tt * 128:(tt + 1) * 128, :],
                               waits=[saf_guard] + p2_last)
                saf_guard = p.do("vector", lambda e, saf=saf, ti=ti: e.tensor_tensor(
                    saf, saf, AB.ap(qxq_off + ti * 2048, 2048), ALU.mult),
                    waits=[tok_sa] + ev_d, inc=True)
                tv_o = p.do("vector", lambda e, saf=saf, xf=xf, ou=ou: e.tensor_tensor(
                    ou, saf, xf, ALU.add), waits=[tok_x2, out_guard], inc=True)
                xf_guard = tv_o
                out_guard = p.dma("gpsimd", out_d[tt * 128:(tt + 1) * 128, :], ou,
                                  waits=[tv_o])
                final_toks.append(out_guard)
            gq_guard = saf_guard
            qx_guard = saf_guard
        p.do("gpsimd", lambda e: e.engine_nop(), waits=[final_toks[-1]])

    from contextlib import ExitStack
    with ExitStack() as stack:
        sems = {nm: stack.enter_context(nc.semaphore(nm)) for nm in ALL_SEMS}
        block = stack.enter_context(nc.Block())
        p.emit(block, sems)
    return nc


def _prep_shared(inputs):
    wqkv = inputs["L0_attn_wqkv"]
    perm = np.arange(DQK)
    for h in range(NH + NKV):
        base = h * HD
        perm[base:base + 64] = base + 2 * np.arange(64)
        perm[base + 64:base + 128] = base + 2 * np.arange(64) + 1
    return {
        "wqkvT": np.ascontiguousarray(wqkv[perm].T),
        "woT": np.ascontiguousarray(inputs["L0_attn_wo"].T),
        "wgT": np.ascontiguousarray(inputs["L0_ffn_w_gate"].T),
        "wuT": np.ascontiguousarray(inputs["L0_ffn_w_up"].T),
        "wdT": np.ascontiguousarray(inputs["L0_ffn_w_down"].T),
        "normw": np.ascontiguousarray(inputs["L0_attn_norm_w"].reshape(1, D)),
        "lnw": np.ascontiguousarray(inputs["L0_ffn_ln_weight"].reshape(1, DI)),
    }


def kernel(**inputs):
    x = np.asarray(inputs["L0_attn_x"], dtype=np.float32)
    shared = _prep_shared({k: np.asarray(v, dtype=np.float32)
                           for k, v in inputs.items() if "cache" not in k and k != "L0_attn_x"})
    nc = build_program(debug=False, phases=5)
    in_maps = []
    for c in range(NCORES):
        m = {"x": np.ascontiguousarray(x[c])}
        m.update(shared)
        in_maps.append(m)
    res = run_bass_kernel_spmd(nc, in_maps, core_ids=list(range(NCORES)))
    return np.stack([np.asarray(r["out"], dtype=np.float32) for r in res.results])


# version 10
# speedup vs baseline: 1.2458x; 1.2458x over previous
import numpy as np
import ml_dtypes
import concourse.bass as bass
import concourse.mybir as mybir
from concourse.bass_utils import run_bass_kernel_spmd

F32 = mybir.dt.float32
BF16 = mybir.dt.bfloat16
ALU = mybir.AluOpType
ACT = mybir.ActivationFunctionType
AX = mybir.AxisListType

NCORES = 8
S = 2048
D = 2048
DQK = 3072
DI = 8192
HD = 128
NH = 16
NKV = 4
NT = S // 128
NF = D // 128
NI = DI // 128

C32 = float(1.5 * 2 ** 23)
C52 = float(1.5 * 2 ** 52)
TWO30 = float(2 ** 30)
ATTN_SCALE = float(np.float32(1.0) / np.sqrt(np.float64(HD)).astype(np.float32))
INV127 = float(np.float32(1.0) / np.float32(127.0))

ENGS = ("sync", "gpsimd", "tensor", "vector", "scalar")
SEM_OF = {"sync": "sd", "gpsimd": "sg", "tensor": "st", "vector": "sv", "scalar": "ss"}
DMA_SEMS = {"sync": ["da0", "da1", "da2", "da3"],
            "gpsimd": ["ga0", "ga1", "ga2", "ga3"]}
ALL_SEMS = ["st", "sv", "ss", "sd", "sg"] + DMA_SEMS["sync"] + DMA_SEMS["gpsimd"]


class Arena:
    def __init__(self, nc, name, dtype, cols):
        self.t = nc.alloc_sbuf_tensor(name, [128, cols], dtype)
        self.free = [(0, cols)]
        self.name = name

    def take(self, n, align=16):
        for i, (off, sz) in enumerate(self.free):
            a = (off + align - 1) // align * align
            if a + n <= off + sz:
                repl = [p for p in ((off, a - off), (a + n, off + sz - a - n)) if p[1] > 0]
                self.free[i:i + 1] = repl
                return a
        raise MemoryError(f"arena {self.name}: need {n}, free={self.free}")

    def give(self, off, n):
        self.free.append((off, n))
        self.free.sort()
        m = []
        for o, s_ in self.free:
            if m and m[-1][0] + m[-1][1] == o:
                m[-1] = (m[-1][0], m[-1][1] + s_)
            else:
                m.append((o, s_))
        self.free = m

    def ap(self, off, n):
        return self.t.ap()[:, off:off + n]


class Plan:
    def __init__(self, nc):
        self.nc = nc
        self.ops = {e: [] for e in ENGS}
        self.cnt = {s_: 0 for s_ in ALL_SEMS}
        self.rr = {"sync": 0, "gpsimd": 0}
        self.maxw = {e: {} for e in ENGS}

    def _waits(self, eng, waits):
        out = []
        own = SEM_OF[eng]
        for t in waits:
            if t is None:
                continue
            sem, val = t
            if sem == own or val <= self.maxw[eng].get(sem, 0):
                continue
            self.maxw[eng][sem] = val
            out.append((sem, val))
        return out

    def do(self, eng, fn, waits=(), inc=False, drain=False):
        w = self._waits(eng, waits)
        tok = None
        auto = eng in ("vector", "scalar")
        if inc or auto:
            sem = SEM_OF[eng]
            self.cnt[sem] += 1
            tok = (sem, self.cnt[sem])
        self.ops[eng].append((fn, w, tok, drain, 1))
        return tok

    def barrier(self):
        return [(sem, cnt) for sem, cnt in self.cnt.items() if cnt > 0]

    def dma(self, q, dst, src, waits=(), slow=False):
        sems = DMA_SEMS[q]
        sem = sems[self.rr[q]]
        self.rr[q] = (self.rr[q] + 1) % len(sems)
        w = self._waits(q, list(waits) + [(sem, self.cnt[sem])])
        self.cnt[sem] += 16
        tok = (sem, self.cnt[sem])
        self.ops[q].append((lambda e, d=dst, s_=src, sl=slow: e.dma_start(
            d, s_, allow_slow_non_contiguous=sl) if sl else e.dma_start(d, s_),
            w, tok, False, 16))
        return tok

    def emit(self, block, sems):
        def runner(eng_name):
            chain = eng_name in ("vector", "scalar")
            own = SEM_OF.get(eng_name)

            def body(e):
                prev = 0
                for (fn, waits, tok, drain, incn) in self.ops[eng_name]:
                    for (sem, val) in waits:
                        e.wait_ge(sems[sem], val)
                    if chain and prev > 0:
                        e.wait_ge(sems[own], prev)
                    inst = fn(e)
                    if tok is not None:
                        if drain:
                            e.drain()
                            e.nop().then_inc(sems[tok[0]], incn)
                        else:
                            inst.then_inc(sems[tok[0]], incn)
                        if chain:
                            prev = tok[1]
                    elif drain:
                        e.drain()
            return body

        block.sync(runner("sync"))
        block.gpsimd(runner("gpsimd"))
        block.tensor(runner("tensor"))
        block.vector(runner("vector"))
        block.scalar(runner("scalar"))


def cap(t, i):
    return t.ap()[:, i:i + 1]


def _consts():
    tvec = np.arange(S, dtype=np.float32).reshape(1, S)
    ivec = (-(np.arange(0, HD, 2, dtype=np.float64) / HD)).astype(np.float32).reshape(1, 64)
    ii = np.arange(128)[:, None]
    jj = np.arange(1024)[None, :]
    maskb = (jj >= ii + 512).astype(ml_dtypes.bfloat16)
    ident = np.eye(128, dtype=ml_dtypes.bfloat16)
    return tvec, ivec, maskb, ident


def build_program(debug=False, phases=5, p2_qbs=4, p3_qs=4):
    nc = bass.Bass("TRN2", target_bir_lowering=False)

    x_d = nc.dram_tensor("x", [S, D], F32, kind="ExternalInput").ap()
    wqkvT_d = nc.dram_tensor("wqkvT", [D, DQK], F32, kind="ExternalInput").ap()
    woT_d = nc.dram_tensor("woT", [D, D], F32, kind="ExternalInput").ap()
    wgT_d = nc.dram_tensor("wgT", [D, DI], F32, kind="ExternalInput").ap()
    wuT_d = nc.dram_tensor("wuT", [D, DI], F32, kind="ExternalInput").ap()
    wdT_d = nc.dram_tensor("wdT", [DI, D], F32, kind="ExternalInput").ap()
    normw_d = nc.dram_tensor("normw", [1, D], F32, kind="ExternalInput").ap()
    lnw_d = nc.dram_tensor("lnw", [1, DI], F32, kind="ExternalInput").ap()

    tvec_np, ivec_np, maskb_np, ident_np = _consts()
    tvec_d = nc.inline_tensor(tvec_np, "tvecc").ap()
    ivec_d = nc.inline_tensor(ivec_np, "ivecc").ap()
    maskb_d = nc.inline_tensor(maskb_np, "maskBc").ap()
    ident_d = nc.inline_tensor(ident_np, "identc").ap()

    out_d = nc.dram_tensor("out", [S, D], F32, kind="ExternalOutput").ap()
    sk = "ExternalOutput" if debug else "Internal"
    qxf_d = nc.dram_tensor("qxf", [D, S], BF16, kind=sk).ap()
    qf_d = nc.dram_tensor("qf", [D, S], BF16, kind=sk).ap()
    kf_d = nc.dram_tensor("kf", [NKV * HD, S], BF16, kind=sk).ap()
    sA_d = nc.dram_tensor("sA", [S, D], BF16, kind=sk).ap()
    tg_d = nc.dram_tensor("tg", [D, DI], BF16, kind=sk).ap()
    tu_d = nc.dram_tensor("tu", [D, DI], BF16, kind=sk).ap()
    td_d = nc.dram_tensor("td", [DI, D], BF16, kind=sk).ap()
    dbg_d = nc.dram_tensor("dbgv", [128, 256], F32, kind=sk).ap() if debug else None
    dbg2_d = nc.dram_tensor("dbg2", [128, 8192], BF16, kind=sk).ap() if debug else None
    dbg3_d = nc.dram_tensor("dbg3", [128, 8192], BF16, kind=sk).ap() if debug else None
    dbg4_d = nc.dram_tensor("dbg4", [128, 8192], BF16, kind=sk).ap() if debug else None
    dbg5_d = nc.dram_tensor("dbg5", [128, 128], F32, kind=sk).ap() if debug else None

    ident_sb = nc.alloc_sbuf_tensor("identb", [128, 128], BF16)
    mask_sb = nc.alloc_sbuf_tensor("maskb", [128, 1024], BF16)
    ropeC_sb = nc.alloc_sbuf_tensor("ropec", [128, NT * 64], BF16)
    ropeS_sb = nc.alloc_sbuf_tensor("ropes", [128, NT * 64], BF16)
    normbc = nc.alloc_sbuf_tensor("normbc", [128, D], BF16)
    s128bc = nc.alloc_sbuf_tensor("s128bc", [128, 1], F32)
    lnwbc = nc.alloc_sbuf_tensor("lnwbc", [128, DI], BF16)
    ones1f = nc.alloc_sbuf_tensor("ones1f", [1, 128], F32)
    onescol = nc.alloc_sbuf_tensor("onescol", [128, 1], F32)
    sq_scr = nc.alloc_sbuf_tensor("sqscr", [128, 4096], BF16)

    def stats(name):
        return nc.alloc_sbuf_tensor(name, [128, NT], F32)

    st_sum = stats("c_sum"); st_ssq = stats("c_ssq"); st_max = stats("c_max")
    st_min = stats("c_min"); st_m = stats("c_m"); st_t1 = stats("c_t1")
    st_t2 = stats("c_t2"); st_r = stats("c_r"); st_ax = stats("c_ax")
    st_sp = stats("c_sp"); st_bp = stats("c_bp")
    st_ag = stats("c_ag"); st_au = stats("c_au")
    o_ssq = stats("o_ssq"); o_r = stats("o_r"); o_ax = stats("o_ax")
    o_sp = stats("o_sp"); o_t1 = stats("o_t1"); o_t2 = stats("o_t2")
    o_inv = stats("o_inv")
    h_ssq = stats("h_ssq"); h_ssq2 = stats("h_ssq2"); h_r = stats("h_r")
    h_ax = stats("h_ax"); h_sp = stats("h_sp"); h_t1 = stats("h_t1")
    h_t2 = stats("h_t2"); h_ad = stats("h_ad")
    dvec = nc.alloc_sbuf_tensor("dvec", [128, 1], F32)
    omd = nc.alloc_sbuf_tensor("omd", [128, 1], F32)
    wsc = nc.alloc_sbuf_tensor("wsc", [128, 8], F32)
    wacc = nc.alloc_sbuf_tensor("wacc", [128, 128], F32)
    wtot = nc.alloc_sbuf_tensor("wtot", [128, 1], F32)

    AB = Arena(nc, "ab", BF16, 69632)
    AF = Arena(nc, "af", F32, 8704)

    quad = nc.alloc_psum_tensor("quad", [128, 2048], F32)
    pv_a = nc.alloc_psum_tensor("pva", [128, 512], F32)
    pv_b = nc.alloc_psum_tensor("pvb", [128, 512], F32)
    tr_a = nc.alloc_psum_tensor("tra", [128, 512], BF16)
    tr_b = nc.alloc_psum_tensor("trb", [128, 512], BF16)
    trp_of = (tr_a, tr_b)

    p = Plan(nc)

    def quad_ap(i, n=512):
        return quad.ap()[:, i * 512:i * 512 + n]

    quad_tok = [None, None, None, None]
    tr_tok = [None, None]
    pv_tok = [None, None]

    g_toks = [
        p.dma("sync", ident_sb.ap(), ident_d),
        p.dma("sync", mask_sb.ap(), maskb_d),
        p.dma("gpsimd", normbc.ap(), bass.AP(normw_d.tensor, 0, [[0, 128], [1, D]])),
        p.dma("gpsimd", lnwbc.ap(), bass.AP(lnw_d.tensor, 0, [[0, 128], [1, DI]])),
    ]
    nb_toks = [g_toks[2]]
    lnw_done = g_toks[3]
    p.do("vector", lambda e: e.memset(ones1f.ap(), 1.0), inc=True, drain=True)
    p.do("vector", lambda e: e.memset(onescol.ap(), 1.0), inc=True, drain=True)
    p.do("vector", lambda e: e.tensor_scalar(dvec.ap(), onescol.ap(), TWO30, -TWO30,
                                             ALU.add, ALU.add), inc=True, drain=True)
    t_ones = p.do("vector", lambda e: e.engine_nop(), inc=True)
    t_omd = p.do("vector", lambda e: e.tensor_scalar(omd.ap(), dvec.ap(), -1.0, 1.0,
                                                     ALU.mult, ALU.add), inc=True, drain=True)
    p.do("vector", lambda e: e.memset(wsc.ap(), 0.0), inc=True)
    p.do("vector", lambda e: e.memset(wacc.ap(), 0.0), inc=True)

    import math as _math
    _tpi = 2 * _math.pi
    _c1 = float(np.float32(_tpi))
    _c2 = float(np.float32(_tpi - _c1))
    _c3 = float(np.float32(_tpi - _c1 - _c2))
    _in2pi = float(np.float32(1.0 / _tpi))
    _d1 = float(np.float32(_math.pi / 2))
    _d2 = float(np.float32(_math.pi / 2 - np.float64(np.float32(_math.pi / 2))))
    rb_off = AF.take(3 * 1024 + 2048 + 128)
    trow = AF.ap(rb_off + 3 * 1024, 2048)[0:1, :]
    irow = AF.ap(rb_off + 3 * 1024 + 2048, 64)[0:1, :]
    lnth = AF.ap(rb_off + 3 * 1024 + 2048 + 64, 1)[0:1, :]
    frow = AF.ap(rb_off + 3 * 1024 + 2048 + 65, 63 + 1)[0:1, 0:64]
    angb = AF.ap(rb_off, 1024)
    ktb = AF.ap(rb_off + 1024, 1024)
    k2b = AF.ap(rb_off + 2048, 1024)
    tok_tv = p.dma("sync", trow, tvec_d)
    tok_iv = p.dma("sync", irow, ivec_d)
    tv_th = p.do("vector", lambda e: e.memset(lnth, 500000.0), inc=True, drain=True)
    ts_fr = p.do("scalar", lambda e: (
        e.activation(lnth, lnth, ACT.Ln), e.drain(),
        e.activation(frow, irow, ACT.Exp, scale=lnth))[-1],
        waits=[tok_tv, tok_iv, tv_th], inc=True, drain=True)
    ang_evs = []
    for ti in range(NT):
        tmr = p.do("tensor", lambda e, ti=ti: e.matmul(
            quad.ap()[:, 64:128], trow[0:1, ti * 128:(ti + 1) * 128], frow,
            start=True, stop=True), waits=[ts_fr, quad_tok[1]] + (
                [ang_evs[-1]] if ang_evs else []), inc=True)
        tve = p.do("vector", lambda e, ti=ti: e.tensor_copy(
            angb[:, ti * 64:(ti + 1) * 64], quad.ap()[:, 64:128]),
            waits=[tmr], inc=True, drain=True)
        ang_evs.append(tve)
    quad_tok[1] = ang_evs[-1]
    rope_done = None
    for half, dst in ((0, ropeS_sb), (1, ropeC_sb)):
        if half == 1:
            p.do("vector", lambda e: e.tensor_scalar(
                angb, angb, _d1, _d2, ALU.add, ALU.add),
                waits=[ang_evs[-1], rope_done], inc=True, drain=True)
        p.do("vector", lambda e: e.tensor_scalar(
            ktb, angb, _in2pi, C32, ALU.mult, ALU.add),
            waits=[t_omd, ang_evs[-1]], inc=True, drain=True)
        p.do("vector", lambda e: e.tensor_single_scalar(
            ktb, ktb, -C32, ALU.add), inc=True, drain=True)
        p.do("vector", lambda e: e.tensor_scalar(
            k2b, ktb, C52, -C52, ALU.add, ALU.add), inc=True, drain=True)
        p.do("vector", lambda e: e.scalar_tensor_tensor(
            ktb, ktb, omd.ap(), k2b, ALU.mult, ALU.add), inc=True, drain=True)
        p.do("vector", lambda e: e.scalar_tensor_tensor(
            k2b, ktb, -_c1, angb, ALU.mult, ALU.add), inc=True, drain=True)
        p.do("vector", lambda e: e.scalar_tensor_tensor(
            k2b, ktb, -_c2, k2b, ALU.mult, ALU.add), inc=True, drain=True)
        p.do("vector", lambda e: e.scalar_tensor_tensor(
            k2b, ktb, -_c3, k2b, ALU.mult, ALU.add), inc=True, drain=True)
        tred = p.do("vector", lambda e: e.tensor_scalar(
            k2b, k2b, 3.141592502593994, -3.141592502593994, ALU.min, ALU.max), inc=True, drain=True)
        rope_done = p.do("scalar", lambda e, dst=dst: e.activation(
            dst.ap(), k2b, ACT.Sin), waits=[tred], inc=True, drain=True)
    tv_s1 = p.do("vector", lambda e: e.memset(lnth, 128.0),
                 waits=[rope_done], inc=True, drain=True)
    ts_s1 = p.do("scalar", lambda e: (
        e.activation(lnth, lnth, ACT.Ln), e.drain(),
        e.activation(lnth, lnth, ACT.Exp, scale=-0.5))[-1],
        waits=[tv_s1], inc=True, drain=True)
    tms = p.do("tensor", lambda e: e.matmul(
        quad.ap()[:, 4:5], ones1f.ap(), lnth, start=True, stop=True),
        waits=[ts_s1, t_ones, quad_tok[0]], inc=True)
    t_s128 = p.do("vector", lambda e: e.tensor_copy(
        s128bc.ap(), quad.ap()[:, 4:5]), waits=[tms], inc=True, drain=True)
    quad_tok[0] = t_s128
    AF.give(rb_off, 3 * 1024 + 2048 + 128)

    WP = 2048

    def phase_w_gen(wT, tern_d, nrows_chunks, wide, col_mc, col_s, done_sink):
        npieces_per_chunk = wide // WP
        npieces = nrows_chunks * npieces_per_chunk
        bar_w1 = p.barrier()
        wb_off = AF.take(2 * WP)
        red_tok = [None, None]
        wsrc = wT.rearrange("(c p) n -> c p n", p=128)
        for i in range(npieces):
            ch, pc = divmod(i, npieces_per_chunk)
            tok_d = p.dma(("sync", "gpsimd")[i % 2], AF.ap(wb_off + (i % 2) * WP, WP),
                          wsrc[ch][:, pc * WP:(pc + 1) * WP],
                          waits=[red_tok[i % 2]] + (bar_w1 if i < 2 else []))
            red_tok[i % 2] = p.do("vector", lambda e, i=i: e.reduce_sum(
                wacc.ap()[:, i:i + 1], AF.ap(wb_off + (i % 2) * WP, WP),
                axis=AX.X, apply_absolute_value=True), waits=[tok_d], inc=True,
                drain=False)
            yield
        tvt = p.do("vector", lambda e, n=npieces: e.reduce_sum(
            wtot.ap(), wacc.ap()[:, 0:n], axis=AX.X), waits=red_tok, inc=True)
        tm = p.do("tensor", lambda e: e.matmul(
            quad.ap()[0:1, 0:1], wtot.ap(), onescol.ap(), start=True, stop=True),
            waits=[tvt, quad_tok[0]], inc=True)
        inv_nel = 1.0 / (128.0 * nrows_chunks * wide)
        tv1 = p.do("vector", lambda e, iv=inv_nel: e.tensor_scalar(
            wsc.ap()[0:1, 6:7], quad.ap()[0:1, 0:1], iv, 1e-5, ALU.mult, ALU.max),
            waits=[tm], inc=True)
        tm2 = p.do("tensor", lambda e: e.matmul(
            quad.ap()[:, 4:5], ones1f.ap(), wsc.ap()[0:1, 6:7], start=True, stop=True),
            waits=[tv1], inc=True)
        p.do("vector", lambda e, c=col_mc: e.tensor_copy(
            wsc.ap()[:, c:c + 1], quad.ap()[:, 4:5]), waits=[tm2], inc=True)
        tv3 = p.do("vector", lambda e, c=col_mc, c2=col_s: e.reciprocal(
            wsc.ap()[:, c2:c2 + 1], wsc.ap()[:, c:c + 1]), inc=True)
        quad_tok[0] = tv3
        AF.give(wb_off, 2 * WP)
        yield

        bar_w2 = p.barrier()
        wb2 = AF.take(2 * WP)
        t1o = AB.take(WP)
        t2o = AB.take(WP)
        too = AB.take(2 * WP)
        tdst = tern_d.rearrange("(c p) n -> c p n", p=128)
        use_tok = [None, None]
        out_tok = [None, None]
        for i in range(npieces):
            ch, pc = divmod(i, npieces_per_chunk)
            tok_d = p.dma(("sync", "gpsimd")[i % 2], AF.ap(wb2 + (i % 2) * WP, WP),
                          wsrc[ch][:, pc * WP:(pc + 1) * WP],
                          waits=[use_tok[i % 2]] + ((bar_w2 + [tv3]) if i < 2 else []))
            sap = wsc.ap()[:, col_s:col_s + 1]
            p.do("vector", lambda e, i=i, sap=sap: e.tensor_scalar(
                AB.ap(t1o, WP), AF.ap(wb2 + (i % 2) * WP, WP), sap, 0.5,
                ALU.mult, ALU.is_ge),
                waits=[tok_d, tv3, out_tok[i % 2]], inc=True)
            p.do("vector", lambda e, i=i, sap=sap: e.tensor_scalar(
                AB.ap(t2o, WP), AF.ap(wb2 + (i % 2) * WP, WP), sap, -0.5,
                ALU.mult, ALU.is_gt), inc=True)
            use_tok[i % 2] = tvq = p.do("vector", lambda e, i=i: e.scalar_tensor_tensor(
                AB.ap(too + (i % 2) * WP, WP), AB.ap(t1o, WP), -1.0, AB.ap(t2o, WP),
                ALU.add, ALU.add), inc=True)
            out_tok[i % 2] = p.dma(("gpsimd", "sync")[i % 2], tdst[ch][:, pc * WP:(pc + 1) * WP],
                                   AB.ap(too + (i % 2) * WP, WP), waits=[tvq])
            yield
        AF.give(wb2, 2 * WP)
        AB.give(t1o, WP); AB.give(t2o, WP); AB.give(too, 2 * WP)
        done_sink.extend(t for t in out_tok if t)
        yield

    tern_done = []

    def _w_gens():
        yield from phase_w_gen(wgT_d, tg_d, NF, DI, 0, 1, tern_done)
        yield from phase_w_gen(wuT_d, tu_d, NF, DI, 2, 3, tern_done)
        yield from phase_w_gen(wdT_d, td_d, NI, D, 4, 5, tern_done)

    w_gen = _w_gens() if phases >= 1 else iter(())

    def drive_w(n):
        for _ in range(n):
            try:
                next(w_gen)
            except StopIteration:
                return

    qxf_off = AB.take(NT * 2048)

    def qxf_ap(f, c0, n):
        return AB.ap(qxf_off + f * 2048 + c0, n)

    p0_last = []
    ag_tok = None
    qxf_done = [None] * NT
    if phases >= 1:
        bar_p0 = p.barrier()
        xt_off = AF.take(2 * 2048)
        y1_off = AF.take(2048)
        t1_off = AB.take(2048)
        t2_off = AB.take(2048)
        qxt_off = AB.take(2 * 2048)
        xt_busy = [None, None]
        qxt_tok = [None, None]
        last_ev = None
        for ti in range(NT):
            xt = AF.ap(xt_off + (ti % 2) * 2048, 2048)
            tok_x = p.dma("sync", xt, x_d[ti * 128:(ti + 1) * 128, :],
                          waits=[xt_busy[ti % 2]] + (bar_p0 if ti < 2 else []))
            p.do("vector", lambda e, xt=xt, ti=ti: e.reduce_sum(
                cap(st_sum, ti), xt, axis=AX.X), waits=[tok_x], inc=True)
            p.do("vector", lambda e, xt=xt, ti=ti: e.reduce_max(
                cap(st_max, ti), xt, axis=AX.X), inc=True)
            p.do("vector", lambda e, xt=xt, ti=ti: e.tensor_reduce(
                cap(st_min, ti), xt, axis=AX.X, op=ALU.min), inc=True)
            ts_sq = p.do("scalar", lambda e, xt=xt, ti=ti: e.activation(
                sq_scr.ap()[:, 0:2048], xt, ACT.Square, accum_out=cap(st_ssq, ti)),
                waits=[tok_x], inc=True, drain=True)
            p.do("vector", lambda e, ti=ti: e.tensor_single_scalar(
                cap(st_m, ti), cap(st_sum, ti), 1.0 / 2048.0, ALU.mult), waits=[ts_sq], inc=True)
            p.do("vector", lambda e, ti=ti: e.tensor_scalar(
                cap(st_t1, ti), cap(st_m, ti), cap(st_m, ti), -1e-5, ALU.mult, ALU.add), inc=True)
            tv_var = p.do("vector", lambda e, ti=ti: e.scalar_tensor_tensor(
                cap(st_t2, ti), cap(st_ssq, ti), 1.0 / 2048.0, cap(st_t1, ti),
                ALU.mult, ALU.subtract), inc=True)
            ts_r0 = p.do("scalar", lambda e, ti=ti: (
                e.activation(cap(st_t1, ti), cap(st_t2, ti), ACT.Ln), e.drain(),
                e.activation(cap(st_r, ti), cap(st_t1, ti), ACT.Exp, scale=-0.5))[-1],
                waits=[tv_var], inc=True, drain=True)
            p.do("vector", lambda e, ti=ti: e.tensor_scalar(
                cap(st_t1, ti), cap(st_t2, ti), cap(st_r, ti), cap(st_r, ti), ALU.mult, ALU.mult),
                waits=[ts_r0], inc=True)
            p.do("vector", lambda e, ti=ti: e.tensor_scalar(
                cap(st_t1, ti), cap(st_t1, ti), -0.5, 1.5, ALU.mult, ALU.add), inc=True)
            p.do("vector", lambda e, ti=ti: e.tensor_scalar_mul(
                cap(st_r, ti), cap(st_r, ti), cap(st_t1, ti)), inc=True)
            p.do("vector", lambda e, ti=ti: e.tensor_single_scalar(
                cap(st_t1, ti), cap(st_max, ti), cap(st_m, ti), ALU.subtract), inc=True)
            p.do("vector", lambda e, ti=ti: e.tensor_scalar(
                cap(st_t2, ti), cap(st_min, ti), -1.0, cap(st_m, ti), ALU.mult, ALU.add), inc=True)
            p.do("vector", lambda e, ti=ti: e.tensor_max(
                cap(st_t1, ti), cap(st_t1, ti), cap(st_t2, ti)), inc=True)
            p.do("vector", lambda e, ti=ti: e.tensor_scalar(
                cap(st_t1, ti), cap(st_t1, ti), cap(st_r, ti), 1e-5, ALU.mult, ALU.max), inc=True)
            p.do("vector", lambda e, ti=ti: e.tensor_single_scalar(
                cap(st_ax, ti), cap(st_t1, ti), INV127, ALU.mult), inc=True)
            p.do("vector", lambda e, ti=ti: e.reciprocal(
                cap(st_t2, ti), cap(st_t1, ti)), inc=True)
            p.do("vector", lambda e, ti=ti: e.tensor_single_scalar(
                cap(st_t2, ti), cap(st_t2, ti), 127.0, ALU.mult), inc=True)
            p.do("vector", lambda e, ti=ti: e.tensor_scalar_mul(
                cap(st_sp, ti), cap(st_r, ti), cap(st_t2, ti)), inc=True)
            p.do("vector", lambda e, ti=ti: e.tensor_scalar(
                cap(st_bp, ti), cap(st_m, ti), cap(st_sp, ti), -1.0, ALU.mult, ALU.mult), inc=True)
            y1 = AF.ap(y1_off, 2048)
            t1b = AB.ap(t1_off, 2048)
            t2b = AB.ap(t2_off, 2048)
            qxt = AB.ap(qxt_off + (ti % 2) * 2048, 2048)
            xt_busy[ti % 2] = p.do("vector", lambda e, xt=xt, y1=y1, ti=ti: e.tensor_scalar(
                y1, xt, cap(st_sp, ti), cap(st_bp, ti), ALU.mult, ALU.add),
                waits=[qxt_tok[ti % 2]], inc=True)
            p.do("vector", lambda e, y1=y1, t1b=t1b: e.tensor_scalar(
                t1b, y1, C32, -C32, ALU.add, ALU.add), inc=True)
            p.do("vector", lambda e, t1b=t1b, t2b=t2b: e.tensor_scalar(
                t2b, t1b, C52, -C52, ALU.add, ALU.add), inc=True)
            tv_q = p.do("vector", lambda e, t1b=t1b, t2b=t2b, qxt=qxt: e.scalar_tensor_tensor(
                qxt, t1b, omd.ap(), t2b, ALU.mult, ALU.add),
                waits=[t_omd], inc=True)
            for j in range(NF):
                trp = trp_of[j % 2]
                tm = p.do("tensor", lambda e, qxt=qxt, j=j, trp=trp: e.transpose(
                    trp.ap()[:, 0:128], qxt[:, j * 128:(j + 1) * 128], ident_sb.ap()),
                    waits=[tv_q, tr_tok[j % 2], g_toks[0]], inc=True)
                tv_e = p.do("vector", lambda e, j=j, ti=ti, trp=trp: e.tensor_copy(
                    qxf_ap(j, ti * 128, 128), trp.ap()[:, 0:128]), waits=[tm], inc=True)
                tr_tok[j % 2] = tv_e
                if j == NF - 1:
                    qxt_tok[ti % 2] = tv_e
                    qxf_done[ti] = tv_e
                    last_ev = tv_e
        tv_d = p.do("vector", lambda e: e.engine_nop(), waits=[last_ev], inc=True)
        qxf_store = p.dma("gpsimd", qxf_d.rearrange("(c p) n -> p c n", p=128),
                          AB.t.ap()[:, qxf_off:qxf_off + NT * 2048]
                          .rearrange("p (c n) -> p c n", n=2048), waits=[tv_d])
        p0_last = [qxf_store, tv_d]
        p.do("vector", lambda e: e.tensor_scalar_mul(
            st_ag.ap(), st_ax.ap(), wsc.ap()[:, 0:1]), waits=[tv_d], inc=True)
        ag_tok = p.do("vector", lambda e: e.tensor_scalar_mul(
            st_au.ap(), st_ax.ap(), wsc.ap()[:, 2:3]), inc=True, drain=True)
        if debug:
            for nm, src_t, c0 in [("sum", st_sum, 0), ("ssq", st_ssq, 16),
                                  ("max", st_max, 32), ("min", st_min, 48),
                                  ("m", st_m, 64), ("r", st_r, 80),
                                  ("ax", st_ax, 96), ("sp", st_sp, 112),
                                  ("bp", st_bp, 128), ("wsc", wsc, 144),
                                  ("wacc", wacc, 160)]:
                n = src_t.shape[1] if hasattr(src_t, 'shape') else NT
                p.dma("gpsimd", dbg_d[:, c0:c0 + src_t.ap().shape[-1]],
                      src_t.ap(), waits=[ag_tok] + tern_done, slow=True)
            p.dma("gpsimd", dbg_d[:, 240:241], wtot.ap(), waits=[ag_tok], slow=True)
            p.dma("gpsimd", dbg_d[:, 241:242], dvec.ap(), waits=[ag_tok], slow=True)
            p.dma("gpsimd", dbg_d[:, 242:243], omd.ap(), waits=[ag_tok], slow=True)

    p1_last = []
    vtok_off = None
    if phases >= 2:
        vtok_off = AB.take(NT * 516)
        wkb_off = AB.take(2 * NF * 512)
        AF.give(xt_off, 2 * 2048); AF.give(y1_off, 2048)
        qev_off = AF.take(2 * 512)
        ta_off = AF.take(256)
        tb_off = AF.take(256)
        rq_off = AB.take(2 * 512)
        assert t2_off == t1_off + 2048 and qxt_off == t2_off + 2048
        stage_off = t1_off
        vt_set = p.do("vector", lambda e: e.memset(AB.ap(vtok_off, NT * 516), 1.0),
                      inc=True)

        def vt_dst(ti):
            return AB.t.ap()[:, vtok_off + ti * 516:vtok_off + (ti + 1) * 516] \
                .rearrange("p (g n) -> p g n", n=129)[:, :, 0:128]

        def rope_bcast(tab, ti):
            return bass.AP(tab, ti * 64, [[NT * 64, 128], [0, 4], [1, 64]])

        OBLK = [("q", 0, 0), ("q", 1, 512), ("q", 2, 1024), ("q", 3, 1536),
                ("k", 0, 2048), ("v", 0, 2560)]
        st_last = [None, None]
        stage_tok = None
        for m, (sec, bi, c0) in enumerate(OBLK):
            tok_w = p.dma("gpsimd",
                          AB.t.ap()[:, wkb_off + (m % 2) * NF * 512:
                                    wkb_off + (m % 2 + 1) * NF * 512]
                          .rearrange("p (c n) -> p c n", n=512),
                          wqkvT_d[:, c0:c0 + 512].rearrange("(c p) n -> p c n", p=128),
                          waits=[st_last[m % 2]])
            for ti in range(NT):
                qi = ti % 4
                tm = None
                for f in range(NF):
                    tm = p.do("tensor", lambda e, f=f, ti=ti, m=m, qi=qi: e.matmul(
                        quad_ap(qi), qxf_ap(f, ti * 128, 128),
                        AB.ap(wkb_off + (m % 2) * NF * 512 + f * 512, 512),
                        start=(f == 0), stop=(f == NF - 1)),
                        waits=[tok_w, quad_tok[qi], qxf_done[ti]] if f == 0 else (),
                        inc=(f == NF - 1))
                st_last[m % 2] = tm
                axc = st_ax.ap()[:, ti:ti + 1]
                if sec == "v":
                    tv = p.do("vector", lambda e, ti=ti, qi=qi, axc=axc: e.tensor_scalar_mul(
                        vt_dst(ti), quad_ap(qi), axc), waits=[tm, vt_set],
                        inc=True)
                    quad_tok[qi] = tv
                    p1_last.append(tv)
                else:
                    qev = AF.ap(qev_off + (ti % 2) * 512, 512)
                    tv0 = p.do("vector", lambda e, qev=qev, qi=qi, axc=axc: e.tensor_scalar_mul(
                        qev, quad_ap(qi), axc), waits=[tm, rope_done], inc=True)
                    quad_tok[qi] = tv0
                    qa = qev.rearrange("p (h n) -> p h n", n=128)[:, :, 0:64]
                    qb_ = qev.rearrange("p (h n) -> p h n", n=128)[:, :, 64:128]
                    cA = rope_bcast(ropeC_sb, ti)
                    sA = rope_bcast(ropeS_sb, ti)
                    ta = AF.ap(ta_off, 256).rearrange("p (h n) -> p h n", n=64)
                    tb = AF.ap(tb_off, 256).rearrange("p (h n) -> p h n", n=64)
                    rq = AB.ap(rq_off + (ti % 2) * 512, 512)
                    ra = rq.rearrange("p (h n) -> p h n", n=128)[:, :, 0:64]
                    rb = rq.rearrange("p (h n) -> p h n", n=128)[:, :, 64:128]
                    p.do("vector", lambda e, ta=ta, qa=qa, cA=cA: e.tensor_tensor(
                        ta, qa, cA, ALU.mult), inc=True)
                    p.do("vector", lambda e, tb=tb, qb_=qb_, sA=sA: e.tensor_tensor(
                        tb, qb_, sA, ALU.mult), inc=True)
                    p.do("vector", lambda e, ra=ra, ta=ta, tb=tb: e.tensor_tensor(
                        ra, ta, tb, ALU.subtract), inc=True)
                    p.do("vector", lambda e, ta=ta, qa=qa, sA=sA: e.tensor_tensor(
                        ta, qa, sA, ALU.mult), inc=True)
                    p.do("vector", lambda e, tb=tb, qb_=qb_, cA=cA: e.tensor_tensor(
                        tb, qb_, cA, ALU.mult), inc=True)
                    tv_r = p.do("vector", lambda e, rb=rb, ta=ta, tb=tb: e.tensor_tensor(
                        rb, ta, tb, ALU.add), inc=True)
                    for r in range(4):
                        trp = trp_of[r % 2]
                        tm2 = p.do("tensor", lambda e, rq=rq, r=r, trp=trp: e.transpose(
                            trp.ap()[:, 0:128], rq[:, r * 128:(r + 1) * 128], ident_sb.ap()),
                            waits=[tv_r, tr_tok[r % 2]] +
                            ([stage_tok] if ti == 0 else []), inc=True)
                        tv_e = p.do("vector", lambda e, r=r, ti=ti, trp=trp: e.tensor_copy(
                            AB.ap(stage_off + r * 2048 + ti * 128, 128),
                            trp.ap()[:, 0:128]), waits=[tm2], inc=True)
                        tr_tok[r % 2] = tv_e
                    if ti == NT - 1:
                        tv_d2 = p.do("vector", lambda e: e.engine_nop(), inc=True)
                        if sec == "q":
                            dst = qf_d.rearrange("(c p) n -> p c n", p=128)[:, bi * 4:(bi + 1) * 4, :]
                        else:
                            dst = kf_d.rearrange("(c p) n -> p c n", p=128)
                        stage_tok = p.dma("gpsimd", dst,
                                          AB.t.ap()[:, stage_off:stage_off + 4 * 2048]
                                          .rearrange("p (c n) -> p c n", n=2048),
                                          waits=[tv_d2])
                        p1_last.append(stage_tok)
        AB.give(wkb_off, 2 * NF * 512)
        AF.give(qev_off, 2 * 512); AF.give(ta_off, 256); AF.give(tb_off, 256)
        AB.give(rq_off, 2 * 512); AB.give(stage_off, 4 * 2048)
        AB.give(qxf_off, NT * 2048)

    p2_last = list(p1_last)
    if phases >= 3:
        bar_p2 = p.barrier()
        p1_last = p1_last + bar_p2
        kf_off = AB.take(NKV * 2048)
        probs_off = AB.take(NT * 512)
        wo_off = AB.take(NF * 512)
        ob_off = AB.take(4 * 2048)
        qof_off = AB.take(4 * 2048)
        sas_off = AB.take(4 * 2048)
        qft_off = AB.take(2 * 512)
        qo_off = AB.take(2048)
        y1_off2 = AF.take(2048)

        tok_kf = p.dma("sync", AB.t.ap()[:, kf_off:kf_off + NKV * 2048]
                       .rearrange("p (c n) -> p c n", n=2048),
                       kf_d.rearrange("(c p) n -> p c n", p=128), waits=p1_last)
        qf_tok = [None, None]
        probs_guard = None
        ob_guard = None
        qo_guard = None
        wo_mm_last = [None]
        sas_tok = [None, None, None, None]
        for qb in range(p2_qbs):
            nch = 4 * (qb + 1)
            pv_done = []
            for h in range(NH):
                g = h // 4
                hh = qb * NH + h
                tok_qf = p.dma("sync", AB.ap(qft_off + (hh % 2) * 512, 512),
                               qf_d[h * 128:(h + 1) * 128, qb * 512:(qb + 1) * 512],
                               waits=[qf_tok[hh % 2]] + p1_last)
                exp_toks = []
                for grp in range(qb + 1):
                    mm_toks = []
                    for cc in range(4):
                        c = grp * 4 + cc
                        tmm = p.do("tensor", lambda e, c=c, cc=cc, g=g, hh=hh: e.matmul(
                            quad_ap(cc),
                            AB.ap(kf_off + g * 2048 + c * 128, 128),
                            AB.ap(qft_off + (hh % 2) * 512, 512),
                            start=True, stop=True),
                            waits=[tok_qf, tok_kf, quad_tok[cc]], inc=True)
                        mm_toks.append(tmm)
                    qf_tok[hh % 2] = mm_toks[-1]
                    texp = p.do("scalar", lambda e, grp=grp: e.activation(
                        AB.ap(probs_off + grp * 2048, 2048), quad.ap(),
                        ACT.Exp, scale=s128bc.ap()),
                        waits=mm_toks + [probs_guard, t_s128], inc=True, drain=True)
                    exp_toks.append(texp)
                    for cc in range(4):
                        quad_tok[cc] = texp
                def _mask(e, qb=qb):
                    r = None
                    for cc in range(4):
                        r = e.tensor_tensor(
                            AB.ap(probs_off + (4 * qb + cc) * 512, 512),
                            AB.ap(probs_off + (4 * qb + cc) * 512, 512),
                            mask_sb.ap()[:, 512 - cc * 128:1024 - cc * 128], ALU.mult)
                    return r
                tmask = p.do("vector", _mask, waits=[exp_toks[-1]], inc=True)
                for j in range(4):
                    pvp = (pv_a, pv_b)[j % 2]
                    tmm = None
                    for c in range(nch):
                        tmm = p.do("tensor", lambda e, c=c, j=j, g=g, pvp=pvp, nch=nch: e.matmul(
                            pvp.ap()[:, 0:129],
                            AB.ap(probs_off + c * 512 + j * 128, 128),
                            AB.ap(vtok_off + c * 516 + g * 129, 129),
                            start=(c == 0), stop=(c == nch - 1)),
                            waits=[tmask, pv_tok[j % 2]] + exp_toks if c == 0 else (),
                            inc=(c == nch - 1))
                    jj = qb * 4 + j
                    p.do("vector", lambda e, pvp=pvp, jj=jj: e.reciprocal(
                        o_inv.ap()[:, jj:jj + 1], pvp.ap()[:, 128:129]),
                        waits=[tmm], inc=True)
                    tpv = p.do("vector", lambda e, pvp=pvp, jj=jj, j=j, h=h: e.tensor_scalar_mul(
                        AB.ap(ob_off + j * 2048 + h * 128, 128),
                        pvp.ap()[:, 0:128], o_inv.ap()[:, jj:jj + 1]),
                        waits=[ob_guard] if (h == 0 and ob_guard) else (),
                        inc=True)
                    pv_tok[j % 2] = tpv
                    if j == 3:
                        probs_guard = tmm
                    if h == NH - 1:
                        pv_done.append(tpv)
                drive_w(7)
            dbg2_tok = None
            if debug and qb == 0:
                dbg2_tok = p.dma("gpsimd", dbg2_d, AB.ap(ob_off, 8192), waits=pv_done)
            qof_done = []
            for j in range(4):
                jj = qb * 4 + j
                obj = AB.ap(ob_off + j * 2048, 2048)
                ts_sq = p.do("scalar", lambda e, obj=obj, jj=jj: e.activation(
                    sq_scr.ap()[:, 0:2048], obj, ACT.Square, accum_out=cap(o_ssq, jj)),
                    waits=pv_done, inc=True, drain=True)
                tnw = p.do("vector", lambda e, obj=obj: e.tensor_tensor(
                    obj, obj, normbc.ap(), ALU.mult),
                    waits=[ts_sq, dbg2_tok] + nb_toks, inc=True)
                tvv = p.do("vector", lambda e, jj=jj: e.tensor_scalar(
                    cap(o_t2, jj), cap(o_ssq, jj), 1.0 / 2048.0, 1e-5, ALU.mult, ALU.add),
                    inc=True)
                tsr = p.do("scalar", lambda e, jj=jj: (
                    e.activation(cap(o_t1, jj), cap(o_t2, jj), ACT.Ln), e.drain(),
                    e.activation(cap(o_r, jj), cap(o_t1, jj), ACT.Exp, scale=-0.5))[-1],
                    waits=[tvv], inc=True, drain=True)
                p.do("vector", lambda e, jj=jj: e.tensor_scalar(
                    cap(o_t1, jj), cap(o_t2, jj), cap(o_r, jj), cap(o_r, jj), ALU.mult, ALU.mult),
                    waits=[tsr], inc=True)
                p.do("vector", lambda e, jj=jj: e.tensor_scalar(
                    cap(o_t1, jj), cap(o_t1, jj), -0.5, 1.5, ALU.mult, ALU.add), inc=True)
                p.do("vector", lambda e, jj=jj: e.tensor_scalar_mul(
                    cap(o_r, jj), cap(o_r, jj), cap(o_t1, jj)), inc=True)
                p.do("vector", lambda e, obj=obj, jj=jj: e.reduce_max(
                    cap(o_t1, jj), obj, axis=AX.X, apply_absolute_value=True),
                    waits=[tnw], inc=True)
                p.do("vector", lambda e, jj=jj: e.tensor_scalar(
                    cap(o_t1, jj), cap(o_t1, jj), cap(o_r, jj), 1e-5, ALU.mult, ALU.max), inc=True)
                p.do("vector", lambda e, jj=jj: e.tensor_single_scalar(
                    cap(o_ax, jj), cap(o_t1, jj), INV127, ALU.mult), inc=True)
                p.do("vector", lambda e, jj=jj: e.reciprocal(
                    cap(o_t2, jj), cap(o_t1, jj)), inc=True)
                p.do("vector", lambda e, jj=jj: e.tensor_single_scalar(
                    cap(o_t2, jj), cap(o_t2, jj), 127.0, ALU.mult), inc=True)
                p.do("vector", lambda e, jj=jj: e.tensor_scalar_mul(
                    cap(o_sp, jj), cap(o_r, jj), cap(o_t2, jj)), inc=True)
                y1 = AF.ap(y1_off2, 2048)
                qo = AB.ap(qo_off, 2048)
                t2b = sq_scr.ap()[:, 2048:4096]
                p.do("vector", lambda e, obj=obj, y1=y1, jj=jj: e.tensor_scalar(
                    y1, obj, cap(o_sp, jj), C32, ALU.mult, ALU.add),
                    waits=[qo_guard], inc=True)
                p.do("vector", lambda e, y1=y1, qo=qo: e.tensor_single_scalar(
                    qo, y1, -C32, ALU.add), inc=True)
                p.do("vector", lambda e, qo=qo, t2b=t2b: e.tensor_scalar(
                    t2b, qo, C52, -C52, ALU.add, ALU.add), inc=True)
                tq = p.do("vector", lambda e, qo=qo, t2b=t2b: e.scalar_tensor_tensor(
                    qo, qo, omd.ap(), t2b, ALU.mult, ALU.add), inc=True)
                ob_guard = tq
                evs = []
                for f in range(NF):
                    trp = trp_of[f % 2]
                    tm2 = p.do("tensor", lambda e, qo=qo, f=f, trp=trp: e.transpose(
                        trp.ap()[:, 0:128], qo[:, f * 128:(f + 1) * 128], ident_sb.ap()),
                        waits=[tq, tr_tok[f % 2]] +
                        ([wo_mm_last[0]] if (f == 0 and wo_mm_last[0]) else []), inc=True)
                    tv_e = p.do("vector", lambda e, f=f, j=j, trp=trp: e.tensor_copy(
                        AB.ap(qof_off + j * 2048 + f * 128, 128), trp.ap()[:, 0:128]),
                        waits=[tm2], inc=True)
                    tr_tok[f % 2] = tv_e
                    evs.append(tv_e)
                qo_guard = evs[-1]
                qof_done.append(p.do("vector", lambda e: e.engine_nop(), waits=[evs[-1]],
                                     inc=True))
            if debug and qb == 0:
                dbg3_tok = p.dma("gpsimd", dbg3_d, AB.ap(qof_off, 8192), waits=qof_done)
                qof_done = qof_done + [dbg3_tok]
            for ob in range(4):
                tok_wo = []
                for hf in range(2):
                    tok_wo.append(p.dma(
                        "gpsimd",
                        AB.t.ap()[:, wo_off + hf * 8 * 512:wo_off + (hf + 1) * 8 * 512]
                        .rearrange("p (c n) -> p c n", n=512),
                        woT_d[hf * 1024:(hf + 1) * 1024, ob * 512:(ob + 1) * 512]
                        .rearrange("(c p) n -> p c n", p=128),
                        waits=[wo_mm_last[0]] + (p1_last if qb == 0 and ob == 0 else [])))
                for j in range(4):
                    jj = qb * 4 + j
                    qi = (2 * (ob % 2)) + (j % 2)
                    tmm = None
                    for f in range(NF):
                        tmm = p.do("tensor", lambda e, f=f, j=j, ob=ob, qi=qi: e.matmul(
                            quad_ap(qi), AB.ap(qof_off + j * 2048 + f * 128, 128),
                            AB.ap(wo_off + f * 512, 512),
                            start=(f == 0), stop=(f == NF - 1)),
                            waits=[tok_wo[0] if f < 8 else tok_wo[1], qof_done[j],
                                   quad_tok[qi]] if f in (0, 8) else (),
                            inc=(f == NF - 1))
                    wo_mm_last[0] = tmm
                    tv_s = p.do("vector", lambda e, ob=ob, j=j, qi=qi, jj=jj:
                                e.tensor_scalar_mul(
                                    AB.ap(sas_off + j * 2048 + ob * 512, 512),
                                    quad_ap(qi), o_ax.ap()[:, jj:jj + 1]),
                                waits=[tmm, sas_tok[j]], inc=True)
                    quad_tok[qi] = tv_s
                    if ob == 3:
                        sas_tok[j] = p.dma(
                            "gpsimd", sA_d[qb * 512 + j * 128: qb * 512 + (j + 1) * 128, :],
                            AB.ap(sas_off + j * 2048, 2048), waits=[tv_s])
            p2_last = [t for t in sas_tok if t]
        drive_w(10000)
        p.do("vector", lambda e: e.tensor_scalar_mul(
            st_ag.ap(), st_ax.ap(), wsc.ap()[:, 0:1]), inc=True, drain=True)
        ag_tok = p.do("vector", lambda e: e.tensor_scalar_mul(
            st_au.ap(), st_ax.ap(), wsc.ap()[:, 2:3]), inc=True, drain=True)
        AB.give(kf_off, NKV * 2048); AB.give(probs_off, NT * 512)
        AB.give(ob_off, 4 * 2048); AB.give(qft_off, 2 * 512)
        AB.give(qo_off, 2048); AB.give(qof_off, 4 * 2048)
        AB.give(sas_off, 4 * 2048); AB.give(wo_off, NF * 512)
        AF.give(y1_off2, 2048)
        AB.give(vtok_off, NT * 516)

    if phases >= 4:
        if phases < 3 or p2_qbs < 4:
            drive_w(10000)
        bar_p3 = p.barrier()
        p2_last = p2_last + bar_p3
        hq_off = AB.take(4 * DI)
        qxq_off = AB.take(NF * 512)
        if debug:
            print("P3 AB free after hq/qxq:", AB.free, "hq_off=", hq_off, "qxq_off=", qxq_off)
        wg_off = AB.take(NF * 512)
        wu_off = AB.take(NF * 512)
        sg_off = AB.take(2 * 512)
        tq_off = AB.take(4096)
        saf_off = AB.take(2048)
        t2s_off = AB.take(4096)
        if debug:
            print("P3 offs: wg", wg_off, "wu", wu_off, "sg", sg_off, "tq", tq_off,
                  "saf", saf_off, "free", AB.free)
        y1h_off = AF.take(4096)
        xf_off = AF.take(2048)
        ou_off = AF.take(2048)

        wgt_tok = [None]
        wut_tok = [None]
        sg_guard = [None, None]
        hq_guard = None
        qx_guard = None
        gq_guard = None
        saf_guard = None
        out_guard = None
        xf_guard = None
        final_toks = []
        for Q in range(p3_qs):
            tok_qx = p.dma("sync", AB.t.ap()[:, qxq_off:qxq_off + NF * 512]
                           .rearrange("p (c n) -> p c n", n=512),
                           qxf_d[:, Q * 512:(Q + 1) * 512]
                           .rearrange("(c p) n -> p c n", p=128),
                           waits=[qx_guard] + p0_last + p2_last)
            h_toks = []
            for ob in range(NF):
                tok_g = []
                tok_u = []
                for hf in range(2):
                    tok_g.append(p.dma(
                        "sync", AB.t.ap()[:, wg_off + hf * 8 * 512:wg_off + (hf + 1) * 8 * 512]
                        .rearrange("p (c n) -> p c n", n=512),
                        tg_d[hf * 1024:(hf + 1) * 1024, ob * 512:(ob + 1) * 512]
                        .rearrange("(c p) n -> p c n", p=128),
                        waits=[wgt_tok[0]] + (tern_done + p2_last if Q == 0 and ob == 0 else [])))
                    tok_u.append(p.dma(
                        "gpsimd", AB.t.ap()[:, wu_off + hf * 8 * 512:wu_off + (hf + 1) * 8 * 512]
                        .rearrange("p (c n) -> p c n", n=512),
                        tu_d[hf * 1024:(hf + 1) * 1024, ob * 512:(ob + 1) * 512]
                        .rearrange("(c p) n -> p c n", p=128),
                        waits=[wut_tok[0]] + (tern_done + p2_last if Q == 0 and ob == 0 else [])))
                for ti in range(4):
                    gi = (ti % 2) * 2
                    ui = gi + 1
                    tg_mm = None
                    tu_mm = None
                    for f in range(NF):
                        tg_mm = p.do("tensor", lambda e, f=f, ti=ti, gi=gi: e.matmul(
                            quad_ap(gi), AB.ap(qxq_off + f * 512 + ti * 128, 128),
                            AB.ap(wg_off + f * 512, 512),
                            start=(f == 0), stop=(f == NF - 1)),
                            waits=[tok_g[0] if f < 8 else tok_g[1], tok_qx,
                                   quad_tok[gi], gq_guard] if f in (0, 8) else (),
                            inc=(f == NF - 1))
                    for f in range(NF):
                        tu_mm = p.do("tensor", lambda e, f=f, ti=ti, ui=ui: e.matmul(
                            quad_ap(ui), AB.ap(qxq_off + f * 512 + ti * 128, 128),
                            AB.ap(wu_off + f * 512, 512),
                            start=(f == 0), stop=(f == NF - 1)),
                            waits=[tok_u[0] if f < 8 else tok_u[1],
                                   quad_tok[ui]] if f in (0, 8) else (),
                            inc=(f == NF - 1))
                    if ti == 3:
                        wgt_tok[0] = tg_mm
                        wut_tok[0] = tu_mm
                    tt = Q * 4 + ti
                    sg = AB.ap(sg_off + (ti % 2) * 512, 512)
                    tsg = p.do("scalar", lambda e, sg=sg, gi=gi, tt=tt: e.activation(
                        sg, quad_ap(gi), ACT.Silu, scale=st_ag.ap()[:, tt:tt + 1]),
                        waits=[tg_mm, ag_tok, sg_guard[ti % 2]], inc=True, drain=True)
                    quad_tok[gi] = tsg
                    th = p.do("vector", lambda e, sg=sg, ui=ui, ti=ti, ob=ob, tt=tt:
                              e.scalar_tensor_tensor(
                                  AB.ap(hq_off + ti * DI + ob * 512, 512),
                                  quad_ap(ui), st_au.ap()[:, tt:tt + 1], sg,
                                  ALU.mult, ALU.mult),
                              waits=[tu_mm, tsg] +
                              ([hq_guard] if (ob == 0 and ti == 0 and hq_guard) else []),
                              inc=True)
                    sg_guard[ti % 2] = th
                    quad_tok[ui] = th
                    h_toks.append(th)
            dbgh_tok = None
            if debug and Q == 0 and phases >= 4:
                dbgh_tok = p.dma("gpsimd", dbg3_d, AB.ap(hq_off, DI), waits=h_toks)
            qhf_done = []
            tq_free = None
            for ti in range(4):
                tt = Q * 4 + ti
                hten = AB.ap(hq_off + ti * DI, DI)
                ts_a = p.do("scalar", lambda e, hten=hten, tt=tt: e.activation(
                    sq_scr.ap(), hten[:, 0:4096], ACT.Square, accum_out=cap(h_ssq, tt)),
                    waits=h_toks, inc=True, drain=True)
                ts_b = p.do("scalar", lambda e, hten=hten, tt=tt: e.activation(
                    sq_scr.ap(), hten[:, 4096:8192], ACT.Square, accum_out=cap(h_ssq2, tt)),
                    inc=True, drain=True)
                tlnw = p.do("vector", lambda e, hten=hten: e.tensor_tensor(
                    hten, hten, lnwbc.ap(), ALU.mult),
                    waits=[ts_b, lnw_done, dbgh_tok], inc=True)
                if debug and Q == 0 and ti == 0 and phases >= 4:
                    dbgl_tok = p.dma("gpsimd", dbg3_d, hten, waits=[tlnw])
                    p.dma("gpsimd", dbg5_d[:, 96:97], cap(h_ssq, tt),
                          waits=[ts_b], slow=True)
                    p.dma("gpsimd", dbg5_d[:, 97:98], cap(h_ssq2, tt),
                          waits=[ts_b], slow=True)
                if debug and Q == 0 and phases >= 4:
                    p.dma("gpsimd", dbg5_d[:, 100 + ti:101 + ti], cap(h_ssq, 0),
                          waits=[tsr], slow=True)
                p.do("vector", lambda e, tt=tt: e.tensor_add(
                    cap(h_ssq, tt), cap(h_ssq, tt), cap(h_ssq2, tt)), inc=True)
                tvv = p.do("vector", lambda e, tt=tt: e.tensor_scalar(
                    cap(h_t2, tt), cap(h_ssq, tt), 1.0 / 8192.0, 1e-6, ALU.mult, ALU.add),
                    inc=True)
                tsr = p.do("scalar", lambda e, tt=tt: (
                    e.activation(cap(h_t1, tt), cap(h_t2, tt), ACT.Ln), e.drain(),
                    e.activation(cap(h_r, tt), cap(h_t1, tt), ACT.Exp, scale=-0.5))[-1],
                    waits=[tvv], inc=True, drain=True)
                p.do("vector", lambda e, tt=tt: e.tensor_scalar(
                    cap(h_t1, tt), cap(h_t2, tt), cap(h_r, tt), cap(h_r, tt), ALU.mult, ALU.mult),
                    waits=[tsr], inc=True)
                p.do("vector", lambda e, tt=tt: e.tensor_scalar(
                    cap(h_t1, tt), cap(h_t1, tt), -0.5, 1.5, ALU.mult, ALU.add), inc=True)
                p.do("vector", lambda e, tt=tt: e.tensor_scalar_mul(
                    cap(h_r, tt), cap(h_r, tt), cap(h_t1, tt)), inc=True)
                p.do("vector", lambda e, hten=hten, tt=tt: e.reduce_max(
                    cap(h_t1, tt), hten, axis=AX.X, apply_absolute_value=True),
                    waits=[tlnw], inc=True)
                p.do("vector", lambda e, tt=tt: e.tensor_scalar(
                    cap(h_t1, tt), cap(h_t1, tt), cap(h_r, tt), 1e-5, ALU.mult, ALU.max), inc=True)
                p.do("vector", lambda e, tt=tt: e.tensor_single_scalar(
                    cap(h_ax, tt), cap(h_t1, tt), INV127, ALU.mult), inc=True)
                p.do("vector", lambda e, tt=tt: e.reciprocal(
                    cap(h_t2, tt), cap(h_t1, tt)), inc=True)
                p.do("vector", lambda e, tt=tt: e.tensor_single_scalar(
                    cap(h_t2, tt), cap(h_t2, tt), 127.0, ALU.mult), inc=True)
                p.do("vector", lambda e, tt=tt: e.tensor_scalar_mul(
                    cap(h_sp, tt), cap(h_r, tt), cap(h_t2, tt)), inc=True)
                tad = p.do("vector", lambda e, tt=tt: e.tensor_scalar_mul(
                    cap(h_ad, tt), cap(h_ax, tt), wsc.ap()[:, 4:5]), inc=True, drain=True)
                if debug and Q == 0 and ti == 0 and phases >= 4:
                    for c0, st_ in [(70, h_ax), (71, h_sp), (72, h_ad), (73, h_r), (74, h_t2)]:
                        p.dma("gpsimd", dbg5_d[:, c0:c0 + 1], cap(st_, tt),
                              waits=[tad], slow=True)
                for half in range(2):
                    hsl = slice(half * 4096, (half + 1) * 4096)
                    qht = AB.ap(tq_off, 4096)
                    p.do("vector", lambda e, hten=hten, hsl=hsl, tt=tt: e.tensor_scalar(
                        AF.ap(y1h_off, 4096), hten[:, hsl], cap(h_sp, tt), C32,
                        ALU.mult, ALU.add), waits=[tq_free], inc=True)
                    p.do("vector", lambda e, qht=qht: e.tensor_single_scalar(
                        qht, AF.ap(y1h_off, 4096), -C32, ALU.add), inc=True)
                    p.do("vector", lambda e, qht=qht: e.tensor_scalar(
                        AB.ap(t2s_off, 4096), qht, C52, -C52, ALU.add, ALU.add), inc=True)
                    tqq = p.do("vector", lambda e, qht=qht: e.scalar_tensor_tensor(
                        qht, qht, omd.ap(), AB.ap(t2s_off, 4096), ALU.mult, ALU.add),
                        inc=True)
                    if debug and Q == 0 and ti == 0 and half == 0 and phases >= 4:
                        tqdump = p.dma("gpsimd", dbg2_d[:, 0:4096], qht, waits=[tqq])
                    evs = []
                    for f in range(32):
                        fg = half * 32 + f
                        trp = trp_of[f % 2]
                        tm2 = p.do("tensor", lambda e, qht=qht, f=f, trp=trp: e.transpose(
                            trp.ap()[:, 0:128], qht[:, f * 128:(f + 1) * 128],
                            ident_sb.ap()), waits=[tqq, tr_tok[f % 2]], inc=True)
                        tv_e = p.do("vector", lambda e, fg=fg, ti=ti, trp=trp: e.tensor_copy(
                            AB.ap(hq_off + ti * DI + fg * 128, 128), trp.ap()[:, 0:128]),
                            waits=[tm2], inc=True)
                        tr_tok[f % 2] = tv_e
                        evs.append(tv_e)
                    tq_free = evs[-1]
                    if debug and Q == 0 and ti == 0 and half == 0 and phases >= 4:
                        p.dma("gpsimd", dbg3_d[:, 0:4096], AB.ap(hq_off, 4096),
                              waits=[tq_free])
                if debug and Q == 0 and phases >= 4:
                    p.dma("gpsimd", dbg5_d[:, 104 + ti:105 + ti], cap(h_ssq, 0),
                          waits=[tq_free], slow=True)
                if debug and Q == 0 and ti == 3 and phases >= 4:
                    p.dma("gpsimd", dbg2_d[:, 4096:8192], AB.ap(hq_off, 4096),
                          waits=[tq_free])
                qhf_done.append(p.do("vector", lambda e: e.engine_nop(),
                                     waits=[tq_free], inc=True))
            td_tok = [None, None]
            ev_d = []
            for ob in range(4):
                dmm_last = [None] * 4
                for sl in range(4):
                    buf = wg_off if sl % 2 == 0 else wu_off
                    tok_td = p.dma(("sync", "gpsimd")[sl % 2],
                                   AB.t.ap()[:, buf:buf + NF * 512]
                                   .rearrange("p (c n) -> p c n", n=512),
                                   td_d[sl * 2048:(sl + 1) * 2048, ob * 512:(ob + 1) * 512]
                                   .rearrange("(c p) n -> p c n", p=128),
                                   waits=[td_tok[sl % 2]] +
                                   ([wgt_tok[0], wut_tok[0]] if ob == 0 and sl < 2 else []))
                    for ti in range(4):
                        for f in range(NF):
                            fg = sl * NF + f
                            dmm_last[ti] = p.do(
                                "tensor", lambda e, f=f, ti=ti, fg=fg, buf=buf: e.matmul(
                                    quad_ap(ti), AB.ap(hq_off + ti * DI + fg * 128, 128),
                                    AB.ap(buf + f * 512, 512),
                                    start=(fg == 0), stop=(fg == NI - 1)),
                                waits=[tok_td, quad_tok[ti]] + qhf_done if f == 0 else (),
                                inc=(f == NF - 1))
                    td_tok[sl % 2] = dmm_last[3]
                if debug and Q == 0 and phases >= 4:
                    p.dma("gpsimd", dbg5_d[:, 108 + ob:109 + ob], cap(h_ssq, 0),
                          waits=[td_tok[1] or td_tok[0]], slow=True)
                for ti in range(4):
                    tt = Q * 4 + ti
                    tsd = p.do("scalar", lambda e, ti=ti, ob=ob, tt=tt: e.activation(
                        AB.ap(qxq_off + ti * 2048 + ob * 512, 512), quad_ap(ti),
                        ACT.Silu, scale=h_ad.ap()[:, tt:tt + 1]),
                        waits=[dmm_last[ti], gq_guard], inc=True, drain=True)
                    quad_tok[ti] = tsd
                    ev_d.append(tsd)
                    if debug and Q == 0 and phases >= 4:
                        p.dma("gpsimd", dbg5_d[:, 112 + 4 * ob + ti:113 + 4 * ob + ti],
                              cap(h_ssq, 0), waits=[tsd], slow=True)
            if debug and Q == 0 and phases >= 4:
                p.dma("gpsimd", dbg3_d[:, 0:2048], AB.ap(qxq_off, 2048), waits=ev_d)
            hq_guard = td_tok[1]
            qx_guard = ev_d[-1]
            for ti in range(4):
                tt = Q * 4 + ti
                xf = AF.ap(xf_off, 2048)
                saf = AB.ap(saf_off, 2048)
                ou = AF.ap(ou_off, 2048)
                tok_x2 = p.dma("sync", xf, x_d[tt * 128:(tt + 1) * 128, :],
                               waits=[xf_guard])
                tok_sa = p.dma("sync", saf, sA_d[tt * 128:(tt + 1) * 128, :],
                               waits=[saf_guard] + p2_last)
                saf_guard = p.do("vector", lambda e, saf=saf, ti=ti: e.tensor_tensor(
                    saf, saf, AB.ap(qxq_off + ti * 2048, 2048), ALU.mult),
                    waits=[tok_sa] + ev_d, inc=True)
                tv_o = p.do("vector", lambda e, saf=saf, xf=xf, ou=ou: e.tensor_tensor(
                    ou, saf, xf, ALU.add), waits=[tok_x2, out_guard], inc=True)
                xf_guard = tv_o
                out_guard = p.dma("gpsimd", out_d[tt * 128:(tt + 1) * 128, :], ou,
                                  waits=[tv_o])
                final_toks.append(out_guard)
            gq_guard = saf_guard
            qx_guard = saf_guard
        p.do("gpsimd", lambda e: e.engine_nop(), waits=[final_toks[-1]])

    from contextlib import ExitStack
    with ExitStack() as stack:
        sems = {nm: stack.enter_context(nc.semaphore(nm)) for nm in ALL_SEMS}
        block = stack.enter_context(nc.Block())
        p.emit(block, sems)
    return nc


def _prep_shared(inputs):
    wqkv = inputs["L0_attn_wqkv"]
    perm = np.arange(DQK)
    for h in range(NH + NKV):
        base = h * HD
        perm[base:base + 64] = base + 2 * np.arange(64)
        perm[base + 64:base + 128] = base + 2 * np.arange(64) + 1
    return {
        "wqkvT": np.ascontiguousarray(wqkv[perm].T),
        "woT": np.ascontiguousarray(inputs["L0_attn_wo"].T),
        "wgT": np.ascontiguousarray(inputs["L0_ffn_w_gate"].T),
        "wuT": np.ascontiguousarray(inputs["L0_ffn_w_up"].T),
        "wdT": np.ascontiguousarray(inputs["L0_ffn_w_down"].T),
        "normw": np.ascontiguousarray(inputs["L0_attn_norm_w"].reshape(1, D)),
        "lnw": np.ascontiguousarray(inputs["L0_ffn_ln_weight"].reshape(1, DI)),
    }


def kernel(**inputs):
    x = np.asarray(inputs["L0_attn_x"], dtype=np.float32)
    shared = _prep_shared({k: np.asarray(v, dtype=np.float32)
                           for k, v in inputs.items() if "cache" not in k and k != "L0_attn_x"})
    nc = build_program(debug=False, phases=5)
    in_maps = []
    for c in range(NCORES):
        m = {"x": np.ascontiguousarray(x[c])}
        m.update(shared)
        in_maps.append(m)
    res = run_bass_kernel_spmd(nc, in_maps, core_ids=list(range(NCORES)))
    return np.stack([np.asarray(r["out"], dtype=np.float32) for r in res.results])


# version 11
# speedup vs baseline: 1.4282x; 1.1464x over previous
import numpy as np
import ml_dtypes
import concourse.bass as bass
import concourse.mybir as mybir
from concourse.bass_utils import run_bass_kernel_spmd

F32 = mybir.dt.float32
BF16 = mybir.dt.bfloat16
ALU = mybir.AluOpType
ACT = mybir.ActivationFunctionType
AX = mybir.AxisListType

NCORES = 8
S = 2048
D = 2048
DQK = 3072
DI = 8192
HD = 128
NH = 16
NKV = 4
NT = S // 128
NF = D // 128
NI = DI // 128

C32 = float(1.5 * 2 ** 23)
C52 = float(1.5 * 2 ** 52)
TWO30 = float(2 ** 30)
ATTN_SCALE = float(np.float32(1.0) / np.sqrt(np.float64(HD)).astype(np.float32))
INV127 = float(np.float32(1.0) / np.float32(127.0))

ENGS = ("sync", "gpsimd", "tensor", "vector", "scalar")
SEM_OF = {"sync": "sd", "gpsimd": "sg", "tensor": "st", "vector": "sv", "scalar": "ss"}
DMA_SEMS = {"sync": ["da0", "da1", "da2", "da3"],
            "gpsimd": ["ga0", "ga1", "ga2", "ga3"],
            "scalar": ["ca0", "ca1"]}
ALL_SEMS = (["st", "sv", "ss", "sd", "sg"] + DMA_SEMS["sync"]
            + DMA_SEMS["gpsimd"] + DMA_SEMS["scalar"])


class Arena:
    def __init__(self, nc, name, dtype, cols):
        self.t = nc.alloc_sbuf_tensor(name, [128, cols], dtype)
        self.free = [(0, cols)]
        self.name = name

    def take(self, n, align=16):
        for i, (off, sz) in enumerate(self.free):
            a = (off + align - 1) // align * align
            if a + n <= off + sz:
                repl = [p for p in ((off, a - off), (a + n, off + sz - a - n)) if p[1] > 0]
                self.free[i:i + 1] = repl
                return a
        raise MemoryError(f"arena {self.name}: need {n}, free={self.free}")

    def give(self, off, n):
        self.free.append((off, n))
        self.free.sort()
        m = []
        for o, s_ in self.free:
            if m and m[-1][0] + m[-1][1] == o:
                m[-1] = (m[-1][0], m[-1][1] + s_)
            else:
                m.append((o, s_))
        self.free = m

    def ap(self, off, n):
        return self.t.ap()[:, off:off + n]


class Plan:
    def __init__(self, nc):
        self.nc = nc
        self.ops = {e: [] for e in ENGS}
        self.cnt = {s_: 0 for s_ in ALL_SEMS}
        self.rr = {"sync": 0, "gpsimd": 0, "scalar": 0}
        self.maxw = {e: {} for e in ENGS}

    def _waits(self, eng, waits):
        out = []
        own = SEM_OF[eng]
        for t in waits:
            if t is None:
                continue
            sem, val = t
            if sem == own or val <= self.maxw[eng].get(sem, 0):
                continue
            self.maxw[eng][sem] = val
            out.append((sem, val))
        return out

    def do(self, eng, fn, waits=(), inc=False, drain=False):
        w = self._waits(eng, waits)
        tok = None
        auto = eng in ("vector", "scalar")
        if inc or auto:
            sem = SEM_OF[eng]
            self.cnt[sem] += 1
            tok = (sem, self.cnt[sem])
        self.ops[eng].append((fn, w, tok, drain, 1))
        return tok

    def barrier(self):
        return [(sem, cnt) for sem, cnt in self.cnt.items() if cnt > 0]

    def dma(self, q, dst, src, waits=(), slow=False):
        sems = DMA_SEMS[q]
        sem = sems[self.rr[q]]
        self.rr[q] = (self.rr[q] + 1) % len(sems)
        w = self._waits(q, list(waits) + [(sem, self.cnt[sem])])
        self.cnt[sem] += 16
        tok = (sem, self.cnt[sem])
        self.ops[q].append((lambda e, d=dst, s_=src, sl=slow: e.dma_start(
            d, s_, allow_slow_non_contiguous=sl) if sl else e.dma_start(d, s_),
            w, tok, False, 16))
        return tok

    def emit(self, block, sems):
        def runner(eng_name):
            chain = eng_name in ("vector", "scalar")
            own = SEM_OF.get(eng_name)

            def body(e):
                prev = 0
                for (fn, waits, tok, drain, incn) in self.ops[eng_name]:
                    for (sem, val) in waits:
                        e.wait_ge(sems[sem], val)
                    if chain and prev > 0:
                        e.wait_ge(sems[own], prev)
                    inst = fn(e)
                    if tok is not None:
                        if drain:
                            e.drain()
                            e.nop().then_inc(sems[tok[0]], incn)
                        else:
                            inst.then_inc(sems[tok[0]], incn)
                        if chain and incn == 1:
                            prev = tok[1]
                    elif drain:
                        e.drain()
            return body

        block.sync(runner("sync"))
        block.gpsimd(runner("gpsimd"))
        block.tensor(runner("tensor"))
        block.vector(runner("vector"))
        block.scalar(runner("scalar"))


def cap(t, i):
    return t.ap()[:, i:i + 1]


def _consts():
    tvec = np.arange(S, dtype=np.float32).reshape(1, S)
    ivec = (-(np.arange(0, HD, 2, dtype=np.float64) / HD)).astype(np.float32).reshape(1, 64)
    ii = np.arange(128)[:, None]
    jj = np.arange(1024)[None, :]
    maskb = (jj >= ii + 512).astype(ml_dtypes.bfloat16)
    ident = np.eye(128, dtype=ml_dtypes.bfloat16)
    return tvec, ivec, maskb, ident


def build_program(debug=False, phases=5, p2_qbs=4, p3_qs=4):
    nc = bass.Bass("TRN2", target_bir_lowering=False)

    x_d = nc.dram_tensor("x", [S, D], F32, kind="ExternalInput").ap()
    wqkvT_d = nc.dram_tensor("wqkvT", [D, DQK], F32, kind="ExternalInput").ap()
    woT_d = nc.dram_tensor("woT", [D, D], F32, kind="ExternalInput").ap()
    wgT_d = nc.dram_tensor("wgT", [D, DI], F32, kind="ExternalInput").ap()
    wuT_d = nc.dram_tensor("wuT", [D, DI], F32, kind="ExternalInput").ap()
    wdT_d = nc.dram_tensor("wdT", [DI, D], F32, kind="ExternalInput").ap()
    normw_d = nc.dram_tensor("normw", [1, D], F32, kind="ExternalInput").ap()
    lnw_d = nc.dram_tensor("lnw", [1, DI], F32, kind="ExternalInput").ap()

    tvec_np, ivec_np, maskb_np, ident_np = _consts()
    tvec_d = nc.inline_tensor(tvec_np, "tvecc").ap()
    ivec_d = nc.inline_tensor(ivec_np, "ivecc").ap()
    maskb_d = nc.inline_tensor(maskb_np, "maskBc").ap()
    ident_d = nc.inline_tensor(ident_np, "identc").ap()

    out_d = nc.dram_tensor("out", [S, D], F32, kind="ExternalOutput").ap()
    sk = "ExternalOutput" if debug else "Internal"
    qxf_d = nc.dram_tensor("qxf", [D, S], BF16, kind=sk).ap()
    qf_d = nc.dram_tensor("qf", [D, S], BF16, kind=sk).ap()
    kf_d = nc.dram_tensor("kf", [NKV * HD, S], BF16, kind=sk).ap()
    sA_d = nc.dram_tensor("sA", [S, D], BF16, kind=sk).ap()
    tg_d = nc.dram_tensor("tg", [D, DI], BF16, kind=sk).ap()
    tu_d = nc.dram_tensor("tu", [D, DI], BF16, kind=sk).ap()
    td_d = nc.dram_tensor("td", [DI, D], BF16, kind=sk).ap()
    dbg_d = nc.dram_tensor("dbgv", [128, 256], F32, kind=sk).ap() if debug else None
    dbg2_d = nc.dram_tensor("dbg2", [128, 8192], BF16, kind=sk).ap() if debug else None
    dbg3_d = nc.dram_tensor("dbg3", [128, 8192], BF16, kind=sk).ap() if debug else None
    dbg4_d = nc.dram_tensor("dbg4", [128, 8192], BF16, kind=sk).ap() if debug else None
    dbg5_d = nc.dram_tensor("dbg5", [128, 128], F32, kind=sk).ap() if debug else None

    ident_sb = nc.alloc_sbuf_tensor("identb", [128, 128], BF16)
    mask_sb = nc.alloc_sbuf_tensor("maskb", [128, 1024], BF16)
    ropeC_sb = nc.alloc_sbuf_tensor("ropec", [128, NT * 64], BF16)
    ropeS_sb = nc.alloc_sbuf_tensor("ropes", [128, NT * 64], BF16)
    normbc = nc.alloc_sbuf_tensor("normbc", [128, D], BF16)
    s128bc = nc.alloc_sbuf_tensor("s128bc", [128, 1], F32)
    lnwbc = nc.alloc_sbuf_tensor("lnwbc", [128, DI], BF16)
    ones1f = nc.alloc_sbuf_tensor("ones1f", [1, 128], F32)
    onescol = nc.alloc_sbuf_tensor("onescol", [128, 1], F32)
    sq_scr = nc.alloc_sbuf_tensor("sqscr", [128, 4096], BF16)

    def stats(name):
        return nc.alloc_sbuf_tensor(name, [128, NT], F32)

    st_sum = stats("c_sum"); st_ssq = stats("c_ssq"); st_max = stats("c_max")
    st_min = stats("c_min"); st_m = stats("c_m"); st_t1 = stats("c_t1")
    st_t2 = stats("c_t2"); st_r = stats("c_r"); st_ax = stats("c_ax")
    st_sp = stats("c_sp"); st_bp = stats("c_bp")
    st_ag = stats("c_ag"); st_au = stats("c_au")
    o_ssq = stats("o_ssq"); o_r = stats("o_r"); o_ax = stats("o_ax")
    o_sp = stats("o_sp"); o_t1 = stats("o_t1"); o_t2 = stats("o_t2")
    o_inv = stats("o_inv")
    h_ssq = stats("h_ssq"); h_ssq2 = stats("h_ssq2"); h_r = stats("h_r")
    h_ax = stats("h_ax"); h_sp = stats("h_sp"); h_t1 = stats("h_t1")
    h_t2 = stats("h_t2"); h_ad = stats("h_ad")
    dvec = nc.alloc_sbuf_tensor("dvec", [128, 1], F32)
    omd = nc.alloc_sbuf_tensor("omd", [128, 1], F32)
    wsc = nc.alloc_sbuf_tensor("wsc", [128, 8], F32)
    wacc = nc.alloc_sbuf_tensor("wacc", [128, 64], F32)
    wtot = nc.alloc_sbuf_tensor("wtot", [128, 1], F32)

    AB = Arena(nc, "ab", BF16, 69632)
    AF = Arena(nc, "af", F32, 8704)

    quad = nc.alloc_psum_tensor("quad", [128, 2048], F32)
    pv_a = nc.alloc_psum_tensor("pva", [128, 512], F32)
    pv_b = nc.alloc_psum_tensor("pvb", [128, 512], F32)
    tr_a = nc.alloc_psum_tensor("tra", [128, 512], BF16)
    tr_b = nc.alloc_psum_tensor("trb", [128, 512], BF16)
    trp_of = (tr_a, tr_b)

    p = Plan(nc)

    def quad_ap(i, n=512):
        return quad.ap()[:, i * 512:i * 512 + n]

    quad_tok = [None, None, None, None]
    tr_tok = [None, None]
    pv_tok = [None, None]

    g_toks = [
        p.dma("sync", ident_sb.ap(), ident_d),
        p.dma("sync", mask_sb.ap(), maskb_d),
        p.dma("gpsimd", normbc.ap(), bass.AP(normw_d.tensor, 0, [[0, 128], [1, D]])),
        p.dma("gpsimd", lnwbc.ap(), bass.AP(lnw_d.tensor, 0, [[0, 128], [1, DI]])),
    ]
    nb_toks = [g_toks[2]]
    lnw_done = g_toks[3]
    p.do("vector", lambda e: e.memset(ones1f.ap(), 1.0), inc=True, drain=True)
    p.do("vector", lambda e: e.memset(onescol.ap(), 1.0), inc=True, drain=True)
    p.do("vector", lambda e: e.tensor_scalar(dvec.ap(), onescol.ap(), TWO30, -TWO30,
                                             ALU.add, ALU.add), inc=True, drain=True)
    t_ones = p.do("vector", lambda e: e.engine_nop(), inc=True)
    t_omd = p.do("vector", lambda e: e.tensor_scalar(omd.ap(), dvec.ap(), -1.0, 1.0,
                                                     ALU.mult, ALU.add), inc=True, drain=True)
    p.do("vector", lambda e: e.memset(wsc.ap(), 0.0), inc=True)
    p.do("vector", lambda e: e.memset(wacc.ap(), 0.0), inc=True)

    import math as _math
    _tpi = 2 * _math.pi
    _c1 = float(np.float32(_tpi))
    _c2 = float(np.float32(_tpi - _c1))
    _c3 = float(np.float32(_tpi - _c1 - _c2))
    _in2pi = float(np.float32(1.0 / _tpi))
    _d1 = float(np.float32(_math.pi / 2))
    _d2 = float(np.float32(_math.pi / 2 - np.float64(np.float32(_math.pi / 2))))
    rb_off = AF.take(3 * 1024 + 2048 + 128)
    trow = AF.ap(rb_off + 3 * 1024, 2048)[0:1, :]
    irow = AF.ap(rb_off + 3 * 1024 + 2048, 64)[0:1, :]
    lnth = AF.ap(rb_off + 3 * 1024 + 2048 + 64, 1)[0:1, :]
    frow = AF.ap(rb_off + 3 * 1024 + 2048 + 65, 63 + 1)[0:1, 0:64]
    angb = AF.ap(rb_off, 1024)
    ktb = AF.ap(rb_off + 1024, 1024)
    k2b = AF.ap(rb_off + 2048, 1024)
    tok_tv = p.dma("sync", trow, tvec_d)
    tok_iv = p.dma("sync", irow, ivec_d)
    tv_th = p.do("vector", lambda e: e.memset(lnth, 500000.0), inc=True, drain=True)
    ts_fr = p.do("scalar", lambda e: (
        e.activation(lnth, lnth, ACT.Ln), e.drain(),
        e.activation(frow, irow, ACT.Exp, scale=lnth))[-1],
        waits=[tok_tv, tok_iv, tv_th], inc=True, drain=True)
    ang_evs = []
    for ti in range(NT):
        tmr = p.do("tensor", lambda e, ti=ti: e.matmul(
            quad.ap()[:, 64:128], trow[0:1, ti * 128:(ti + 1) * 128], frow,
            start=True, stop=True), waits=[ts_fr, quad_tok[1]] + (
                [ang_evs[-1]] if ang_evs else []), inc=True)
        tve = p.do("vector", lambda e, ti=ti: e.tensor_copy(
            angb[:, ti * 64:(ti + 1) * 64], quad.ap()[:, 64:128]),
            waits=[tmr], inc=True, drain=True)
        ang_evs.append(tve)
    quad_tok[1] = ang_evs[-1]
    rope_done = None
    for half, dst in ((0, ropeS_sb), (1, ropeC_sb)):
        if half == 1:
            p.do("vector", lambda e: e.tensor_scalar(
                angb, angb, _d1, _d2, ALU.add, ALU.add),
                waits=[ang_evs[-1], rope_done], inc=True, drain=True)
        p.do("vector", lambda e: e.tensor_scalar(
            ktb, angb, _in2pi, C32, ALU.mult, ALU.add),
            waits=[t_omd, ang_evs[-1]], inc=True, drain=True)
        p.do("vector", lambda e: e.tensor_single_scalar(
            ktb, ktb, -C32, ALU.add), inc=True, drain=True)
        p.do("vector", lambda e: e.tensor_scalar(
            k2b, ktb, C52, -C52, ALU.add, ALU.add), inc=True, drain=True)
        p.do("vector", lambda e: e.scalar_tensor_tensor(
            ktb, ktb, omd.ap(), k2b, ALU.mult, ALU.add), inc=True, drain=True)
        p.do("vector", lambda e: e.scalar_tensor_tensor(
            k2b, ktb, -_c1, angb, ALU.mult, ALU.add), inc=True, drain=True)
        p.do("vector", lambda e: e.scalar_tensor_tensor(
            k2b, ktb, -_c2, k2b, ALU.mult, ALU.add), inc=True, drain=True)
        p.do("vector", lambda e: e.scalar_tensor_tensor(
            k2b, ktb, -_c3, k2b, ALU.mult, ALU.add), inc=True, drain=True)
        tred = p.do("vector", lambda e: e.tensor_scalar(
            k2b, k2b, 3.141592502593994, -3.141592502593994, ALU.min, ALU.max), inc=True, drain=True)
        rope_done = p.do("scalar", lambda e, dst=dst: e.activation(
            dst.ap(), k2b, ACT.Sin), waits=[tred], inc=True, drain=True)
    tv_s1 = p.do("vector", lambda e: e.memset(lnth, 128.0),
                 waits=[rope_done], inc=True, drain=True)
    ts_s1 = p.do("scalar", lambda e: (
        e.activation(lnth, lnth, ACT.Ln), e.drain(),
        e.activation(lnth, lnth, ACT.Exp, scale=-0.5))[-1],
        waits=[tv_s1], inc=True, drain=True)
    tms = p.do("tensor", lambda e: e.matmul(
        quad.ap()[:, 4:5], ones1f.ap(), lnth, start=True, stop=True),
        waits=[ts_s1, t_ones, quad_tok[0]], inc=True)
    t_s128 = p.do("vector", lambda e: e.tensor_copy(
        s128bc.ap(), quad.ap()[:, 4:5]), waits=[tms], inc=True, drain=True)
    quad_tok[0] = t_s128
    AF.give(rb_off, 3 * 1024 + 2048 + 128)

    WP = 2048

    def phase_w_gen(wT, tern_d, nrows_chunks, wide, col_mc, col_s, done_sink):
        npieces_per_chunk = wide // WP
        npieces = nrows_chunks * npieces_per_chunk
        bar_w1 = p.barrier()
        wb_off = AF.take(2 * WP)
        red_tok = [None, None]
        wsrc = wT.rearrange("(c p) n -> c p n", p=128)
        for i in range(npieces):
            ch, pc = divmod(i, npieces_per_chunk)
            tok_d = p.dma("scalar", AF.ap(wb_off + (i % 2) * WP, WP),
                          wsrc[ch][:, pc * WP:(pc + 1) * WP],
                          waits=[red_tok[i % 2]] + (bar_w1 if i < 2 else []))
            red_tok[i % 2] = p.do("vector", lambda e, i=i: e.reduce_sum(
                wacc.ap()[:, i:i + 1], AF.ap(wb_off + (i % 2) * WP, WP),
                axis=AX.X, apply_absolute_value=True), waits=[tok_d], inc=True,
                drain=False)
            yield
        tvt = p.do("vector", lambda e, n=npieces: e.reduce_sum(
            wtot.ap(), wacc.ap()[:, 0:n], axis=AX.X), waits=red_tok, inc=True)
        tm = p.do("tensor", lambda e: e.matmul(
            quad.ap()[0:1, 0:1], wtot.ap(), onescol.ap(), start=True, stop=True),
            waits=[tvt, quad_tok[0]], inc=True)
        inv_nel = 1.0 / (128.0 * nrows_chunks * wide)
        tv1 = p.do("vector", lambda e, iv=inv_nel: e.tensor_scalar(
            wsc.ap()[0:1, 6:7], quad.ap()[0:1, 0:1], iv, 1e-5, ALU.mult, ALU.max),
            waits=[tm], inc=True)
        tm2 = p.do("tensor", lambda e: e.matmul(
            quad.ap()[:, 4:5], ones1f.ap(), wsc.ap()[0:1, 6:7], start=True, stop=True),
            waits=[tv1], inc=True)
        p.do("vector", lambda e, c=col_mc: e.tensor_copy(
            wsc.ap()[:, c:c + 1], quad.ap()[:, 4:5]), waits=[tm2], inc=True)
        tv3 = p.do("vector", lambda e, c=col_mc, c2=col_s: e.reciprocal(
            wsc.ap()[:, c2:c2 + 1], wsc.ap()[:, c:c + 1]), inc=True)
        quad_tok[0] = tv3
        AF.give(wb_off, 2 * WP)
        yield

        bar_w2 = p.barrier()
        wb2 = AF.take(2 * WP)
        t1o = AB.take(WP)
        t2o = AB.take(WP)
        too = AB.take(2 * WP)
        tdst = tern_d.rearrange("(c p) n -> c p n", p=128)
        use_tok = [None, None]
        out_tok = [None, None]
        for i in range(npieces):
            ch, pc = divmod(i, npieces_per_chunk)
            tok_d = p.dma("scalar", AF.ap(wb2 + (i % 2) * WP, WP),
                          wsrc[ch][:, pc * WP:(pc + 1) * WP],
                          waits=[use_tok[i % 2]] + ((bar_w2 + [tv3]) if i < 2 else []))
            sap = wsc.ap()[:, col_s:col_s + 1]
            p.do("vector", lambda e, i=i, sap=sap: e.tensor_scalar(
                AB.ap(t1o, WP), AF.ap(wb2 + (i % 2) * WP, WP), sap, 0.5,
                ALU.mult, ALU.is_ge),
                waits=[tok_d, tv3, out_tok[i % 2]], inc=True)
            p.do("vector", lambda e, i=i, sap=sap: e.tensor_scalar(
                AB.ap(t2o, WP), AF.ap(wb2 + (i % 2) * WP, WP), sap, -0.5,
                ALU.mult, ALU.is_gt), inc=True)
            use_tok[i % 2] = tvq = p.do("vector", lambda e, i=i: e.scalar_tensor_tensor(
                AB.ap(too + (i % 2) * WP, WP), AB.ap(t1o, WP), -1.0, AB.ap(t2o, WP),
                ALU.add, ALU.add), inc=True)
            out_tok[i % 2] = p.dma("gpsimd", tdst[ch][:, pc * WP:(pc + 1) * WP],
                                   AB.ap(too + (i % 2) * WP, WP), waits=[tvq])
            yield
        AF.give(wb2, 2 * WP)
        AB.give(t1o, WP); AB.give(t2o, WP); AB.give(too, 2 * WP)
        done_sink.extend(t for t in out_tok if t)
        yield

    tern_done = []

    def _w_gens():
        yield from phase_w_gen(wgT_d, tg_d, NF, DI, 0, 1, tern_done)
        yield from phase_w_gen(wuT_d, tu_d, NF, DI, 2, 3, tern_done)
        yield from phase_w_gen(wdT_d, td_d, NI, D, 4, 5, tern_done)

    w_gen = _w_gens() if phases >= 1 else iter(())

    def drive_w(n):
        for _ in range(n):
            try:
                next(w_gen)
            except StopIteration:
                return

    qxf_off = AB.take(NT * 2048)

    def qxf_ap(f, c0, n):
        return AB.ap(qxf_off + f * 2048 + c0, n)

    p0_last = []
    ag_tok = None
    qxf_done = [None] * NT
    if phases >= 1:
        bar_p0 = p.barrier()
        xt_off = AF.take(2 * 2048)
        y1_off = AF.take(2048)
        t1_off = AB.take(2048)
        t2_off = AB.take(2048)
        qxt_off = AB.take(2 * 2048)
        xt_busy = [None, None]
        qxt_tok = [None, None]
        last_ev = None
        for ti in range(NT):
            xt = AF.ap(xt_off + (ti % 2) * 2048, 2048)
            tok_x = p.dma("sync", xt, x_d[ti * 128:(ti + 1) * 128, :],
                          waits=[xt_busy[ti % 2]] + (bar_p0 if ti < 2 else []))
            p.do("vector", lambda e, xt=xt, ti=ti: e.reduce_sum(
                cap(st_sum, ti), xt, axis=AX.X), waits=[tok_x], inc=True)
            p.do("vector", lambda e, xt=xt, ti=ti: e.reduce_max(
                cap(st_max, ti), xt, axis=AX.X), inc=True)
            p.do("vector", lambda e, xt=xt, ti=ti: e.tensor_reduce(
                cap(st_min, ti), xt, axis=AX.X, op=ALU.min), inc=True)
            ts_sq = p.do("scalar", lambda e, xt=xt, ti=ti: e.activation(
                sq_scr.ap()[:, 0:2048], xt, ACT.Square, accum_out=cap(st_ssq, ti)),
                waits=[tok_x], inc=True, drain=True)
            p.do("vector", lambda e, ti=ti: e.tensor_single_scalar(
                cap(st_m, ti), cap(st_sum, ti), 1.0 / 2048.0, ALU.mult), waits=[ts_sq], inc=True)
            p.do("vector", lambda e, ti=ti: e.tensor_scalar(
                cap(st_t1, ti), cap(st_m, ti), cap(st_m, ti), -1e-5, ALU.mult, ALU.add), inc=True)
            tv_var = p.do("vector", lambda e, ti=ti: e.scalar_tensor_tensor(
                cap(st_t2, ti), cap(st_ssq, ti), 1.0 / 2048.0, cap(st_t1, ti),
                ALU.mult, ALU.subtract), inc=True)
            ts_r0 = p.do("scalar", lambda e, ti=ti: (
                e.activation(cap(st_t1, ti), cap(st_t2, ti), ACT.Ln), e.drain(),
                e.activation(cap(st_r, ti), cap(st_t1, ti), ACT.Exp, scale=-0.5))[-1],
                waits=[tv_var], inc=True, drain=True)
            p.do("vector", lambda e, ti=ti: e.tensor_scalar(
                cap(st_t1, ti), cap(st_t2, ti), cap(st_r, ti), cap(st_r, ti), ALU.mult, ALU.mult),
                waits=[ts_r0], inc=True)
            p.do("vector", lambda e, ti=ti: e.tensor_scalar(
                cap(st_t1, ti), cap(st_t1, ti), -0.5, 1.5, ALU.mult, ALU.add), inc=True)
            p.do("vector", lambda e, ti=ti: e.tensor_scalar_mul(
                cap(st_r, ti), cap(st_r, ti), cap(st_t1, ti)), inc=True)
            p.do("vector", lambda e, ti=ti: e.tensor_single_scalar(
                cap(st_t1, ti), cap(st_max, ti), cap(st_m, ti), ALU.subtract), inc=True)
            p.do("vector", lambda e, ti=ti: e.tensor_scalar(
                cap(st_t2, ti), cap(st_min, ti), -1.0, cap(st_m, ti), ALU.mult, ALU.add), inc=True)
            p.do("vector", lambda e, ti=ti: e.tensor_max(
                cap(st_t1, ti), cap(st_t1, ti), cap(st_t2, ti)), inc=True)
            p.do("vector", lambda e, ti=ti: e.tensor_scalar(
                cap(st_t1, ti), cap(st_t1, ti), cap(st_r, ti), 1e-5, ALU.mult, ALU.max), inc=True)
            p.do("vector", lambda e, ti=ti: e.tensor_single_scalar(
                cap(st_ax, ti), cap(st_t1, ti), INV127, ALU.mult), inc=True)
            p.do("vector", lambda e, ti=ti: e.reciprocal(
                cap(st_t2, ti), cap(st_t1, ti)), inc=True)
            p.do("vector", lambda e, ti=ti: e.tensor_single_scalar(
                cap(st_t2, ti), cap(st_t2, ti), 127.0, ALU.mult), inc=True)
            p.do("vector", lambda e, ti=ti: e.tensor_scalar_mul(
                cap(st_sp, ti), cap(st_r, ti), cap(st_t2, ti)), inc=True)
            p.do("vector", lambda e, ti=ti: e.tensor_scalar(
                cap(st_bp, ti), cap(st_m, ti), cap(st_sp, ti), -1.0, ALU.mult, ALU.mult), inc=True)
            y1 = AF.ap(y1_off, 2048)
            t1b = AB.ap(t1_off, 2048)
            t2b = AB.ap(t2_off, 2048)
            qxt = AB.ap(qxt_off + (ti % 2) * 2048, 2048)
            xt_busy[ti % 2] = p.do("vector", lambda e, xt=xt, y1=y1, ti=ti: e.tensor_scalar(
                y1, xt, cap(st_sp, ti), cap(st_bp, ti), ALU.mult, ALU.add),
                waits=[qxt_tok[ti % 2]], inc=True)
            p.do("vector", lambda e, y1=y1, t1b=t1b: e.tensor_scalar(
                t1b, y1, C32, -C32, ALU.add, ALU.add), inc=True)
            p.do("vector", lambda e, t1b=t1b, t2b=t2b: e.tensor_scalar(
                t2b, t1b, C52, -C52, ALU.add, ALU.add), inc=True)
            tv_q = p.do("vector", lambda e, t1b=t1b, t2b=t2b, qxt=qxt: e.scalar_tensor_tensor(
                qxt, t1b, omd.ap(), t2b, ALU.mult, ALU.add),
                waits=[t_omd], inc=True)
            for j in range(NF):
                trp = trp_of[j % 2]
                tm = p.do("tensor", lambda e, qxt=qxt, j=j, trp=trp: e.transpose(
                    trp.ap()[:, 0:128], qxt[:, j * 128:(j + 1) * 128], ident_sb.ap()),
                    waits=[tv_q, tr_tok[j % 2], g_toks[0]], inc=True)
                tv_e = p.do("vector", lambda e, j=j, ti=ti, trp=trp: e.tensor_copy(
                    qxf_ap(j, ti * 128, 128), trp.ap()[:, 0:128]), waits=[tm], inc=True)
                tr_tok[j % 2] = tv_e
                if j == NF - 1:
                    qxt_tok[ti % 2] = tv_e
                    qxf_done[ti] = tv_e
                    last_ev = tv_e
        tv_d = p.do("vector", lambda e: e.engine_nop(), waits=[last_ev], inc=True)
        qxf_store = p.dma("gpsimd", qxf_d.rearrange("(c p) n -> p c n", p=128),
                          AB.t.ap()[:, qxf_off:qxf_off + NT * 2048]
                          .rearrange("p (c n) -> p c n", n=2048), waits=[tv_d])
        p0_last = [qxf_store, tv_d]
        p.do("vector", lambda e: e.tensor_scalar_mul(
            st_ag.ap(), st_ax.ap(), wsc.ap()[:, 0:1]), waits=[tv_d], inc=True)
        ag_tok = p.do("vector", lambda e: e.tensor_scalar_mul(
            st_au.ap(), st_ax.ap(), wsc.ap()[:, 2:3]), inc=True, drain=True)
        if debug:
            for nm, src_t, c0 in [("sum", st_sum, 0), ("ssq", st_ssq, 16),
                                  ("max", st_max, 32), ("min", st_min, 48),
                                  ("m", st_m, 64), ("r", st_r, 80),
                                  ("ax", st_ax, 96), ("sp", st_sp, 112),
                                  ("bp", st_bp, 128), ("wsc", wsc, 144),
                                  ("wacc", wacc, 160)]:
                n = src_t.shape[1] if hasattr(src_t, 'shape') else NT
                p.dma("gpsimd", dbg_d[:, c0:c0 + src_t.ap().shape[-1]],
                      src_t.ap(), waits=[ag_tok] + tern_done, slow=True)
            p.dma("gpsimd", dbg_d[:, 240:241], wtot.ap(), waits=[ag_tok], slow=True)
            p.dma("gpsimd", dbg_d[:, 241:242], dvec.ap(), waits=[ag_tok], slow=True)
            p.dma("gpsimd", dbg_d[:, 242:243], omd.ap(), waits=[ag_tok], slow=True)

    p1_last = []
    vtok_off = None
    if phases >= 2:
        vtok_off = AB.take(NT * 516)
        wkb_off = AB.take(2 * NF * 512)
        qev_off = AF.take(2 * 512)
        ta_off = AF.take(256)
        tb_off = AF.take(256)
        rq_off = AB.take(2 * 512)
        assert t2_off == t1_off + 2048 and qxt_off == t2_off + 2048
        stage_off = t1_off
        vt_set = p.do("vector", lambda e: e.memset(AB.ap(vtok_off, NT * 516), 1.0),
                      inc=True)

        def vt_dst(ti):
            return AB.t.ap()[:, vtok_off + ti * 516:vtok_off + (ti + 1) * 516] \
                .rearrange("p (g n) -> p g n", n=129)[:, :, 0:128]

        def rope_bcast(tab, ti):
            return bass.AP(tab, ti * 64, [[NT * 64, 128], [0, 4], [1, 64]])

        OBLK = [("q", 0, 0), ("q", 1, 512), ("q", 2, 1024), ("q", 3, 1536),
                ("k", 0, 2048), ("v", 0, 2560)]
        st_last = [None, None]
        stage_tok = None
        for m, (sec, bi, c0) in enumerate(OBLK):
            tok_w = p.dma("gpsimd",
                          AB.t.ap()[:, wkb_off + (m % 2) * NF * 512:
                                    wkb_off + (m % 2 + 1) * NF * 512]
                          .rearrange("p (c n) -> p c n", n=512),
                          wqkvT_d[:, c0:c0 + 512].rearrange("(c p) n -> p c n", p=128),
                          waits=[st_last[m % 2]])
            for ti in range(NT):
                qi = ti % 4
                tm = None
                for f in range(NF):
                    tm = p.do("tensor", lambda e, f=f, ti=ti, m=m, qi=qi: e.matmul(
                        quad_ap(qi), qxf_ap(f, ti * 128, 128),
                        AB.ap(wkb_off + (m % 2) * NF * 512 + f * 512, 512),
                        start=(f == 0), stop=(f == NF - 1)),
                        waits=[tok_w, quad_tok[qi], qxf_done[ti]] if f == 0 else (),
                        inc=(f == NF - 1))
                st_last[m % 2] = tm
                axc = st_ax.ap()[:, ti:ti + 1]
                if sec == "v":
                    tv = p.do("vector", lambda e, ti=ti, qi=qi, axc=axc: e.tensor_scalar_mul(
                        vt_dst(ti), quad_ap(qi), axc), waits=[tm, vt_set],
                        inc=True)
                    quad_tok[qi] = tv
                    p1_last.append(tv)
                else:
                    qev = AF.ap(qev_off + (ti % 2) * 512, 512)
                    tv0 = p.do("vector", lambda e, qev=qev, qi=qi, axc=axc: e.tensor_scalar_mul(
                        qev, quad_ap(qi), axc), waits=[tm, rope_done], inc=True)
                    quad_tok[qi] = tv0
                    qa = qev.rearrange("p (h n) -> p h n", n=128)[:, :, 0:64]
                    qb_ = qev.rearrange("p (h n) -> p h n", n=128)[:, :, 64:128]
                    cA = rope_bcast(ropeC_sb, ti)
                    sA = rope_bcast(ropeS_sb, ti)
                    ta = AF.ap(ta_off, 256).rearrange("p (h n) -> p h n", n=64)
                    tb = AF.ap(tb_off, 256).rearrange("p (h n) -> p h n", n=64)
                    rq = AB.ap(rq_off + (ti % 2) * 512, 512)
                    ra = rq.rearrange("p (h n) -> p h n", n=128)[:, :, 0:64]
                    rb = rq.rearrange("p (h n) -> p h n", n=128)[:, :, 64:128]
                    p.do("vector", lambda e, ta=ta, qa=qa, cA=cA: e.tensor_tensor(
                        ta, qa, cA, ALU.mult), inc=True)
                    p.do("vector", lambda e, tb=tb, qb_=qb_, sA=sA: e.tensor_tensor(
                        tb, qb_, sA, ALU.mult), inc=True)
                    p.do("vector", lambda e, ra=ra, ta=ta, tb=tb: e.tensor_tensor(
                        ra, ta, tb, ALU.subtract), inc=True)
                    p.do("vector", lambda e, ta=ta, qa=qa, sA=sA: e.tensor_tensor(
                        ta, qa, sA, ALU.mult), inc=True)
                    p.do("vector", lambda e, tb=tb, qb_=qb_, cA=cA: e.tensor_tensor(
                        tb, qb_, cA, ALU.mult), inc=True)
                    tv_r = p.do("vector", lambda e, rb=rb, ta=ta, tb=tb: e.tensor_tensor(
                        rb, ta, tb, ALU.add), inc=True)
                    for r in range(4):
                        trp = trp_of[r % 2]
                        tm2 = p.do("tensor", lambda e, rq=rq, r=r, trp=trp: e.transpose(
                            trp.ap()[:, 0:128], rq[:, r * 128:(r + 1) * 128], ident_sb.ap()),
                            waits=[tv_r, tr_tok[r % 2]] +
                            ([stage_tok] if ti == 0 else []), inc=True)
                        tv_e = p.do("vector", lambda e, r=r, ti=ti, trp=trp: e.tensor_copy(
                            AB.ap(stage_off + r * 2048 + ti * 128, 128),
                            trp.ap()[:, 0:128]), waits=[tm2], inc=True)
                        tr_tok[r % 2] = tv_e
                    if ti == NT - 1:
                        tv_d2 = p.do("vector", lambda e: e.engine_nop(), inc=True)
                        if sec == "q":
                            dst = qf_d.rearrange("(c p) n -> p c n", p=128)[:, bi * 4:(bi + 1) * 4, :]
                        else:
                            dst = kf_d.rearrange("(c p) n -> p c n", p=128)
                        stage_tok = p.dma("gpsimd", dst,
                                          AB.t.ap()[:, stage_off:stage_off + 4 * 2048]
                                          .rearrange("p (c n) -> p c n", n=2048),
                                          waits=[tv_d2])
                        p1_last.append(stage_tok)
        AB.give(wkb_off, 2 * NF * 512)
        AF.give(qev_off, 2 * 512); AF.give(ta_off, 256); AF.give(tb_off, 256)
        AF.give(xt_off, 2 * 2048); AF.give(y1_off, 2048)
        AB.give(rq_off, 2 * 512); AB.give(stage_off, 4 * 2048)
        AB.give(qxf_off, NT * 2048)

    p2_last = list(p1_last)
    if phases >= 3:
        bar_p2 = p.barrier()
        p1_last = p1_last + bar_p2
        kf_off = AB.take(NKV * 2048)
        probs_off = AB.take(NT * 512)
        wo_off = AB.take(NF * 512)
        ob_off = AB.take(4 * 2048)
        qof_off = AB.take(4 * 2048)
        sas_off = AB.take(4 * 2048)
        qft_off = AB.take(2 * 512)
        qo_off = AB.take(2048)
        y1_off2 = AF.take(2048)

        tok_kf = p.dma("sync", AB.t.ap()[:, kf_off:kf_off + NKV * 2048]
                       .rearrange("p (c n) -> p c n", n=2048),
                       kf_d.rearrange("(c p) n -> p c n", p=128), waits=p1_last)
        qf_tok = [None, None]
        probs_guard = None
        ob_guard = None
        qo_guard = None
        wo_mm_last = [None]
        sas_tok = [None, None, None, None]
        for qb in range(p2_qbs):
            nch = 4 * (qb + 1)
            pv_done = []
            for h in range(NH):
                g = h // 4
                hh = qb * NH + h
                tok_qf = p.dma("sync", AB.ap(qft_off + (hh % 2) * 512, 512),
                               qf_d[h * 128:(h + 1) * 128, qb * 512:(qb + 1) * 512],
                               waits=[qf_tok[hh % 2]] + p1_last)
                exp_toks = []
                for grp in range(qb + 1):
                    mm_toks = []
                    for cc in range(4):
                        c = grp * 4 + cc
                        tmm = p.do("tensor", lambda e, c=c, cc=cc, g=g, hh=hh: e.matmul(
                            quad_ap(cc),
                            AB.ap(kf_off + g * 2048 + c * 128, 128),
                            AB.ap(qft_off + (hh % 2) * 512, 512),
                            start=True, stop=True),
                            waits=[tok_qf, tok_kf, quad_tok[cc]], inc=True)
                        mm_toks.append(tmm)
                    qf_tok[hh % 2] = mm_toks[-1]
                    texp = p.do("scalar", lambda e, grp=grp: e.activation(
                        AB.ap(probs_off + grp * 2048, 2048), quad.ap(),
                        ACT.Exp, scale=s128bc.ap()),
                        waits=mm_toks + [probs_guard, t_s128], inc=True, drain=True)
                    exp_toks.append(texp)
                    for cc in range(4):
                        quad_tok[cc] = texp
                def _mask(e, qb=qb):
                    r = None
                    for cc in range(4):
                        r = e.tensor_tensor(
                            AB.ap(probs_off + (4 * qb + cc) * 512, 512),
                            AB.ap(probs_off + (4 * qb + cc) * 512, 512),
                            mask_sb.ap()[:, 512 - cc * 128:1024 - cc * 128], ALU.mult)
                    return r
                tmask = p.do("vector", _mask, waits=[exp_toks[-1]], inc=True)
                for j in range(4):
                    pvp = (pv_a, pv_b)[j % 2]
                    tmm = None
                    for c in range(nch):
                        tmm = p.do("tensor", lambda e, c=c, j=j, g=g, pvp=pvp, nch=nch: e.matmul(
                            pvp.ap()[:, 0:129],
                            AB.ap(probs_off + c * 512 + j * 128, 128),
                            AB.ap(vtok_off + c * 516 + g * 129, 129),
                            start=(c == 0), stop=(c == nch - 1)),
                            waits=[tmask, pv_tok[j % 2]] + exp_toks if c == 0 else (),
                            inc=(c == nch - 1))
                    jj = qb * 4 + j
                    p.do("vector", lambda e, pvp=pvp, jj=jj: e.reciprocal(
                        o_inv.ap()[:, jj:jj + 1], pvp.ap()[:, 128:129]),
                        waits=[tmm], inc=True)
                    tpv = p.do("vector", lambda e, pvp=pvp, jj=jj, j=j, h=h: e.tensor_scalar_mul(
                        AB.ap(ob_off + j * 2048 + h * 128, 128),
                        pvp.ap()[:, 0:128], o_inv.ap()[:, jj:jj + 1]),
                        waits=[ob_guard] if (h == 0 and ob_guard) else (),
                        inc=True)
                    pv_tok[j % 2] = tpv
                    if j == 3:
                        probs_guard = tmm
                    if h == NH - 1:
                        pv_done.append(tpv)
                drive_w(7)
            dbg2_tok = None
            if debug and qb == 0:
                dbg2_tok = p.dma("gpsimd", dbg2_d, AB.ap(ob_off, 8192), waits=pv_done)
            qof_done = []
            for j in range(4):
                jj = qb * 4 + j
                obj = AB.ap(ob_off + j * 2048, 2048)
                ts_sq = p.do("scalar", lambda e, obj=obj, jj=jj: e.activation(
                    sq_scr.ap()[:, 0:2048], obj, ACT.Square, accum_out=cap(o_ssq, jj)),
                    waits=pv_done, inc=True, drain=True)
                tnw = p.do("vector", lambda e, obj=obj: e.tensor_tensor(
                    obj, obj, normbc.ap(), ALU.mult),
                    waits=[ts_sq, dbg2_tok] + nb_toks, inc=True)
                tvv = p.do("vector", lambda e, jj=jj: e.tensor_scalar(
                    cap(o_t2, jj), cap(o_ssq, jj), 1.0 / 2048.0, 1e-5, ALU.mult, ALU.add),
                    inc=True)
                tsr = p.do("scalar", lambda e, jj=jj: (
                    e.activation(cap(o_t1, jj), cap(o_t2, jj), ACT.Ln), e.drain(),
                    e.activation(cap(o_r, jj), cap(o_t1, jj), ACT.Exp, scale=-0.5))[-1],
                    waits=[tvv], inc=True, drain=True)
                p.do("vector", lambda e, jj=jj: e.tensor_scalar(
                    cap(o_t1, jj), cap(o_t2, jj), cap(o_r, jj), cap(o_r, jj), ALU.mult, ALU.mult),
                    waits=[tsr], inc=True)
                p.do("vector", lambda e, jj=jj: e.tensor_scalar(
                    cap(o_t1, jj), cap(o_t1, jj), -0.5, 1.5, ALU.mult, ALU.add), inc=True)
                p.do("vector", lambda e, jj=jj: e.tensor_scalar_mul(
                    cap(o_r, jj), cap(o_r, jj), cap(o_t1, jj)), inc=True)
                p.do("vector", lambda e, obj=obj, jj=jj: e.reduce_max(
                    cap(o_t1, jj), obj, axis=AX.X, apply_absolute_value=True),
                    waits=[tnw], inc=True)
                p.do("vector", lambda e, jj=jj: e.tensor_scalar(
                    cap(o_t1, jj), cap(o_t1, jj), cap(o_r, jj), 1e-5, ALU.mult, ALU.max), inc=True)
                p.do("vector", lambda e, jj=jj: e.tensor_single_scalar(
                    cap(o_ax, jj), cap(o_t1, jj), INV127, ALU.mult), inc=True)
                p.do("vector", lambda e, jj=jj: e.reciprocal(
                    cap(o_t2, jj), cap(o_t1, jj)), inc=True)
                p.do("vector", lambda e, jj=jj: e.tensor_single_scalar(
                    cap(o_t2, jj), cap(o_t2, jj), 127.0, ALU.mult), inc=True)
                p.do("vector", lambda e, jj=jj: e.tensor_scalar_mul(
                    cap(o_sp, jj), cap(o_r, jj), cap(o_t2, jj)), inc=True)
                y1 = AF.ap(y1_off2, 2048)
                qo = AB.ap(qo_off, 2048)
                t2b = sq_scr.ap()[:, 2048:4096]
                p.do("vector", lambda e, obj=obj, y1=y1, jj=jj: e.tensor_scalar(
                    y1, obj, cap(o_sp, jj), C32, ALU.mult, ALU.add),
                    waits=[qo_guard], inc=True)
                p.do("vector", lambda e, y1=y1, qo=qo: e.tensor_single_scalar(
                    qo, y1, -C32, ALU.add), inc=True)
                p.do("vector", lambda e, qo=qo, t2b=t2b: e.tensor_scalar(
                    t2b, qo, C52, -C52, ALU.add, ALU.add), inc=True)
                tq = p.do("vector", lambda e, qo=qo, t2b=t2b: e.scalar_tensor_tensor(
                    qo, qo, omd.ap(), t2b, ALU.mult, ALU.add), inc=True)
                ob_guard = tq
                evs = []
                for f in range(NF):
                    trp = trp_of[f % 2]
                    tm2 = p.do("tensor", lambda e, qo=qo, f=f, trp=trp: e.transpose(
                        trp.ap()[:, 0:128], qo[:, f * 128:(f + 1) * 128], ident_sb.ap()),
                        waits=[tq, tr_tok[f % 2]] +
                        ([wo_mm_last[0]] if (f == 0 and wo_mm_last[0]) else []), inc=True)
                    tv_e = p.do("vector", lambda e, f=f, j=j, trp=trp: e.tensor_copy(
                        AB.ap(qof_off + j * 2048 + f * 128, 128), trp.ap()[:, 0:128]),
                        waits=[tm2], inc=True)
                    tr_tok[f % 2] = tv_e
                    evs.append(tv_e)
                qo_guard = evs[-1]
                qof_done.append(p.do("vector", lambda e: e.engine_nop(), waits=[evs[-1]],
                                     inc=True))
            if debug and qb == 0:
                dbg3_tok = p.dma("gpsimd", dbg3_d, AB.ap(qof_off, 8192), waits=qof_done)
                qof_done = qof_done + [dbg3_tok]
            for ob in range(4):
                tok_wo = []
                for hf in range(2):
                    tok_wo.append(p.dma(
                        "gpsimd",
                        AB.t.ap()[:, wo_off + hf * 8 * 512:wo_off + (hf + 1) * 8 * 512]
                        .rearrange("p (c n) -> p c n", n=512),
                        woT_d[hf * 1024:(hf + 1) * 1024, ob * 512:(ob + 1) * 512]
                        .rearrange("(c p) n -> p c n", p=128),
                        waits=[wo_mm_last[0]] + (p1_last if qb == 0 and ob == 0 else [])))
                for j in range(4):
                    jj = qb * 4 + j
                    qi = (2 * (ob % 2)) + (j % 2)
                    tmm = None
                    for f in range(NF):
                        tmm = p.do("tensor", lambda e, f=f, j=j, ob=ob, qi=qi: e.matmul(
                            quad_ap(qi), AB.ap(qof_off + j * 2048 + f * 128, 128),
                            AB.ap(wo_off + f * 512, 512),
                            start=(f == 0), stop=(f == NF - 1)),
                            waits=[tok_wo[0] if f < 8 else tok_wo[1], qof_done[j],
                                   quad_tok[qi]] if f in (0, 8) else (),
                            inc=(f == NF - 1))
                    wo_mm_last[0] = tmm
                    tv_s = p.do("vector", lambda e, ob=ob, j=j, qi=qi, jj=jj:
                                e.tensor_scalar_mul(
                                    AB.ap(sas_off + j * 2048 + ob * 512, 512),
                                    quad_ap(qi), o_ax.ap()[:, jj:jj + 1]),
                                waits=[tmm, sas_tok[j]], inc=True)
                    quad_tok[qi] = tv_s
                    if ob == 3:
                        sas_tok[j] = p.dma(
                            "gpsimd", sA_d[qb * 512 + j * 128: qb * 512 + (j + 1) * 128, :],
                            AB.ap(sas_off + j * 2048, 2048), waits=[tv_s])
            p2_last = [t for t in sas_tok if t]
        drive_w(10000)
        p.do("vector", lambda e: e.tensor_scalar_mul(
            st_ag.ap(), st_ax.ap(), wsc.ap()[:, 0:1]), inc=True, drain=True)
        ag_tok = p.do("vector", lambda e: e.tensor_scalar_mul(
            st_au.ap(), st_ax.ap(), wsc.ap()[:, 2:3]), inc=True, drain=True)
        AB.give(kf_off, NKV * 2048); AB.give(probs_off, NT * 512)
        AB.give(ob_off, 4 * 2048); AB.give(qft_off, 2 * 512)
        AB.give(qo_off, 2048); AB.give(qof_off, 4 * 2048)
        AB.give(sas_off, 4 * 2048); AB.give(wo_off, NF * 512)
        AF.give(y1_off2, 2048)
        AB.give(vtok_off, NT * 516)

    if phases >= 4:
        if phases < 3 or p2_qbs < 4:
            drive_w(10000)
        bar_p3 = p.barrier()
        p2_last = p2_last + bar_p3
        hq_off = AB.take(4 * DI)
        qxq_off = AB.take(NF * 512)
        if debug:
            print("P3 AB free after hq/qxq:", AB.free, "hq_off=", hq_off, "qxq_off=", qxq_off)
        wg_off = AB.take(NF * 512)
        wu_off = AB.take(NF * 512)
        sg_off = AB.take(2 * 512)
        tq_off = AB.take(4096)
        saf_off = AB.take(2048)
        t2s_off = AB.take(4096)
        if debug:
            print("P3 offs: wg", wg_off, "wu", wu_off, "sg", sg_off, "tq", tq_off,
                  "saf", saf_off, "free", AB.free)
        y1h_off = AF.take(4096)
        xf_off = AF.take(2048)
        ou_off = AF.take(2048)

        wgt_tok = [None]
        wut_tok = [None]
        sg_guard = [None, None]
        hq_guard = None
        qx_guard = None
        gq_guard = None
        saf_guard = None
        out_guard = None
        xf_guard = None
        final_toks = []
        for Q in range(p3_qs):
            tok_qx = p.dma("sync", AB.t.ap()[:, qxq_off:qxq_off + NF * 512]
                           .rearrange("p (c n) -> p c n", n=512),
                           qxf_d[:, Q * 512:(Q + 1) * 512]
                           .rearrange("(c p) n -> p c n", p=128),
                           waits=[qx_guard] + p0_last + p2_last)
            h_toks = []
            for ob in range(NF):
                tok_g = []
                tok_u = []
                for hf in range(2):
                    tok_g.append(p.dma(
                        "sync", AB.t.ap()[:, wg_off + hf * 8 * 512:wg_off + (hf + 1) * 8 * 512]
                        .rearrange("p (c n) -> p c n", n=512),
                        tg_d[hf * 1024:(hf + 1) * 1024, ob * 512:(ob + 1) * 512]
                        .rearrange("(c p) n -> p c n", p=128),
                        waits=[wgt_tok[0]] + (tern_done + p2_last if Q == 0 and ob == 0 else [])))
                    tok_u.append(p.dma(
                        "scalar", AB.t.ap()[:, wu_off + hf * 8 * 512:wu_off + (hf + 1) * 8 * 512]
                        .rearrange("p (c n) -> p c n", n=512),
                        tu_d[hf * 1024:(hf + 1) * 1024, ob * 512:(ob + 1) * 512]
                        .rearrange("(c p) n -> p c n", p=128),
                        waits=[wut_tok[0]] + (tern_done + p2_last if Q == 0 and ob == 0 else [])))
                for ti in range(4):
                    gi = (ti % 2) * 2
                    ui = gi + 1
                    tg_mm = None
                    tu_mm = None
                    for f in range(NF):
                        tg_mm = p.do("tensor", lambda e, f=f, ti=ti, gi=gi: e.matmul(
                            quad_ap(gi), AB.ap(qxq_off + f * 512 + ti * 128, 128),
                            AB.ap(wg_off + f * 512, 512),
                            start=(f == 0), stop=(f == NF - 1)),
                            waits=[tok_g[0] if f < 8 else tok_g[1], tok_qx,
                                   quad_tok[gi], gq_guard] if f in (0, 8) else (),
                            inc=(f == NF - 1))
                    for f in range(NF):
                        tu_mm = p.do("tensor", lambda e, f=f, ti=ti, ui=ui: e.matmul(
                            quad_ap(ui), AB.ap(qxq_off + f * 512 + ti * 128, 128),
                            AB.ap(wu_off + f * 512, 512),
                            start=(f == 0), stop=(f == NF - 1)),
                            waits=[tok_u[0] if f < 8 else tok_u[1],
                                   quad_tok[ui]] if f in (0, 8) else (),
                            inc=(f == NF - 1))
                    if ti == 3:
                        wgt_tok[0] = tg_mm
                        wut_tok[0] = tu_mm
                    tt = Q * 4 + ti
                    sg = AB.ap(sg_off + (ti % 2) * 512, 512)
                    tsg = p.do("scalar", lambda e, sg=sg, gi=gi, tt=tt: e.activation(
                        sg, quad_ap(gi), ACT.Silu, scale=st_ag.ap()[:, tt:tt + 1]),
                        waits=[tg_mm, ag_tok, sg_guard[ti % 2]], inc=True, drain=True)
                    quad_tok[gi] = tsg
                    th = p.do("vector", lambda e, sg=sg, ui=ui, ti=ti, ob=ob, tt=tt:
                              e.scalar_tensor_tensor(
                                  AB.ap(hq_off + ti * DI + ob * 512, 512),
                                  quad_ap(ui), st_au.ap()[:, tt:tt + 1], sg,
                                  ALU.mult, ALU.mult),
                              waits=[tu_mm, tsg] +
                              ([hq_guard] if (ob == 0 and ti == 0 and hq_guard) else []),
                              inc=True)
                    sg_guard[ti % 2] = th
                    quad_tok[ui] = th
                    h_toks.append(th)
            dbgh_tok = None
            if debug and Q == 0 and phases >= 4:
                dbgh_tok = p.dma("gpsimd", dbg3_d, AB.ap(hq_off, DI), waits=h_toks)
            qhf_done = []
            tq_free = None
            for ti in range(4):
                tt = Q * 4 + ti
                hten = AB.ap(hq_off + ti * DI, DI)
                ts_a = p.do("scalar", lambda e, hten=hten, tt=tt: e.activation(
                    sq_scr.ap(), hten[:, 0:4096], ACT.Square, accum_out=cap(h_ssq, tt)),
                    waits=h_toks, inc=True, drain=True)
                ts_b = p.do("scalar", lambda e, hten=hten, tt=tt: e.activation(
                    sq_scr.ap(), hten[:, 4096:8192], ACT.Square, accum_out=cap(h_ssq2, tt)),
                    inc=True, drain=True)
                tlnw = p.do("vector", lambda e, hten=hten: e.tensor_tensor(
                    hten, hten, lnwbc.ap(), ALU.mult),
                    waits=[ts_b, lnw_done, dbgh_tok], inc=True)
                if debug and Q == 0 and ti == 0 and phases >= 4:
                    dbgl_tok = p.dma("gpsimd", dbg3_d, hten, waits=[tlnw])
                    p.dma("gpsimd", dbg5_d[:, 96:97], cap(h_ssq, tt),
                          waits=[ts_b], slow=True)
                    p.dma("gpsimd", dbg5_d[:, 97:98], cap(h_ssq2, tt),
                          waits=[ts_b], slow=True)
                if debug and Q == 0 and phases >= 4:
                    p.dma("gpsimd", dbg5_d[:, 100 + ti:101 + ti], cap(h_ssq, 0),
                          waits=[tsr], slow=True)
                p.do("vector", lambda e, tt=tt: e.tensor_add(
                    cap(h_ssq, tt), cap(h_ssq, tt), cap(h_ssq2, tt)), inc=True)
                tvv = p.do("vector", lambda e, tt=tt: e.tensor_scalar(
                    cap(h_t2, tt), cap(h_ssq, tt), 1.0 / 8192.0, 1e-6, ALU.mult, ALU.add),
                    inc=True)
                tsr = p.do("scalar", lambda e, tt=tt: (
                    e.activation(cap(h_t1, tt), cap(h_t2, tt), ACT.Ln), e.drain(),
                    e.activation(cap(h_r, tt), cap(h_t1, tt), ACT.Exp, scale=-0.5))[-1],
                    waits=[tvv], inc=True, drain=True)
                p.do("vector", lambda e, tt=tt: e.tensor_scalar(
                    cap(h_t1, tt), cap(h_t2, tt), cap(h_r, tt), cap(h_r, tt), ALU.mult, ALU.mult),
                    waits=[tsr], inc=True)
                p.do("vector", lambda e, tt=tt: e.tensor_scalar(
                    cap(h_t1, tt), cap(h_t1, tt), -0.5, 1.5, ALU.mult, ALU.add), inc=True)
                p.do("vector", lambda e, tt=tt: e.tensor_scalar_mul(
                    cap(h_r, tt), cap(h_r, tt), cap(h_t1, tt)), inc=True)
                p.do("vector", lambda e, hten=hten, tt=tt: e.reduce_max(
                    cap(h_t1, tt), hten, axis=AX.X, apply_absolute_value=True),
                    waits=[tlnw], inc=True)
                p.do("vector", lambda e, tt=tt: e.tensor_scalar(
                    cap(h_t1, tt), cap(h_t1, tt), cap(h_r, tt), 1e-5, ALU.mult, ALU.max), inc=True)
                p.do("vector", lambda e, tt=tt: e.tensor_single_scalar(
                    cap(h_ax, tt), cap(h_t1, tt), INV127, ALU.mult), inc=True)
                p.do("vector", lambda e, tt=tt: e.reciprocal(
                    cap(h_t2, tt), cap(h_t1, tt)), inc=True)
                p.do("vector", lambda e, tt=tt: e.tensor_single_scalar(
                    cap(h_t2, tt), cap(h_t2, tt), 127.0, ALU.mult), inc=True)
                p.do("vector", lambda e, tt=tt: e.tensor_scalar_mul(
                    cap(h_sp, tt), cap(h_r, tt), cap(h_t2, tt)), inc=True)
                tad = p.do("vector", lambda e, tt=tt: e.tensor_scalar_mul(
                    cap(h_ad, tt), cap(h_ax, tt), wsc.ap()[:, 4:5]), inc=True, drain=True)
                if debug and Q == 0 and ti == 0 and phases >= 4:
                    for c0, st_ in [(70, h_ax), (71, h_sp), (72, h_ad), (73, h_r), (74, h_t2)]:
                        p.dma("gpsimd", dbg5_d[:, c0:c0 + 1], cap(st_, tt),
                              waits=[tad], slow=True)
                for half in range(2):
                    hsl = slice(half * 4096, (half + 1) * 4096)
                    qht = AB.ap(tq_off, 4096)
                    p.do("vector", lambda e, hten=hten, hsl=hsl, tt=tt: e.tensor_scalar(
                        AF.ap(y1h_off, 4096), hten[:, hsl], cap(h_sp, tt), C32,
                        ALU.mult, ALU.add), waits=[tq_free], inc=True)
                    p.do("vector", lambda e, qht=qht: e.tensor_single_scalar(
                        qht, AF.ap(y1h_off, 4096), -C32, ALU.add), inc=True)
                    p.do("vector", lambda e, qht=qht: e.tensor_scalar(
                        AB.ap(t2s_off, 4096), qht, C52, -C52, ALU.add, ALU.add), inc=True)
                    tqq = p.do("vector", lambda e, qht=qht: e.scalar_tensor_tensor(
                        qht, qht, omd.ap(), AB.ap(t2s_off, 4096), ALU.mult, ALU.add),
                        inc=True)
                    if debug and Q == 0 and ti == 0 and half == 0 and phases >= 4:
                        tqdump = p.dma("gpsimd", dbg2_d[:, 0:4096], qht, waits=[tqq])
                    evs = []
                    for f in range(32):
                        fg = half * 32 + f
                        trp = trp_of[f % 2]
                        tm2 = p.do("tensor", lambda e, qht=qht, f=f, trp=trp: e.transpose(
                            trp.ap()[:, 0:128], qht[:, f * 128:(f + 1) * 128],
                            ident_sb.ap()), waits=[tqq, tr_tok[f % 2]], inc=True)
                        tv_e = p.do("vector", lambda e, fg=fg, ti=ti, trp=trp: e.tensor_copy(
                            AB.ap(hq_off + ti * DI + fg * 128, 128), trp.ap()[:, 0:128]),
                            waits=[tm2], inc=True)
                        tr_tok[f % 2] = tv_e
                        evs.append(tv_e)
                    tq_free = evs[-1]
                    if debug and Q == 0 and ti == 0 and half == 0 and phases >= 4:
                        p.dma("gpsimd", dbg3_d[:, 0:4096], AB.ap(hq_off, 4096),
                              waits=[tq_free])
                if debug and Q == 0 and phases >= 4:
                    p.dma("gpsimd", dbg5_d[:, 104 + ti:105 + ti], cap(h_ssq, 0),
                          waits=[tq_free], slow=True)
                if debug and Q == 0 and ti == 3 and phases >= 4:
                    p.dma("gpsimd", dbg2_d[:, 4096:8192], AB.ap(hq_off, 4096),
                          waits=[tq_free])
                qhf_done.append(p.do("vector", lambda e: e.engine_nop(),
                                     waits=[tq_free], inc=True))
            td_tok = [None, None]
            ev_d = []
            for ob in range(4):
                dmm_last = [None] * 4
                for sl in range(4):
                    buf = wg_off if sl % 2 == 0 else wu_off
                    tok_td = p.dma(("sync", "scalar")[sl % 2],
                                   AB.t.ap()[:, buf:buf + NF * 512]
                                   .rearrange("p (c n) -> p c n", n=512),
                                   td_d[sl * 2048:(sl + 1) * 2048, ob * 512:(ob + 1) * 512]
                                   .rearrange("(c p) n -> p c n", p=128),
                                   waits=[td_tok[sl % 2]] +
                                   ([wgt_tok[0], wut_tok[0]] if ob == 0 and sl < 2 else []))
                    for ti in range(4):
                        for f in range(NF):
                            fg = sl * NF + f
                            dmm_last[ti] = p.do(
                                "tensor", lambda e, f=f, ti=ti, fg=fg, buf=buf: e.matmul(
                                    quad_ap(ti), AB.ap(hq_off + ti * DI + fg * 128, 128),
                                    AB.ap(buf + f * 512, 512),
                                    start=(fg == 0), stop=(fg == NI - 1)),
                                waits=[tok_td, quad_tok[ti]] + qhf_done if f == 0 else (),
                                inc=(f == NF - 1))
                    td_tok[sl % 2] = dmm_last[3]
                if debug and Q == 0 and phases >= 4:
                    p.dma("gpsimd", dbg5_d[:, 108 + ob:109 + ob], cap(h_ssq, 0),
                          waits=[td_tok[1] or td_tok[0]], slow=True)
                for ti in range(4):
                    tt = Q * 4 + ti
                    tsd = p.do("scalar", lambda e, ti=ti, ob=ob, tt=tt: e.activation(
                        AB.ap(qxq_off + ti * 2048 + ob * 512, 512), quad_ap(ti),
                        ACT.Silu, scale=h_ad.ap()[:, tt:tt + 1]),
                        waits=[dmm_last[ti], gq_guard], inc=True, drain=True)
                    quad_tok[ti] = tsd
                    ev_d.append(tsd)
                    if debug and Q == 0 and phases >= 4:
                        p.dma("gpsimd", dbg5_d[:, 112 + 4 * ob + ti:113 + 4 * ob + ti],
                              cap(h_ssq, 0), waits=[tsd], slow=True)
            if debug and Q == 0 and phases >= 4:
                p.dma("gpsimd", dbg3_d[:, 0:2048], AB.ap(qxq_off, 2048), waits=ev_d)
            hq_guard = td_tok[1]
            qx_guard = ev_d[-1]
            for ti in range(4):
                tt = Q * 4 + ti
                xf = AF.ap(xf_off, 2048)
                saf = AB.ap(saf_off, 2048)
                ou = AF.ap(ou_off, 2048)
                tok_x2 = p.dma("sync", xf, x_d[tt * 128:(tt + 1) * 128, :],
                               waits=[xf_guard])
                tok_sa = p.dma("sync", saf, sA_d[tt * 128:(tt + 1) * 128, :],
                               waits=[saf_guard] + p2_last)
                saf_guard = p.do("vector", lambda e, saf=saf, ti=ti: e.tensor_tensor(
                    saf, saf, AB.ap(qxq_off + ti * 2048, 2048), ALU.mult),
                    waits=[tok_sa] + ev_d, inc=True)
                tv_o = p.do("vector", lambda e, saf=saf, xf=xf, ou=ou: e.tensor_tensor(
                    ou, saf, xf, ALU.add), waits=[tok_x2, out_guard], inc=True)
                xf_guard = tv_o
                out_guard = p.dma("gpsimd", out_d[tt * 128:(tt + 1) * 128, :], ou,
                                  waits=[tv_o])
                final_toks.append(out_guard)
            gq_guard = saf_guard
            qx_guard = saf_guard
        p.do("gpsimd", lambda e: e.engine_nop(), waits=[final_toks[-1]])

    from contextlib import ExitStack
    with ExitStack() as stack:
        sems = {nm: stack.enter_context(nc.semaphore(nm)) for nm in ALL_SEMS}
        block = stack.enter_context(nc.Block())
        p.emit(block, sems)
    return nc


def _prep_shared(inputs):
    wqkv = inputs["L0_attn_wqkv"]
    perm = np.arange(DQK)
    for h in range(NH + NKV):
        base = h * HD
        perm[base:base + 64] = base + 2 * np.arange(64)
        perm[base + 64:base + 128] = base + 2 * np.arange(64) + 1
    return {
        "wqkvT": np.ascontiguousarray(wqkv[perm].T),
        "woT": np.ascontiguousarray(inputs["L0_attn_wo"].T),
        "wgT": np.ascontiguousarray(inputs["L0_ffn_w_gate"].T),
        "wuT": np.ascontiguousarray(inputs["L0_ffn_w_up"].T),
        "wdT": np.ascontiguousarray(inputs["L0_ffn_w_down"].T),
        "normw": np.ascontiguousarray(inputs["L0_attn_norm_w"].reshape(1, D)),
        "lnw": np.ascontiguousarray(inputs["L0_ffn_ln_weight"].reshape(1, DI)),
    }


def kernel(**inputs):
    x = np.asarray(inputs["L0_attn_x"], dtype=np.float32)
    shared = _prep_shared({k: np.asarray(v, dtype=np.float32)
                           for k, v in inputs.items() if "cache" not in k and k != "L0_attn_x"})
    nc = build_program(debug=False, phases=5)
    in_maps = []
    for c in range(NCORES):
        m = {"x": np.ascontiguousarray(x[c])}
        m.update(shared)
        in_maps.append(m)
    res = run_bass_kernel_spmd(nc, in_maps, core_ids=list(range(NCORES)))
    return np.stack([np.asarray(r["out"], dtype=np.float32) for r in res.results])
